# version 02
# baseline (speedup 1.0000x reference)
import numpy as np
from contextlib import ExitStack
import concourse.bass as bass
import concourse.mybir as mybir
from concourse.bass_utils import run_bass_kernel_spmd

F32 = mybir.dt.float32
BF16 = mybir.dt.bfloat16
AF = mybir.ActivationFunctionType
ALU = mybir.AluOpType
Pool = mybir.EngineType.Pool

B, L, D = 4, 4096, 1024
H, DH, DS, DTR = 16, 64, 48, 64
KC = 4
DI = 2048
HF = 4096
HPC = 8
MYC = 512
TH = 2048
NSH = DS // 2
NSLAB = D // 128
DP1 = D + 1
DEBUG = False
PROBE = False
STAGE = 9


def _stage_mats(weights):
    w = np.asarray(weights, np.float32)
    lowN = np.zeros((6, 64, 64, 64), np.float32)
    highN = np.zeros((6, 64, 64, 64), np.float32)
    for s in range(12):
        s2 = 1 << s
        ws = w[s].reshape(HF // (2 * s2), s2, 2, 2)
        for blk in range(HF // (2 * s2)):
            for r in range(s2):
                for half in range(2):
                    i = blk * 2 * s2 + half * s2 + r
                    j0 = blk * 2 * s2 + r
                    j1 = j0 + s2
                    if s < 6:
                        lowN[s, i // 64, i % 64, j0 % 64] = ws[blk, r, half, 0]
                        lowN[s, i // 64, i % 64, j1 % 64] = ws[blk, r, half, 1]
                    else:
                        l = r % 64
                        highN[s - 6, l, i // 64, j0 // 64] = ws[blk, r, half, 0]
                        highN[s - 6, l, i // 64, j1 // 64] = ws[blk, r, half, 1]
    lowT = lowN.transpose(0, 1, 3, 2)
    highT = highN.transpose(0, 1, 3, 2)
    def arr(a):
        return np.ascontiguousarray(a.transpose(0, 2, 1, 3).reshape(6, 64, 64 * 64))
    return arr(lowN), arr(lowT), arr(highN), arr(highT)


def _blkdiag2(a):
    r, c = a.shape
    out = np.zeros((2 * r, 2 * c), np.float32)
    out[0:r, 0:c] = a
    out[r:2 * r, c:2 * c] = a
    return out


def _col(v, p=128):
    v = np.asarray(v, np.float32)
    return np.ascontiguousarray(v.reshape(-1, p).T)


def _slabs(ap2d, p=128):
    rows, C = ap2d.shape
    Sn = rows // p
    return bass.AP(ap2d.tensor, ap2d.offset, [[C, p], [p * C, Sn], [1, C]])


class Ctr:
    def __init__(self, es, nc, name):
        self.h = es.enter_context(nc.semaphore(name))
        self.n = 0

    def inc(self, k=1):
        self.n += k
        return self.h, k


def build_nc(dbg=False, stage=9, probe=False):
    nc = bass.Bass("TRN2", target_bir_lowering=False, debug=False)
    es = ExitStack()

    def dram(name, shape, dt=F32, kind="Internal"):
        return nc.dram_tensor(name, list(shape), dt, kind=kind)

    def pdram(name, shape, dt=F32):
        return nc.dram_tensor(name, list(shape), dt,
                              kind="ExternalOutput" if probe else "Internal")

    xT = dram("xT", (D, L), kind="ExternalInput").ap()
    xTh = dram("xTh", (D, TH), kind="ExternalInput").ap()
    w1col = dram("w1col", (128, NSLAB), kind="ExternalInput").ap()
    w2col = dram("w2col", (128, NSLAB), kind="ExternalInput").ap()
    kercol = dram("kercol", (128, NSLAB * KC), kind="ExternalInput").ap()
    dwbcol = dram("dwbcol", (128, NSLAB), kind="ExternalInput").ap()
    pwT = dram("pwT", (D, MYC), kind="ExternalInput").ap()
    pwbc = dram("pwbc", (128, MYC // 128), kind="ExternalInput").ap()
    xpT = dram("xpT", (128, 320), kind="ExternalInput").ap()
    dtpT = dram("dtpT", (128, 128), kind="ExternalInput").ap()
    dtb2 = dram("dtb2", (128, 1), kind="ExternalInput").ap()
    alog8 = dram("alog8", (HPC, DS), kind="ExternalInput").ap()
    opT = dram("opT", (D, D), kind="ExternalInput").ap()
    opbc = dram("opbc", (128, NSLAB), kind="ExternalInput").ap()
    gT = dram("gT", (D, DI), kind="ExternalInput").ap()
    uT = dram("uT", (D, DI), kind="ExternalInput").ap()
    dTw = dram("dTw", (DI, D), kind="ExternalInput").ap()
    EXPc = dram("EXPc", (HF, DP1), kind="ExternalInput").ap()
    WCS = dram("WCS", (HF, D), kind="ExternalInput").ap()
    ubcol = dram("ubcol", (128, HF // 128), kind="ExternalInput").ap()
    bcncol = dram("bcncol", (128, NSLAB), kind="ExternalInput").ap()
    dbcol = dram("dbcol", (128, NSLAB), kind="ExternalInput").ap()
    bstk = {}
    for nm in ["bfu_low", "bfu_lowT", "bfu_high", "bfu_highT",
               "bfd_low", "bfd_lowT", "bfd_high", "bfd_highT"]:
        bstk[nm] = dram(nm, (6, 64, 64 * 64), kind="ExternalInput").ap()

    outT = dram("outT", (D, TH), kind="ExternalOutput").ap()
    WexpP = pdram("WexpP", (NSLAB, 128, HF), BF16).ap()
    WexpB = pdram("WexpB", (1, HF)).ap()
    WconP = pdram("WconP", (HF, D), BF16).ap()
    bcSp = pdram("bcSp", (HPC, 96, L), BF16).ap()
    dtxSp = pdram("dtxSp", (HPC // 2, 128, L), BF16).ap()
    xhSp = pdram("xhSp", (HPC // 2, 128, L), BF16).ap()
    xnhSp = pdram("xnhSp", (NSLAB, 128, TH), BF16).ap()
    ffnSc = pdram("ffnSc", (NSLAB, 128, TH)).ap()
    x1Sc = pdram("x1Sc", (NSLAB, 128, TH)).ap()
    b1bnc = dram("b1bnc", (1, HF)).ap()
    cin = nc.dram_tensor("cin", [HPC // 2, 128, L], BF16)
    cout = nc.dram_tensor("cout", [2, HPC // 2, 128, L], BF16)
    if probe:
        hrowsD = dram("hrowsD", (128, TH), kind="ExternalOutput").ap()
        cinD = dram("cinD", (HPC // 2, 128, L), BF16, kind="ExternalOutput").ap()
        coutD = dram("coutD", (2, HPC // 2, 128, L), BF16, kind="ExternalOutput").ap()
        bcolD = dram("bcolD", (128, NSLAB + HF // 128 + HPC * NSH + 80),
                     kind="ExternalOutput").ap()

    dbg_out = {}
    if dbg:
        for nm, shp in [("dxn", (128, L)), ("duc", (128, L)), ("dxh", (128, L)),
                        ("ddtx", (128, L)), ("dbc", (96, L)), ("ddt", (128, L)),
                        ("dy", (128, L)), ("dx1", (128, TH)), ("dffn", (128, TH)),
                        ("dW1", (128, HF)), ("dWc", (128, D)), ("drows", (8, L))]:
            dbg_out[nm] = dram(nm, shp, kind="ExternalOutput").ap()

    S = lambda name: Ctr(es, nc, name)
    sW, sXs, sSq, sV, sT, sEv, sSp = S("sW"), S("sXs"), S("sSq"), S("sV"), S("sT"), S("sEv"), S("sSp")
    sBC, sAB, sVs, sCC, sBk, sOut = S("sBC"), S("sAB"), S("sVs"), S("sCC"), S("sBk"), S("sOut")

    sb = lambda name, shape, dt=F32: nc.sbuf_tensor(name, list(shape), dt)

    ones = es.enter_context(sb("ones", (128, 1)))
    hrows = es.enter_context(sb("hrows", (128, TH)))
    colbuf = es.enter_context(sb("colbuf", (128, 80)))
    b1col = es.enter_context(sb("b1col", (128, HF // 128)))
    b2col = es.enter_context(sb("b2col", (128, NSLAB)))
    ascl = es.enter_context(sb("ascl", (128, HPC * NSH)))
    R_M, R_RLN, R_GU, R_T0, R_T1, R_T2 = 0, 1, 2, 3, 4, 5
    rows_ref = {}
    ROWMAP = {0: (0, 0), 1: (64, 0), 2: (96, 0), 3: (32, 0),
              4: (32, 1), 5: (64, 1), 6: (0, 1), 7: (32, 2)}

    def ROW(r, c0=0, c1=L):
        p, c = ROWMAP[r]
        return rows_ref["t"][p:p + 1, c * L + c0:c * L + c1]
    C_W1, C_W2, C_KER, C_DWB, C_PWB, C_OPB, C_DTB = 0, 8, 16, 48, 56, 60, 68

    nc.vector.memset(ones[:], 1.0)

    nc.sync.dma_start(out=colbuf[:, C_W1:C_W1 + 8], in_=w1col).then_inc(*sW.inc(16))
    nc.sync.dma_start(out=colbuf[:, C_W2:C_W2 + 8], in_=w2col).then_inc(*sW.inc(16))
    nc.sync.dma_start(out=colbuf[:, C_KER:C_KER + 32], in_=kercol).then_inc(*sW.inc(16))
    nc.sync.dma_start(out=colbuf[:, C_DWB:C_DWB + 8], in_=dwbcol).then_inc(*sW.inc(16))
    nc.sync.dma_start(out=colbuf[:, C_PWB:C_PWB + 4], in_=pwbc).then_inc(*sW.inc(16))
    nc.sync.dma_start(out=colbuf[:, C_OPB:C_OPB + 8], in_=opbc).then_inc(*sW.inc(16))
    nc.sync.dma_start(out=colbuf[:, C_DTB:C_DTB + 1], in_=dtb2).then_inc(*sW.inc(16))
    sW_const = sW.n

    atmp = es.enter_context(sb("atmp", (HPC, DS)))
    nc.sync.dma_start(out=atmp[:], in_=alog8).then_inc(*sW.inc(16))
    nc.scalar.wait_ge(sW.h, sW.n)
    nc.scalar.activation(out=atmp[:], in_=atmp[:], func=AF.Exp).then_inc(*sSq.inc())
    nc.vector.wait_ge(sSq.h, sSq.n)
    nc.vector.tensor_scalar_mul(atmp[:], atmp[:], -1.0).then_inc(*sV.inc())
    aeo = es.enter_context(sb("aeo", (HPC, DS)))
    nc.vector.wait_ge(sV.h, sV.n)
    nc.vector.tensor_copy(out=aeo[:, 0:NSH], in_=atmp[:, 0:DS:2]).then_inc(*sV.inc())
    nc.vector.wait_ge(sV.h, sV.n)
    nc.vector.tensor_copy(out=aeo[:, NSH:DS], in_=atmp[:, 1:DS:2]).then_inc(*sV.inc())
    nc.gpsimd.wait_ge(sV.h, sV.n)
    for h in range(HPC):
        src0 = aeo[h:h + 1, 0:NSH].unsqueeze(1).broadcast_to((1, 64, NSH))
        src1 = aeo[h:h + 1, NSH:DS].unsqueeze(1).broadcast_to((1, 64, NSH))
        nc.gpsimd.dma_start(out=ascl[0:64, h * NSH:(h + 1) * NSH], in_=src0).then_inc(*sW.inc(16))
        nc.gpsimd.dma_start(out=ascl[64:128, h * NSH:(h + 1) * NSH], in_=src1).then_inc(*sW.inc(16))

    psA = es.enter_context(nc.psum_tensor("psA", [128, 512], F32))
    psB = es.enter_context(nc.psum_tensor("psB", [128, 512], F32))
    psC = es.enter_context(nc.psum_tensor("psC", [128, 512], F32))
    psD = es.enter_context(nc.psum_tensor("psD", [128, 512], F32))
    psS = es.enter_context(nc.psum_tensor("psS", [64, 512], F32))

    pid = nc.partition_id([Pool])
    parity = nc.gpsimd.compute_val(pid % 2)

    def make_compose(stg, Rbuf):
        def compose(dst, step_stack, steps, init_stack, init_idx):
                nc.sync.wait_ge(sT.h, sT.n)
                nc.sync.dma_start(out=Rbuf[:],
                                  in_=init_stack[init_idx]) \
                    .then_inc(*sXs.inc(16))
                cur, other = Rbuf, dst
                for s in steps:
                    nc.sync.wait_ge(sT.h, sT.n)
                    nc.sync.dma_start(out=stg[:],
                                      in_=step_stack[s]) \
                        .then_inc(*sXs.inc(16))
                    nc.tensor.wait_ge(sXs.h, sXs.n)
                    nc.tensor.wait_ge(sV.h, sV.n)
                    for grp in range(8):
                        nc.tensor.wait_ge(sEv.h, sEv.n)
                        for jb in range(8):
                            blk = grp * 8 + jb
                            mm = nc.tensor.matmul(psA[0:64, jb * 64:(jb + 1) * 64],
                                                  stg[:, blk * 64:(blk + 1) * 64],
                                                  cur[:, blk * 64:(blk + 1) * 64],
                                                  start=True, stop=True)
                            if jb == 7:
                                mm.then_inc(*sT.inc())
                        nc.scalar.wait_ge(sT.h, sT.n)
                        nc.scalar.copy(out=other[0:64, grp * 512:(grp + 1) * 512],
                                       in_=psA[0:64, 0:512]).then_inc(*sEv.inc())
                    cur, other = other, cur
                if cur is not dst:
                    nc.vector.wait_ge(sEv.h, sEv.n)
                    nc.vector.tensor_copy(out=dst[:], in_=cur[:]).then_inc(*sV.inc())
        return compose

    with sb("DupT", (64, 4096)) as DupT, sb("HupT", (64, 4096)) as HupT:
        with sb("stg", (64, 4096)) as stg, sb("Rbuf", (64, 4096)) as Rbuf:
            compose = make_compose(stg, Rbuf)
            compose(DupT, bstk["bfu_low"], [4, 3, 2, 1, 0], bstk["bfu_lowT"], 5)
            compose(HupT, bstk["bfu_high"], [4, 3, 2, 1, 0], bstk["bfu_highT"], 5)
            nc.sync.wait_ge(sEv.h, sEv.n)
            nc.sync.wait_ge(sV.h, sV.n)

        with sb("V2", (128, 32 * DP1)) as V2, sb("Eb", (64, DP1)) as Eb, \
             sb("Vsh", (64, DP1)) as Vsh, sb("wst", (128, 64), BF16) as wst, \
             sb("bst", (1, 64)) as bst:
            for ih in range(64):
                pe = ih % 2
                nc.sync.wait_ge(sT.h, sT.n)
                nc.sync.dma_start(out=Eb[:],
                                  in_=EXPc[ih * 64:(ih + 1) * 64, :]).then_inc(*sXs.inc(16))
                nc.tensor.wait_ge(sXs.h, sXs.n)
                for nch in range(3):
                    n0 = nch * 512
                    n1 = min(n0 + 512, DP1)
                    nc.tensor.wait_ge(sEv.h, sEv.n)
                    nc.tensor.matmul(psB[0:64, 0:n1 - n0],
                                     DupT[:, ih * 64:(ih + 1) * 64],
                                     Eb[:, n0:n1],
                                     start=True, stop=True).then_inc(*sT.inc())
                    nc.scalar.wait_ge(sT.h, sT.n)
                    nc.scalar.copy(out=V2[pe * 64:pe * 64 + 64,
                                          (ih // 2) * DP1 + n0:(ih // 2) * DP1 + n1],
                                   in_=psB[0:64, 0:n1 - n0]).then_inc(*sEv.inc())
            nc.gpsimd.wait_ge(sEv.h, sEv.n)
            for jl in range(64):
                srcA = V2[jl:jl + 1, :].rearrange("p (g c) -> p g c", g=32)
                srcB = V2[64 + jl:64 + jl + 1, :].rearrange("p (g c) -> p g c", g=32)
                dstA = bass.AP(Vsh[:].tensor, 0, [[2 * DP1, 32], [1, DP1]])
                dstB = bass.AP(Vsh[:].tensor, DP1, [[2 * DP1, 32], [1, DP1]])
                nc.gpsimd.wait_ge(sT.h, sT.n)
                nc.gpsimd.dma_start(out=dstA, in_=srcA).then_inc(*sSp.inc(16))
                nc.gpsimd.dma_start(out=dstB, in_=srcB).then_inc(*sSp.inc(16))
                nc.tensor.wait_ge(sSp.h, sSp.n)
                for dch in range(9):
                    d0 = dch * 128
                    dl = 128 if dch < 8 else 1
                    nc.tensor.wait_ge(sEv.h, sEv.n)
                    nc.tensor.matmul(psC[0:dl, 0:64],
                                     Vsh[:, d0:d0 + dl],
                                     HupT[:, jl * 64:(jl + 1) * 64],
                                     start=True, stop=True).then_inc(*sT.inc())
                    nc.scalar.wait_ge(sT.h, sT.n)
                    if dch < 8:
                        nc.scalar.copy(out=wst[:, 0:64], in_=psC[0:128, 0:64]).then_inc(*sEv.inc())
                        nc.gpsimd.wait_ge(sEv.h, sEv.n)
                        nc.gpsimd.dma_start(out=WexpP[dch, :, jl * 64:(jl + 1) * 64],
                                            in_=wst[:, 0:64]).then_inc(*sSp.inc(16))
                    else:
                        nc.scalar.copy(out=bst[:], in_=psC[0:1, 0:64]).then_inc(*sEv.inc())
                        nc.gpsimd.wait_ge(sEv.h, sEv.n)
                        nc.gpsimd.dma_start(out=WexpB[0:1, jl * 64:(jl + 1) * 64],
                                            in_=bst[:]).then_inc(*sSp.inc(16))
                    nc.scalar.wait_ge(sSp.h, sSp.n)

    with sb("Hdn", (64, 4096)) as Hdn, sb("Ddn", (64, 4096)) as Ddn:
        with sb("stg2", (64, 4096)) as stg, sb("Rbuf2", (64, 4096)) as Rbuf:
            compose = make_compose(stg, Rbuf)
            compose(Hdn, bstk["bfd_highT"], [1, 2, 3, 4, 5], bstk["bfd_high"], 0)
            compose(Ddn, bstk["bfd_lowT"], [1, 2, 3, 4, 5], bstk["bfd_low"], 0)
            nc.sync.wait_ge(sEv.h, sEv.n)
            nc.sync.wait_ge(sV.h, sV.n)
        with sb("U2", (128, 32 * D)) as U2, sb("Wcb", (64, 2 * D)) as Wcb, \
             sb("Ush", (64, 2 * D)) as Ush, sb("wst2", (64, D), BF16) as wst2:
            for jl in range(64):
                pe = jl % 2
                nc.sync.wait_ge(sT.h, sT.n)
                nc.sync.dma_start(out=Wcb[:, pe * D:(pe + 1) * D],
                                  in_=WCS[jl * 64:(jl + 1) * 64, :]).then_inc(*sXs.inc(16))
                nc.tensor.wait_ge(sXs.h, sXs.n)
                for nch in range(2):
                    n0 = nch * 512
                    nc.tensor.wait_ge(sEv.h, sEv.n)
                    nc.tensor.matmul(psB[0:64, 0:512],
                                     Hdn[:, jl * 64:(jl + 1) * 64],
                                     Wcb[:, pe * D + n0:pe * D + n0 + 512],
                                     start=True, stop=True).then_inc(*sT.inc())
                    nc.scalar.wait_ge(sT.h, sT.n)
                    nc.scalar.copy(out=U2[pe * 64:pe * 64 + 64,
                                          (jl // 2) * D + n0:(jl // 2) * D + n0 + 512],
                                   in_=psB[0:64, 0:512]).then_inc(*sEv.inc())
            nc.gpsimd.wait_ge(sEv.h, sEv.n)
            for ih in range(64):
                pe = ih % 2
                srcA = U2[ih:ih + 1, :].rearrange("p (g c) -> p g c", g=32)
                srcB = U2[64 + ih:64 + ih + 1, :].rearrange("p (g c) -> p g c", g=32)
                dstA = bass.AP(Ush[:].tensor, pe * D, [[2 * (2 * D), 32], [1, D]])
                dstB = bass.AP(Ush[:].tensor, (2 * D) + pe * D, [[2 * (2 * D), 32], [1, D]])
                nc.gpsimd.wait_ge(sT.h, sT.n)
                nc.gpsimd.dma_start(out=dstA, in_=srcA).then_inc(*sSp.inc(16))
                nc.gpsimd.dma_start(out=dstB, in_=srcB).then_inc(*sSp.inc(16))
                nc.tensor.wait_ge(sSp.h, sSp.n)
                for nch in range(2):
                    n0 = nch * 512
                    nc.tensor.wait_ge(sEv.h, sEv.n)
                    nc.tensor.matmul(psB[0:64, 0:512],
                                     Ddn[:, ih * 64:(ih + 1) * 64],
                                     Ush[:, pe * D + n0:pe * D + n0 + 512],
                                     start=True, stop=True).then_inc(*sT.inc())
                    nc.scalar.wait_ge(sT.h, sT.n)
                    nc.scalar.copy(out=wst2[0:64, n0:n0 + 512],
                                   in_=psB[0:64, 0:512]).then_inc(*sEv.inc())
                nc.gpsimd.wait_ge(sEv.h, sEv.n)
                dstw = bass.AP(WconP.tensor, ih * D, [[64 * D, 64], [1, D]])
                nc.gpsimd.dma_start(out=dstw, in_=wst2[0:64, :]).then_inc(*sSp.inc(16))
                nc.scalar.wait_ge(sSp.h, sSp.n)

    nc.sync.wait_ge(sSp.h, sSp.n)
    nc.sync.dma_start(out=b1bnc, in_=WexpB).then_inc(*sW.inc(16))
    nc.sync.wait_ge(sW.h, sW.n)
    nc.sync.dma_start(out=b1col[:], in_=bass.AP(b1bnc.tensor, 0, [[1, 128], [128, HF // 128]]),
                      allow_slow_non_contiguous=True).then_inc(*sW.inc(16))
    nc.sync.dma_start(out=b2col[:], in_=bcncol).then_inc(*sW.inc(16))
    with sb("tmpc", (128, NSLAB + HF // 128)) as tmpc:
        nc.sync.dma_start(out=tmpc[:, 0:NSLAB], in_=dbcol).then_inc(*sW.inc(16))
        nc.sync.dma_start(out=tmpc[:, NSLAB:], in_=ubcol).then_inc(*sW.inc(16))
        nc.vector.wait_ge(sW.h, sW.n)
        nc.vector.tensor_add(b2col[:], b2col[:], tmpc[:, 0:NSLAB])
        nc.vector.tensor_add(b1col[:], b1col[:], tmpc[:, NSLAB:]).then_inc(*sV.inc())

    if dbg:
        nc.gpsimd.wait_ge(sSp.h, sSp.n)
        nc.gpsimd.dma_start(out=dbg_out["dW1"], in_=WexpP[0]).then_inc(*sOut.inc(16))
        nc.gpsimd.dma_start(out=dbg_out["dWc"], in_=WconP[0:128, :]).then_inc(*sOut.inc(16))

    def finish():
        nc.gpsimd.wait_ge(sOut.h, sOut.n)
        es.close()
        return nc

    if stage < 1:
        return finish()

    esF = ExitStack()
    rows_ref["t"] = esF.enter_context(sb("rows", (128, 3 * L)))
    with sb("xsl", (128, 2 * L)) as xsl, sb("sqt", (128, L)) as sqt:
        for k in range(NSLAB):
            pe = k % 2
            nc.sync.wait_ge(sSq.h, sSq.n)
            nc.sync.wait_ge(sT.h, sT.n)
            nc.sync.dma_start(out=xsl[:, pe * L:(pe + 1) * L],
                              in_=xT[k * 128:(k + 1) * 128, :]).then_inc(*sXs.inc(16))
            nc.scalar.wait_ge(sXs.h, sXs.n)
            nc.scalar.activation(out=sqt[:], in_=xsl[:, pe * L:(pe + 1) * L],
                                 func=AF.Square).then_inc(*sSq.inc())
            nc.tensor.wait_ge(sSq.h, sSq.n)
            for nch in range(8):
                nc.tensor.wait_ge(sEv.h, sEv.n)
                nc.tensor.matmul(psS[0:1, :], ones[:],
                                 xsl[:, pe * L + nch * 512:pe * L + (nch + 1) * 512],
                                 start=True, stop=True)
                nc.tensor.matmul(psA[0:1, :], ones[:],
                                 sqt[:, nch * 512:(nch + 1) * 512],
                                 start=True, stop=True).then_inc(*sT.inc())
                nc.scalar.wait_ge(sT.h, sT.n)
                nc.scalar.copy(out=ROW(6, nch * 512, (nch + 1) * 512), in_=psS[0:1, :])
                nc.scalar.copy(out=ROW(7, nch * 512, (nch + 1) * 512),
                               in_=psA[0:1, :]).then_inc(*sEv.inc())
            nc.vector.wait_ge(sEv.h, sEv.n)
            if k == 0:
                nc.vector.tensor_scalar_mul(ROW(R_M), ROW(6), 1.0 / D)
                nc.vector.tensor_scalar_mul(ROW(R_T0), ROW(7), 1.0 / D)
            else:
                nc.vector.scalar_tensor_tensor(out=ROW(R_M), in0=ROW(6),
                                               scalar=1.0 / D, in1=ROW(R_M),
                                               op0=ALU.mult, op1=ALU.add)
                nc.vector.scalar_tensor_tensor(out=ROW(R_T0), in0=ROW(7),
                                               scalar=1.0 / D, in1=ROW(R_T0),
                                               op0=ALU.mult, op1=ALU.add)
            nc.vector.engine_nop().then_inc(*sV.inc())
            nc.sync.wait_ge(sV.h, sV.n)
        nc.vector.wait_ge(sV.h, sV.n)
        for opf in [
            lambda: nc.vector.tensor_mul(ROW(R_T1), ROW(R_M), ROW(R_M)),
            lambda: nc.vector.tensor_sub(ROW(R_T1), ROW(R_T0), ROW(R_T1)),
            lambda: nc.vector.tensor_scalar_add(ROW(R_T2), ROW(R_T1), 1e-5),
            lambda: nc.vector.reciprocal(ROW(R_T0), ROW(R_T2)),
        ]:
            opf().then_inc(*sV.inc())
            nc.vector.wait_ge(sV.h, sV.n)
        nc.scalar.wait_ge(sV.h, sV.n)
        nc.scalar.activation(out=ROW(R_RLN), in_=ROW(R_T0),
                             func=AF.Sqrt).then_inc(*sSq.inc())
        nc.vector.wait_ge(sSq.h, sSq.n)
        for opf in [
            lambda: nc.vector.tensor_mul(ROW(R_T2), ROW(R_T1), ROW(R_T0)),
            lambda: nc.vector.tensor_scalar_add(ROW(R_T2), ROW(R_T2), 1e-6),
            lambda: nc.vector.reciprocal(ROW(R_T2), ROW(R_T2)),
        ]:
            opf().then_inc(*sV.inc())
            nc.vector.wait_ge(sV.h, sV.n)
        nc.vector.engine_nop().then_inc(*sV.inc())
        nc.scalar.wait_ge(sV.h, sV.n)
        nc.scalar.activation(out=ROW(R_T2), in_=ROW(R_T2),
                             func=AF.Sqrt).then_inc(*sSq.inc())
        nc.vector.wait_ge(sSq.h, sSq.n)
        nc.vector.tensor_mul(ROW(R_GU), ROW(R_RLN),
                             ROW(R_T2)).then_inc(*sV.inc())
    if dbg:
        nc.gpsimd.wait_ge(sV.h, sV.n)
        nc.gpsimd.dma_start(out=dbg_out["drows"][0:1, :], in_=ROW(R_M)).then_inc(*sOut.inc(16))
        nc.gpsimd.dma_start(out=dbg_out["drows"][1:2, :], in_=ROW(R_RLN)).then_inc(*sOut.inc(16))
        nc.gpsimd.dma_start(out=dbg_out["drows"][2:3, :], in_=ROW(R_GU)).then_inc(*sOut.inc(16))

    if stage < 2:
        esF.close()
        return finish()
    nc.gpsimd.wait_ge(sV.h, sV.n)
    for case in nc.gpsimd.Switch(parity, 2):
        o = case * TH
        nc.gpsimd.dma_start(out=hrows[0:1, :], in_=ROW(R_M, o, o + TH)).then_inc(sBk.h, 16)
        nc.gpsimd.dma_start(out=hrows[32:33, :], in_=ROW(R_RLN, o, o + TH)).then_inc(sBk.h, 16)
    sBk.n += 32

    if stage < 3:
        esF.close()
        return finish()

    with sb("ucsl", (128, NSLAB * L), BF16) as ucsl:
      with sb("mB", (128, L), BF16) as mB, sb("rlnB", (128, L), BF16) as rlnB, sb("guB", (128, L), BF16) as guB, \
         sb("xsl2", (128, L)) as xsl2, sb("t1t", (128, L)) as t1t, \
         sb("xnsl", (128, L), BF16) as xnsl, sb("upad", (128, L + KC - 1)) as upad:
        nc.sync.wait_ge(sV.h, sV.n)
        nc.gpsimd.wait_ge(sV.h, sV.n)
        nc.gpsimd.dma_start(out=mB[:],
                            in_=ROW(R_M).unsqueeze(1).broadcast_to((1, 128, L))) \
            .then_inc(*sW.inc(16))
        for rsrc, rdst in [(R_RLN, rlnB), (R_GU, guB)]:
            nc.gpsimd.dma_start(out=rdst[:],
                                in_=ROW(rsrc).unsqueeze(1).broadcast_to((1, 128, L))) \
                .then_inc(*sW.inc(16))
        nc.vector.wait_ge(sW.h, sW.n)
        nc.vector.memset(upad[:, 0:KC - 1], 0.0)
        for k in range(NSLAB):
            pe = k % 2
            nc.sync.wait_ge(sV.h, sV.n)
            nc.sync.dma_start(out=xsl2[:],
                              in_=xT[k * 128:(k + 1) * 128, :]).then_inc(*sXs.inc(16))
            nc.vector.wait_ge(sXs.h, sXs.n)
            nc.vector.wait_ge(sSp.h, sSp.n)
            xs = xsl2[:]
            nc.vector.tensor_sub(t1t[:], xs, mB[:])
            nc.vector.tensor_mul(xnsl[:], t1t[:], rlnB[:])
            nc.vector.scalar_tensor_tensor(out=upad[:, KC - 1:], in0=t1t[:],
                                           scalar=colbuf[:, C_W1 + k:C_W1 + k + 1], in1=guB[:],
                                           op0=ALU.mult, op1=ALU.mult)
            nc.vector.tensor_scalar_mul(t1t[:], upad[:, 0:L],
                                        colbuf[:, C_KER + k * 4:C_KER + k * 4 + 1])
            for kk in range(1, KC):
                dst = ucsl[:, k * L:(k + 1) * L] if kk == KC - 1 else t1t[:]
                nc.vector.scalar_tensor_tensor(out=dst, in0=upad[:, kk:kk + L],
                                               scalar=colbuf[:, C_KER + k * 4 + kk:C_KER + k * 4 + kk + 1],
                                               in1=t1t[:], op0=ALU.mult, op1=ALU.add)
            nc.vector.engine_nop().then_inc(*sV.inc())
            for case in nc.gpsimd.Switch(parity, 2):
                o = case * TH
                nc.gpsimd.wait_ge(sV.h, sV.n)
                nc.gpsimd.dma_start(out=xnhSp[k], in_=xnsl[:, o:o + TH]).then_inc(sSp.h, 16)
            sSp.n += 16
            if dbg and k == 0:
                nc.gpsimd.wait_ge(sV.h, sV.n)
                nc.gpsimd.dma_start(out=dbg_out["dxn"], in_=xnsl[:]).then_inc(*sOut.inc(16))
                nc.gpsimd.dma_start(out=dbg_out["duc"], in_=ucsl[:, 0:L]).then_inc(*sOut.inc(16))
            if dbg:
                nc.vector.wait_ge(sOut.h, sOut.n)

      if True:
        nc.gpsimd.wait_ge(sSp.h, sSp.n)
        with sb("pwTs", (128, NSLAB * MYC), BF16) as pwTs, sb("pwbe", (128, 4)) as pwbe, \
             sb("dwbf", (128, NSLAB), BF16) as dwbf, \
             sb("xhs", (128, (HPC // 2) * L), BF16) as xhs:
            nc.gpsimd.wait_ge(sW.h, sW_const)
            nc.gpsimd.dma_start(out=pwTs[:], in_=_slabs(pwT)) \
                .then_inc(*sSp.inc(16))
            nc.gpsimd.dma_start(out=dwbf[:], in_=colbuf[:, C_DWB:C_DWB + 8]).then_inc(*sSp.inc(16))
            nc.tensor.wait_ge(sSp.h, sSp.n)
            for m in range(4):
                for k in range(NSLAB):
                    mm = nc.tensor.matmul(psA[0:128, m:m + 1],
                                          pwTs[:, k * MYC + m * 128:k * MYC + (m + 1) * 128],
                                          dwbf[:, k:k + 1],
                                          start=(k == 0), stop=(k == NSLAB - 1))
                    if k == NSLAB - 1:
                        mm.then_inc(*sT.inc())
            nc.scalar.wait_ge(sT.h, sT.n)
            for m in range(4):
                nc.scalar.activation(out=pwbe[:, m:m + 1], in_=psA[0:128, m:m + 1],
                                     func=AF.Identity, bias=colbuf[:, C_PWB + m:C_PWB + m + 1])
            nc.scalar.activation(out=pwbe[:, 0:4], in_=pwbe[:, 0:4], func=AF.Copy) \
                .then_inc(*sSq.inc())
            nc.tensor.wait_ge(sV.h, sV.n)
            nc.tensor.wait_ge(sSq.h, sSq.n)
            for m in range(4):
                psm = [psA, psB, psC, psD][m]
                for nch in range(8):
                    nc.tensor.wait_ge(sEv.h, sEv.n)
                    for k in range(NSLAB):
                        mm = nc.tensor.matmul(psm[:, 0:512],
                                              pwTs[:, k * MYC + m * 128:k * MYC + (m + 1) * 128],
                                              ucsl[:, k * L + nch * 512:k * L + (nch + 1) * 512],
                                              start=(k == 0), stop=(k == NSLAB - 1))
                        if k == NSLAB - 1:
                            mm.then_inc(*sT.inc())
                    nc.scalar.wait_ge(sT.h, sT.n)
                    nc.scalar.activation(out=xhs[:, m * L + nch * 512:m * L + (nch + 1) * 512],
                                         in_=psm[:, 0:512], func=AF.Identity,
                                         bias=pwbe[:, m:m + 1]).then_inc(*sEv.inc())
            nc.gpsimd.wait_ge(sEv.h, sEv.n)
            for j in range(HPC // 2):
                nc.gpsimd.dma_start(out=xhSp[j], in_=xhs[:, j * L:(j + 1) * L]).then_inc(*sSp.inc(16))
            if dbg:
                nc.gpsimd.dma_start(out=dbg_out["dxh"], in_=xhs[:, 0:L]).then_inc(*sOut.inc(16))

            with sb("xpTs", (128, 320), BF16) as xpTs, sb("bct", (96, L), BF16) as bct, \
                 sb("dtxt", (128, L), BF16) as dtxt:
                nc.gpsimd.dma_start(out=xpTs[:], in_=xpT).then_inc(*sSp.inc(16))
                nc.tensor.wait_ge(sSp.h, sSp.n)
                for h in range(HPC):
                    sl, po = h // 2, (h % 2) * 64
                    wo = (h % 2) * 160
                    for nch in range(8):
                        nc.tensor.wait_ge(sEv.h, sEv.n)
                        nc.tensor.matmul(psC[0:64, 0:512], xpTs[:, wo:wo + 64],
                                         xhs[:, sl * L + nch * 512:sl * L + (nch + 1) * 512],
                                         start=True, stop=True)
                        nc.tensor.matmul(psD[0:96, 0:512], xpTs[:, wo + 64:wo + 160],
                                         xhs[:, sl * L + nch * 512:sl * L + (nch + 1) * 512],
                                         start=True, stop=True).then_inc(*sT.inc())
                        nc.scalar.wait_ge(sT.h, sT.n)
                        nc.scalar.copy(out=dtxt[po:po + 64, nch * 512:(nch + 1) * 512],
                                       in_=psC[0:64, 0:512])
                        nc.scalar.copy(out=bct[:, nch * 512:(nch + 1) * 512],
                                       in_=psD[0:96, 0:512]).then_inc(*sEv.inc())
                    nc.gpsimd.wait_ge(sEv.h, sEv.n)
                    nc.gpsimd.dma_start(out=bcSp[h], in_=bct[:]).then_inc(*sSp.inc(16))
                    if h % 2 == 1:
                        nc.gpsimd.dma_start(out=dtxSp[h // 2], in_=dtxt[:]).then_inc(*sSp.inc(16))
                    if dbg and h == 0:
                        nc.gpsimd.dma_start(out=dbg_out["dbc"], in_=bct[:]).then_inc(*sOut.inc(16))
                    if dbg and h == 1:
                        nc.gpsimd.dma_start(out=dbg_out["ddtx"], in_=dtxt[:]).then_inc(*sOut.inc(16))
                    if dbg:
                        nc.scalar.wait_ge(sOut.h, sOut.n)
                    nc.scalar.wait_ge(sSp.h, sSp.n)

    esF.close()
    if stage < 5:
        return finish()

    with sb("ystg", (64, L), BF16) as ystg, sb("yfold", (64, L)) as yfold, \
         sb("dtpTs", (128, 128), BF16) as dtpTs, \
         sb("dtx2", (128, L), BF16) as dtx2, sb("dtp", (128, L)) as dtp, \
         sb("zsp", (128, 1024)) as zsp, sb("dt2", (128, L)) as dt2, \
         sb("xh2", (128, L), BF16) as xh2, sb("w2t", (128, L)) as w2t, \
         sb("abar", (128, L)) as abar, sb("inp", (128, L)) as inpt, \
         sb("hst", (128, L)) as hst, sb("y2t", (128, L)) as y2t, \
         sb("bcl", (96, L), BF16) as bcl, \
         sb("Bb", (128, 2 * L), BF16) as Bb, sb("Cb", (128, 2 * L), BF16) as Cb:
        nc.gpsimd.dma_start(out=dtpTs[:], in_=dtpT).then_inc(*sW.inc(16))
        consumed_B = {0: 0, 1: 0}
        consumed_C = {0: 0, 1: 0}
        for h in range(HPC):
            pr, po = h // 2, (h % 2) * 64
            if h % 2 == 0:
                nc.sync.wait_ge(sVs.h, sVs.n)
                nc.sync.wait_ge(sW.h, sW.n)
                nc.sync.dma_start(out=dtx2[:], in_=dtxSp[pr]).then_inc(*sXs.inc(16))
                nc.tensor.wait_ge(sXs.h, sXs.n)
                for nch in range(8):
                    nc.tensor.wait_ge(sSq.h, sSq.n)
                    nc.tensor.matmul(psA[0:64, 0:512], dtpTs[:, 0:64],
                                     dtx2[:, nch * 512:(nch + 1) * 512],
                                     start=True, stop=True)
                    nc.tensor.matmul(psB[0:64, 0:512], dtpTs[:, 64:128],
                                     dtx2[:, nch * 512:(nch + 1) * 512],
                                     start=True, stop=True).then_inc(*sT.inc())
                    nc.scalar.wait_ge(sT.h, sT.n)
                    nc.scalar.wait_ge(sV.h, sV.n)
                    zpe = (nch % 2) * 512
                    zs = zsp[:, zpe:zpe + 512]
                    nc.scalar.activation(out=zsp[0:64, zpe:zpe + 512],
                                         in_=psA[0:64, 0:512], func=AF.Identity,
                                         bias=colbuf[0:64, C_DTB:C_DTB + 1])
                    nc.scalar.activation(out=zsp[64:128, zpe:zpe + 512],
                                         in_=psB[0:64, 0:512], func=AF.Identity,
                                         bias=colbuf[64:128, C_DTB:C_DTB + 1])
                    ab = dtp[:, nch * 512:(nch + 1) * 512]
                    nc.scalar.activation(out=ab, in_=zs, func=AF.Abs)
                    nc.scalar.activation(out=ab, in_=ab, func=AF.Exp, scale=-1.0)
                    nc.scalar.activation(out=ab, in_=ab, func=AF.Ln, bias=1.0).then_inc(*sSq.inc())
                    nc.vector.wait_ge(sSq.h, sSq.n)
                    nc.vector.scalar_tensor_tensor(out=ab, in0=zs, scalar=0.0, in1=ab,
                                                   op0=ALU.max, op1=ALU.add).then_inc(*sV.inc())
                if dbg and h == 0:
                    nc.gpsimd.wait_ge(sV.h, sV.n)
                    nc.gpsimd.dma_start(out=dbg_out["ddt"], in_=dtp[:]).then_inc(*sOut.inc(16))
                if dbg:
                    nc.scalar.wait_ge(sOut.h, sOut.n)
            nc.sync.wait_ge(sV.h, sV.n)
            nc.sync.wait_ge(sVs.h, sVs.n)
            nc.sync.wait_ge(sBC.h, sBC.n)
            nc.sync.dma_start(out=dt2[0:64, :], in_=dtp[po:po + 64, :]).then_inc(*sXs.inc(16))
            nc.sync.dma_start(out=dt2[64:128, :], in_=dtp[po:po + 64, :]).then_inc(*sXs.inc(16))
            nc.sync.dma_start(out=xh2[0:64, :], in_=xhSp[pr][po:po + 64, :]).then_inc(*sXs.inc(16))
            nc.sync.dma_start(out=xh2[64:128, :], in_=xhSp[pr][po:po + 64, :]).then_inc(*sXs.inc(16))
            nc.sync.dma_start(out=bcl[:], in_=bcSp[h]).then_inc(*sXs.inc(16))
            sXs_tiles = sXs.n
            nc.vector.wait_ge(sXs.h, sXs.n)
            nc.vector.tensor_mul(w2t[:], dt2[:], xh2[:]).then_inc(*sV.inc())
            sV_w2 = sV.n
            abar_free = sVs.n
            for shi in range(NSH):
                pe = shi % 2
                nc.gpsimd.wait_ge(sVs.h, sVs.n)
                nc.gpsimd.wait_ge(sAB.h, sAB.n)
                nc.gpsimd.wait_ge(sBC.h, sBC.n)
                nc.gpsimd.wait_ge(sXs.h, sXs_tiles)
                nc.gpsimd.dma_start(out=Bb[:, pe * L:(pe + 1) * L],
                                    in_=bcl[2 * shi:2 * shi + 2, :].unsqueeze(1)
                                    .broadcast_to((2, 64, L))).then_inc(*sBC.inc(16))
                sBC_B = sBC.n
                nc.gpsimd.wait_ge(sBC.h, sBC.n)
                nc.gpsimd.dma_start(out=Cb[:, pe * L:(pe + 1) * L],
                                    in_=bcl[DS + 2 * shi:DS + 2 * shi + 2, :].unsqueeze(1)
                                    .broadcast_to((2, 64, L))).then_inc(*sBC.inc(16))
                sBC_C = sBC.n
                nc.scalar.wait_ge(sVs.h, abar_free)
                nc.scalar.wait_ge(sV.h, sV_w2)
                nc.scalar.activation(out=abar[:], in_=dt2[:], func=AF.Exp,
                                     scale=ascl[:, h * NSH + shi:h * NSH + shi + 1]) \
                    .then_inc(*sAB.inc())
                if dbg and h == 0 and shi == 0:
                    nc.gpsimd.wait_ge(sAB.h, sAB.n)
                    nc.gpsimd.dma_start(out=dbg_out["dffn"], in_=abar[:, 0:TH]).then_inc(*sOut.inc(16))
                    nc.vector.wait_ge(sOut.h, sOut.n)
                nc.vector.wait_ge(sBC.h, sBC_B)
                nc.vector.tensor_mul(inpt[:], w2t[:], Bb[:, pe * L:(pe + 1) * L]) \
                    .then_inc(*sVs.inc())
                consumed_B[pe] = sVs.n
                nc.vector.wait_ge(sAB.h, sAB.n)
                nc.vector.tensor_tensor_scan(out=hst[:], data0=abar[:], data1=inpt[:],
                                             initial=0.0, op0=ALU.mult, op1=ALU.add) \
                    .then_inc(*sVs.inc())
                abar_free = sVs.n
                nc.vector.wait_ge(sVs.h, abar_free)
                if dbg and h == 0 and shi == 0:
                    nc.gpsimd.wait_ge(sVs.h, abar_free)
                    nc.gpsimd.dma_start(out=dbg_out["dx1"], in_=hst[:, 0:TH]).then_inc(*sOut.inc(16))
                    nc.gpsimd.dma_start(out=dbg_out["dWc"][:, 0:D], in_=inpt[:, 0:D]).then_inc(*sOut.inc(16))
                    nc.vector.wait_ge(sOut.h, sOut.n)
                nc.vector.wait_ge(sBC.h, sBC_C)
                nc.vector.tensor_mul(hst[:], hst[:], Cb[:, pe * L:(pe + 1) * L]) \
                    .then_inc(*sVs.inc())
                consumed_C[pe] = sVs.n
                nc.vector.wait_ge(sVs.h, consumed_C[pe])
                if shi == 0:
                    nc.vector.tensor_copy(out=y2t[:], in_=hst[:]).then_inc(*sVs.inc())
                else:
                    nc.vector.tensor_add(y2t[:], y2t[:], hst[:]).then_inc(*sVs.inc())
                if dbg and h == 0 and shi in (0, 1):
                    nc.gpsimd.wait_ge(sVs.h, sVs.n)
                    nc.gpsimd.dma_start(out=dbg_out["dW1"][:, shi * 512:(shi + 1) * 512],
                                        in_=y2t[:, 0:512]).then_inc(*sOut.inc(16))
                    nc.gpsimd.dma_start(out=dbg_out["dW1"][:, 1024 + shi * 512:1024 + (shi + 1) * 512],
                                        in_=hst[:, 0:512]).then_inc(*sOut.inc(16))
                    nc.gpsimd.dma_start(out=dbg_out["dW1"][:, 2048 + shi * 512:2048 + (shi + 1) * 512],
                                        in_=Cb[:, pe * L:pe * L + 512]).then_inc(*sOut.inc(16))
                    nc.gpsimd.dma_start(out=dbg_out["dW1"][:, 3072 + shi * 512:3072 + (shi + 1) * 512],
                                        in_=Bb[:, pe * L:pe * L + 512]).then_inc(*sOut.inc(16))
                    nc.vector.wait_ge(sOut.h, sOut.n)
            nc.gpsimd.wait_ge(sVs.h, sVs.n)
            nc.gpsimd.dma_start(out=yfold[:], in_=y2t[64:128, :]).then_inc(*sSp.inc(16))
            nc.vector.wait_ge(sSp.h, sSp.n)
            nc.vector.tensor_add(ystg[:], y2t[0:64, :], yfold[:]).then_inc(*sVs.inc())
            nc.gpsimd.wait_ge(sVs.h, sVs.n)
            nc.gpsimd.dma_start(out=cin.ap()[pr][po:po + 64, :], in_=ystg[:]).then_inc(*sSp.inc(16))
            if dbg and h <= 1:
                nc.gpsimd.dma_start(out=dbg_out["dy"][po:po + 64, :], in_=ystg[:]).then_inc(*sOut.inc(16))
            if dbg:
                nc.vector.wait_ge(sOut.h, sOut.n)
        nc.gpsimd.wait_ge(sSp.h, sSp.n)

    if stage < 6:
        return finish()

    nc.gpsimd.wait_ge(sSp.h, sSp.n)
    nc.gpsimd.collective_compute(
        "AllGather", ALU.bypass,
        replica_groups=[[0, 1], [2, 3], [4, 5], [6, 7]],
        ins=[cin.ap().opt()], outs=[cout.ap().opt()],
    ).then_inc(*sCC.inc())

    if True:
        with sb("W1s", (128, NSLAB * HF), BF16) as W1s, sb("Wcs", (128, 32 * D), BF16) as Wcs, \
             sb("xnh", (128, NSLAB * 512), BF16) as xnh, \
             sb("h1t", (128, 32 * 512), BF16) as h1t, sb("ffo", (128, 512)) as ffo:
            nc.sync.wait_ge(sVs.h, sVs.n)
            nc.sync.wait_ge(sAB.h, sAB.n)
            nc.sync.wait_ge(sSp.h, sSp.n)
            for k in range(NSLAB):
                nc.sync.dma_start(out=W1s[:, k * HF:(k + 1) * HF], in_=WexpP[k]).then_inc(*sXs.inc(16))
            nc.sync.dma_start(out=Wcs[:], in_=_slabs(WconP)) \
                .then_inc(*sXs.inc(16))
            for st_ in range(TH // 512):
                for k in range(NSLAB):
                    nc.sync.wait_ge(sT.h, sT.n)
                    nc.sync.dma_start(out=xnh[:, k * 512:(k + 1) * 512],
                                      in_=xnhSp[k][:, st_ * 512:(st_ + 1) * 512]).then_inc(*sXs.inc(16))
                nc.tensor.wait_ge(sXs.h, sXs.n)
                nc.tensor.wait_ge(sV.h, sV.n)
                for m in range(32):
                    psm = [psA, psB][m % 2]
                    nc.tensor.wait_ge(sEv.h, sEv.n - 1 if m >= 2 else 0)
                    for k in range(NSLAB):
                        mm = nc.tensor.matmul(psm[:, 0:512],
                                              W1s[:, k * HF + m * 128:k * HF + (m + 1) * 128],
                                              xnh[:, k * 512:(k + 1) * 512],
                                              start=(k == 0), stop=(k == NSLAB - 1))
                        if k == NSLAB - 1:
                            mm.then_inc(*sT.inc())
                    nc.scalar.wait_ge(sT.h, sT.n)
                    nc.scalar.activation(out=h1t[:, m * 512:(m + 1) * 512], in_=psm[:, 0:512],
                                         func=AF.Gelu, bias=b1col[:, m:m + 1]).then_inc(*sEv.inc())
                for m in range(NSLAB):
                    nc.tensor.wait_ge(sEv.h, sEv.n)
                    nc.tensor.wait_ge(sSp.h, sSp.n)
                    for k in range(32):
                        mm = nc.tensor.matmul(psC[:, 0:512],
                                              Wcs[:, k * D + m * 128:k * D + (m + 1) * 128],
                                              h1t[:, k * 512:(k + 1) * 512],
                                              start=(k == 0), stop=(k == 31))
                        if k == 31:
                            mm.then_inc(*sT.inc())
                    nc.scalar.wait_ge(sT.h, sT.n)
                    nc.scalar.activation(out=ffo[:], in_=psC[:, 0:512], func=AF.Identity,
                                         bias=b2col[:, m:m + 1]).then_inc(*sEv.inc())
                    nc.gpsimd.wait_ge(sEv.h, sEv.n)
                    nc.gpsimd.dma_start(out=ffnSc[m][:, st_ * 512:(st_ + 1) * 512],
                                        in_=ffo[:]).then_inc(*sSp.inc(16))
            if dbg:
                nc.gpsimd.wait_ge(sSp.h, sSp.n)
                nc.gpsimd.dma_start(out=dbg_out["dffn"], in_=ffnSc[0]).then_inc(*sOut.inc(16))

        with sb("mhB", (128, TH)) as mhB, sb("rlhB", (128, TH)) as rlhB, \
             sb("sq2", (128, 512)) as sq2, sb("x1w", (128, 1024)) as x1w:
          with sb("yall", (128, NSLAB * TH), BF16) as yall, \
               sb("opTs", (128, NSLAB * D), BF16) as opTs, sb("xht", (128, TH)) as xht:
            nc.gpsimd.wait_ge(sCC.h, sCC.n)
            for case in nc.gpsimd.Switch(parity, 2):
                o = case * TH
                for slot in range(2):
                    for j in range(HPC // 2):
                        nc.gpsimd.dma_start(
                            out=yall[:, (slot * 4 + j) * TH:(slot * 4 + j + 1) * TH],
                            in_=cout.ap()[slot, j][:, o:o + TH]).then_inc(sBk.h, 16)
            sBk.n += 128
            nc.gpsimd.dma_start(out=opTs[:], in_=_slabs(opT)).then_inc(*sSp.inc(16))
            nc.gpsimd.wait_ge(sBk.h, sBk.n)
            nc.gpsimd.dma_start(out=mhB[:], in_=hrows[0:1, :].unsqueeze(1).broadcast_to((1, 128, TH))) \
                .then_inc(*sW.inc(16))
            nc.gpsimd.dma_start(out=rlhB[:], in_=hrows[32:33, :].unsqueeze(1).broadcast_to((1, 128, TH))) \
                .then_inc(*sW.inc(16))
            nc.tensor.wait_ge(sSp.h, sSp.n)
            nc.tensor.wait_ge(sBk.h, sBk.n)
            nc.vector.wait_ge(sW.h, sW.n)
            nc.sync.wait_ge(sSp.h, sSp.n)
            for m in range(NSLAB):
                nc.sync.wait_ge(sV.h, sV.n)
                nc.sync.dma_start(out=xht[:],
                                  in_=xTh[m * 128:(m + 1) * 128, :]).then_inc(*sXs.inc(16))
                for nch in range(TH // 512):
                    psm = [psA, psB][nch % 2]
                    pe = nch % 2
                    nc.tensor.wait_ge(sEv.h, sEv.n)
                    nc.tensor.wait_ge(sSp.h, sSp.n)
                    for k in range(NSLAB):
                        mm = nc.tensor.matmul(psm[:, 0:512],
                                              opTs[:, k * D + m * 128:k * D + (m + 1) * 128],
                                              yall[:, k * TH + nch * 512:k * TH + (nch + 1) * 512],
                                              start=(k == 0), stop=(k == NSLAB - 1))
                        if k == NSLAB - 1:
                            mm.then_inc(*sT.inc())
                    nc.scalar.wait_ge(sT.h, sT.n)
                    x1v = x1w[:, pe * 512:(pe + 1) * 512]
                    nc.scalar.activation(out=x1v, in_=psm[:, 0:512], func=AF.Identity,
                                         bias=colbuf[:, C_OPB + m:C_OPB + m + 1]).then_inc(*sEv.inc())
                    nc.vector.wait_ge(sEv.h, sEv.n)
                    nc.vector.wait_ge(sXs.h, sXs.n)
                    xv = xht[:, nch * 512:(nch + 1) * 512]
                    nc.vector.tensor_sub(sq2[:], xv, mhB[:, nch * 512:(nch + 1) * 512])
                    nc.vector.tensor_mul(sq2[:], sq2[:], rlhB[:, nch * 512:(nch + 1) * 512])
                    nc.vector.tensor_add(x1v, x1v, sq2[:]).then_inc(*sV.inc())
                    nc.gpsimd.wait_ge(sV.h, sV.n)
                    nc.gpsimd.dma_start(out=x1Sc[m][:, nch * 512:(nch + 1) * 512],
                                        in_=x1v).then_inc(*sSp.inc(16))
            if dbg:
                nc.gpsimd.wait_ge(sSp.h, sSp.n)
                nc.gpsimd.dma_start(out=dbg_out["dx1"], in_=x1Sc[0]).then_inc(*sOut.inc(16))
          nc.scalar.wait_ge(sV.h, sV.n)
          nc.sync.wait_ge(sSp.h, sSp.n)
          if True:
            for nch in range(TH // 512):
                for k in range(NSLAB):
                    pe2 = k % 2
                    nc.sync.wait_ge(sSq.h, sSq.n)
                    nc.sync.dma_start(out=x1w[:, pe2 * 512:(pe2 + 1) * 512],
                                      in_=x1Sc[k][:, nch * 512:(nch + 1) * 512]).then_inc(*sXs.inc(16))
                    nc.scalar.wait_ge(sXs.h, sXs.n)
                    nc.scalar.wait_ge(sT.h, sT.n) if k == 0 else None
                    nc.scalar.activation(out=sq2[:], in_=x1w[:, pe2 * 512:(pe2 + 1) * 512],
                                         func=AF.Square).then_inc(*sSq.inc())
                    nc.tensor.wait_ge(sSq.h, sSq.n)
                    mm = nc.tensor.matmul(psS[0:1, 0:512], ones[:], sq2[:],
                                          start=(k == 0), stop=(k == NSLAB - 1))
                    if k == NSLAB - 1:
                        mm.then_inc(*sT.inc())
                nc.scalar.wait_ge(sT.h, sT.n)
                nc.scalar.copy(out=hrows[64:65, nch * 512:(nch + 1) * 512],
                               in_=psS[0:1, 0:512]).then_inc(*sSq.inc())
            nc.vector.wait_ge(sSq.h, sSq.n)
            nc.vector.tensor_scalar(out=hrows[64:65, :], in0=hrows[64:65, :],
                                    scalar1=1.0 / D, scalar2=1e-6,
                                    op0=ALU.mult, op1=ALU.add).then_inc(*sV.inc())
            nc.vector.wait_ge(sV.h, sV.n)
            nc.vector.reciprocal(hrows[64:65, :], hrows[64:65, :]).then_inc(*sV.inc())
            nc.scalar.wait_ge(sV.h, sV.n)
            nc.scalar.activation(out=hrows[64:65, :], in_=hrows[64:65, :], func=AF.Sqrt) \
                .then_inc(*sSq.inc())
            nc.gpsimd.wait_ge(sSq.h, sSq.n)
            nc.gpsimd.dma_start(out=rlhB[:], in_=hrows[64:65, :].unsqueeze(1).broadcast_to((1, 128, TH))) \
                .then_inc(*sW.inc(16))

            with sb("gTs", (128, NSLAB * DI), BF16) as gTs, sb("uTs", (128, NSLAB * DI), BF16) as uTs, \
                 sb("dTs", (128, 16 * D), BF16) as dTs, sb("h2t", (128, NSLAB * 512), BF16) as h2t, \
                 sb("gsl", (128, 16 * 512), BF16) as gsl, sb("msl", (128, 16 * 512), BF16) as msl, \
                 sb("fres", (128, 2 * 512)) as fres, sb("xres", (128, 2 * 512)) as xres:
                nc.gpsimd.wait_ge(sV.h, sV.n)
                nc.gpsimd.dma_start(out=gTs[:], in_=_slabs(gT)) \
                    .then_inc(*sSp.inc(16))
                nc.gpsimd.dma_start(out=uTs[:], in_=_slabs(uT)) \
                    .then_inc(*sSp.inc(16))
                nc.gpsimd.dma_start(out=dTs[:], in_=_slabs(dTw)) \
                    .then_inc(*sSp.inc(16))
                nc.vector.wait_ge(sW.h, sW.n)
                nc.tensor.wait_ge(sSp.h, sSp.n)
                for st_ in range(TH // 512):
                    for k in range(NSLAB):
                        pe2 = k % 2
                        nc.sync.wait_ge(sV.h, sV.n)
                        nc.sync.dma_start(out=x1w[:, pe2 * 512:(pe2 + 1) * 512],
                                          in_=x1Sc[k][:, st_ * 512:(st_ + 1) * 512]) \
                            .then_inc(*sXs.inc(16))
                        nc.vector.wait_ge(sXs.h, sXs.n)
                        nc.vector.scalar_tensor_tensor(
                            out=h2t[:, k * 512:(k + 1) * 512],
                            in0=x1w[:, pe2 * 512:(pe2 + 1) * 512],
                            scalar=colbuf[:, C_W2 + k:C_W2 + k + 1],
                            in1=rlhB[:, st_ * 512:(st_ + 1) * 512],
                            op0=ALU.mult, op1=ALU.mult).then_inc(*sV.inc())
                    nc.vector.engine_nop().then_inc(*sV.inc())
                    nc.tensor.wait_ge(sV.h, sV.n)
                    for m in range(16):
                        psm = [psA, psB][m % 2]
                        nc.tensor.wait_ge(sEv.h, sEv.n - 1 if m >= 2 else 0)
                        for k in range(NSLAB):
                            mm = nc.tensor.matmul(psm[:, 0:512],
                                                  gTs[:, k * DI + m * 128:k * DI + (m + 1) * 128],
                                                  h2t[:, k * 512:(k + 1) * 512],
                                                  start=(k == 0), stop=(k == NSLAB - 1))
                            if k == NSLAB - 1:
                                mm.then_inc(*sT.inc())
                        nc.scalar.wait_ge(sT.h, sT.n)
                        nc.scalar.activation(out=gsl[:, m * 512:(m + 1) * 512],
                                             in_=psm[:, 0:512], func=AF.Silu).then_inc(*sEv.inc())
                    for m in range(16):
                        psm = [psC, psD][m % 2]
                        nc.tensor.wait_ge(sV.h, sV.n - 1 if m >= 2 else 0)
                        for k in range(NSLAB):
                            mm = nc.tensor.matmul(psm[:, 0:512],
                                                  uTs[:, k * DI + m * 128:k * DI + (m + 1) * 128],
                                                  h2t[:, k * 512:(k + 1) * 512],
                                                  start=(k == 0), stop=(k == NSLAB - 1))
                            if k == NSLAB - 1:
                                mm.then_inc(*sT.inc())
                        nc.vector.wait_ge(sT.h, sT.n)
                        nc.vector.wait_ge(sEv.h, sEv.n)
                        nc.vector.tensor_mul(msl[:, m * 512:(m + 1) * 512],
                                             psm[:, 0:512], gsl[:, m * 512:(m + 1) * 512]) \
                            .then_inc(*sV.inc())
                    for m in range(NSLAB):
                        pe = m % 2
                        nc.sync.wait_ge(sOut.h, sOut.n)
                        nc.sync.dma_start(out=fres[:, pe * 512:(pe + 1) * 512],
                                          in_=ffnSc[m][:, st_ * 512:(st_ + 1) * 512]) \
                            .then_inc(*sXs.inc(16))
                        sXs_ffn = sXs.n
                        psm = [psA, psB][m % 2]
                        nc.tensor.wait_ge(sV.h, sV.n)
                        nc.tensor.wait_ge(sEv.h, sEv.n)
                        for k in range(16):
                            mm = nc.tensor.matmul(psm[:, 0:512],
                                                  dTs[:, k * D + m * 128:k * D + (m + 1) * 128],
                                                  msl[:, k * 512:(k + 1) * 512],
                                                  start=(k == 0), stop=(k == 15))
                            if k == 15:
                                mm.then_inc(*sT.inc())
                        nc.scalar.wait_ge(sXs.h, sXs_ffn)
                        nc.scalar.wait_ge(sT.h, sT.n)
                        nc.sync.wait_ge(sV.h, sV.n)
                        nc.sync.dma_start(out=xres[:, pe * 512:(pe + 1) * 512],
                                          in_=xTh[m * 128:(m + 1) * 128, st_ * 512:(st_ + 1) * 512]) \
                            .then_inc(*sXs.inc(16))
                        nc.sync.wait_ge(sV.h, sV.n)
                        nc.sync.dma_start(out=x1w[:, pe * 512:(pe + 1) * 512],
                                          in_=x1Sc[m][:, st_ * 512:(st_ + 1) * 512]) \
                            .then_inc(*sXs.inc(16))
                        sXs_x1 = sXs.n
                        ov = fres[:, pe * 512:(pe + 1) * 512]
                        nc.vector.wait_ge(sXs.h, sXs_ffn)
                        nc.vector.wait_ge(sXs.h, sXs_x1)
                        nc.vector.wait_ge(sT.h, sT.n)
                        nc.vector.tensor_add(ov, ov, x1w[:, pe * 512:(pe + 1) * 512])
                        nc.vector.tensor_add(ov, ov, psm[:, 0:512]).then_inc(*sEv.inc())
                        nc.vector.wait_ge(sXs.h, sXs.n)
                        nc.vector.tensor_add(ov, ov, xres[:, pe * 512:(pe + 1) * 512]).then_inc(*sV.inc())
                        nc.gpsimd.wait_ge(sV.h, sV.n)
                        nc.gpsimd.dma_start(out=outT[m * 128:(m + 1) * 128, st_ * 512:(st_ + 1) * 512],
                                            in_=ov).then_inc(*sOut.inc(16))
    if probe:
        nc.gpsimd.wait_ge(sV.h, sV.n)
        nc.gpsimd.wait_ge(sSq.h, sSq.n)
        nc.gpsimd.wait_ge(sCC.h, sCC.n)
        nc.gpsimd.dma_start(out=hrowsD, in_=hrows[:]).then_inc(*sOut.inc(16))
        o1 = HF // 128
        o2 = o1 + NSLAB
        o3 = o2 + HPC * NSH
        nc.gpsimd.dma_start(out=bcolD[:, 0:o1], in_=b1col[:]).then_inc(*sOut.inc(16))
        nc.gpsimd.dma_start(out=bcolD[:, o1:o2], in_=b2col[:]).then_inc(*sOut.inc(16))
        nc.gpsimd.dma_start(out=bcolD[:, o2:o3], in_=ascl[:]).then_inc(*sOut.inc(16))
        nc.gpsimd.dma_start(out=bcolD[:, o3:o3 + 80], in_=colbuf[:]).then_inc(*sOut.inc(16))
        for j in range(HPC // 2):
            nc.gpsimd.dma_start(out=cinD[j], in_=cin.ap()[j]).then_inc(*sOut.inc(16))
            nc.gpsimd.dma_start(out=coutD[0, j], in_=cout.ap()[0, j]).then_inc(*sOut.inc(16))
            nc.gpsimd.dma_start(out=coutD[1, j], in_=cout.ap()[1, j]).then_inc(*sOut.inc(16))
    nc.gpsimd.wait_ge(sOut.h, sOut.n)
    es.close()
    return nc


_NC_CACHE = {}


def kernel(**inp):
    x = np.asarray(inp["L0_attn_x"], np.float32)
    n1w = np.asarray(inp["L0_attn_norm1_w"], np.float32)
    n2w = np.asarray(inp["L0_attn_norm2_w"], np.float32)
    dwk = np.asarray(inp["L0_attn_dw_kernel"], np.float32)
    dwb = np.asarray(inp["L0_attn_dw_bias"], np.float32)
    pww = np.asarray(inp["L0_attn_pw_w"], np.float32)
    pwb = np.asarray(inp["L0_attn_pw_b"], np.float32)
    xpw = np.asarray(inp["L0_attn_x_proj_w"], np.float32)
    dtw = np.asarray(inp["L0_attn_dt_proj_w"], np.float32)
    dtb = np.asarray(inp["L0_attn_dt_proj_b"], np.float32)
    alog = np.asarray(inp["L0_attn_A_log"], np.float32)
    opw = np.asarray(inp["L0_attn_out_proj_w"], np.float32)
    opb = np.asarray(inp["L0_attn_out_proj_b"], np.float32)
    gw = np.asarray(inp["L0_attn_gate_w"], np.float32)
    uw = np.asarray(inp["L0_attn_up_w"], np.float32)
    dw = np.asarray(inp["L0_attn_down_w"], np.float32)
    Wexp = np.asarray(inp["L0_ffn_W_exp"], np.float32)
    bexp = np.asarray(inp["L0_ffn_b_exp"], np.float32)
    upw_bf = np.asarray(inp["L0_ffn_up_weights"], np.float32)
    upb = np.asarray(inp["L0_ffn_up_bias"], np.float32)
    dnw_bf = np.asarray(inp["L0_ffn_down_weights"], np.float32)
    Wcon = np.asarray(inp["L0_ffn_W_con"], np.float32)
    bcon = np.asarray(inp["L0_ffn_b_con"], np.float32)
    dnb = np.asarray(inp["L0_ffn_down_bias"], np.float32)

    u_lowN, u_lowT, u_highN, u_highT = _stage_mats(upw_bf)
    d_lowN, d_lowT, d_highN, d_highT = _stage_mats(dnw_bf)
    ub_perm = np.ascontiguousarray(upb.reshape(64, 64).T).reshape(-1)

    common = dict(
        w1col=_col(n1w), w2col=_col(n2w),
        kercol=np.ascontiguousarray(
            dwk[:, 0, :].reshape(NSLAB, 128, KC).transpose(1, 0, 2).reshape(128, NSLAB * KC)),
        dwbcol=_col(dwb),
        xpT=_blkdiag2(xpw.T), dtpT=_blkdiag2(dtw.T),
        dtb2=np.ascontiguousarray(np.concatenate([dtb, dtb])[:, None]),
        opT=np.ascontiguousarray(opw.T), opbc=_col(opb),
        gT=np.ascontiguousarray(gw.T), uT=np.ascontiguousarray(uw.T),
        dTw=np.ascontiguousarray(dw.T),
        EXPc=np.ascontiguousarray(np.concatenate([Wexp.T, bexp[:, None]], axis=1)),
        WCS=np.ascontiguousarray(Wcon.reshape(64, 64, D).transpose(1, 0, 2).reshape(HF, D)),
        ubcol=_col(ub_perm), bcncol=_col(bcon), dbcol=_col(dnb),
        bfu_low=u_lowN, bfu_lowT=u_lowT, bfu_high=u_highN, bfu_highT=u_highT,
        bfd_low=d_lowN, bfd_lowT=d_lowT, bfd_high=d_highN, bfd_highT=d_highT,
    )

    in_maps = []
    for c in range(8):
        b, hg = c // 2, c % 2
        xbT = np.ascontiguousarray(x[b].T)
        m = dict(common)
        m["xT"] = xbT
        m["xTh"] = np.ascontiguousarray(xbT[:, hg * TH:(hg + 1) * TH])
        m["pwT"] = np.ascontiguousarray(pww.T[:, hg * MYC:(hg + 1) * MYC])
        m["pwbc"] = _col(pwb[hg * MYC:(hg + 1) * MYC])
        m["alog8"] = np.ascontiguousarray(alog[hg * HPC:(hg + 1) * HPC])
        in_maps.append(m)

    key = ("nc_dbg" if DEBUG else "nc") + str(STAGE) + ("p" if PROBE else "")
    if key not in _NC_CACHE:
        _NC_CACHE[key] = build_nc(dbg=DEBUG, stage=STAGE, probe=PROBE)
    nc = _NC_CACHE[key]
    res = run_bass_kernel_spmd(nc, in_maps, core_ids=list(range(8)))
    out = np.empty((B, L, D), np.float32)
    for c in range(8):
        b, th = c // 2, c % 2
        out[b, th * TH:(th + 1) * TH, :] = res.results[c]["outT"].T
    if DEBUG or PROBE:
        kernel.last_results = res
    return out


# version 03
# speedup vs baseline: 1.7209x; 1.7209x over previous
import numpy as np
from contextlib import ExitStack
import concourse.bass as bass
import concourse.mybir as mybir
from concourse.bass_utils import run_bass_kernel_spmd

F32 = mybir.dt.float32
BF16 = mybir.dt.bfloat16
AF = mybir.ActivationFunctionType
ALU = mybir.AluOpType
Pool = mybir.EngineType.Pool

B, L, D = 4, 4096, 1024
H, DH, DS, DTR = 16, 64, 48, 64
KC = 4
DI = 2048
HF = 4096
HPC = 8
MYC = 512
TH = 2048
NSH = DS // 2
NSLAB = D // 128
DP1 = D + 1
DEBUG = False
PROBE = False
STAGE = 9
SER_BCAST = False
SER_ONEFLIGHT = False


def _stage_mats(weights):
    w = np.asarray(weights, np.float32)
    lowN = np.zeros((6, 64, 64, 64), np.float32)
    highN = np.zeros((6, 64, 64, 64), np.float32)
    for s in range(12):
        s2 = 1 << s
        ws = w[s].reshape(HF // (2 * s2), s2, 2, 2)
        for blk in range(HF // (2 * s2)):
            for r in range(s2):
                for half in range(2):
                    i = blk * 2 * s2 + half * s2 + r
                    j0 = blk * 2 * s2 + r
                    j1 = j0 + s2
                    if s < 6:
                        lowN[s, i // 64, i % 64, j0 % 64] = ws[blk, r, half, 0]
                        lowN[s, i // 64, i % 64, j1 % 64] = ws[blk, r, half, 1]
                    else:
                        l = r % 64
                        highN[s - 6, l, i // 64, j0 // 64] = ws[blk, r, half, 0]
                        highN[s - 6, l, i // 64, j1 // 64] = ws[blk, r, half, 1]
    lowT = lowN.transpose(0, 1, 3, 2)
    highT = highN.transpose(0, 1, 3, 2)
    def arr(a):
        return np.ascontiguousarray(a.transpose(0, 2, 1, 3).reshape(6, 64, 64 * 64))
    return arr(lowN), arr(lowT), arr(highN), arr(highT)


def _blkdiag2(a):
    r, c = a.shape
    out = np.zeros((2 * r, 2 * c), np.float32)
    out[0:r, 0:c] = a
    out[r:2 * r, c:2 * c] = a
    return out


def _col(v, p=128):
    v = np.asarray(v, np.float32)
    return np.ascontiguousarray(v.reshape(-1, p).T)


def _slabs(ap2d, p=128):
    rows, C = ap2d.shape
    Sn = rows // p
    return bass.AP(ap2d.tensor, ap2d.offset, [[C, p], [p * C, Sn], [1, C]])


class Ctr:
    def __init__(self, es, nc, name):
        self.h = es.enter_context(nc.semaphore(name))
        self.n = 0

    def inc(self, k=1):
        self.n += k
        return self.h, k


def build_nc(dbg=False, stage=9, probe=False):
    nc = bass.Bass("TRN2", target_bir_lowering=False, debug=False)
    es = ExitStack()

    def dram(name, shape, dt=F32, kind="Internal"):
        return nc.dram_tensor(name, list(shape), dt, kind=kind)

    def pdram(name, shape, dt=F32):
        return nc.dram_tensor(name, list(shape), dt,
                              kind="ExternalOutput" if probe else "Internal")

    xT = dram("xT", (D, L), kind="ExternalInput").ap()
    xTh = dram("xTh", (D, TH), kind="ExternalInput").ap()
    w1col = dram("w1col", (128, NSLAB), kind="ExternalInput").ap()
    w2col = dram("w2col", (128, NSLAB), kind="ExternalInput").ap()
    kercol = dram("kercol", (128, NSLAB * KC), kind="ExternalInput").ap()
    dwbcol = dram("dwbcol", (128, NSLAB), kind="ExternalInput").ap()
    pwT = dram("pwT", (D, MYC), kind="ExternalInput").ap()
    pwbc = dram("pwbc", (128, MYC // 128), kind="ExternalInput").ap()
    xpT = dram("xpT", (128, 320), kind="ExternalInput").ap()
    dtpT = dram("dtpT", (128, 128), kind="ExternalInput").ap()
    dtb2 = dram("dtb2", (128, 1), kind="ExternalInput").ap()
    alog8 = dram("alog8", (HPC, DS), kind="ExternalInput").ap()
    opT = dram("opT", (D, D), kind="ExternalInput").ap()
    opbc = dram("opbc", (128, NSLAB), kind="ExternalInput").ap()
    gT = dram("gT", (D, DI), kind="ExternalInput").ap()
    uT = dram("uT", (D, DI), kind="ExternalInput").ap()
    dTw = dram("dTw", (DI, D), kind="ExternalInput").ap()
    EXPc = dram("EXPc", (HF, DP1), kind="ExternalInput").ap()
    WCS = dram("WCS", (HF, D), kind="ExternalInput").ap()
    ubcol = dram("ubcol", (128, HF // 128), kind="ExternalInput").ap()
    bcncol = dram("bcncol", (128, NSLAB), kind="ExternalInput").ap()
    dbcol = dram("dbcol", (128, NSLAB), kind="ExternalInput").ap()
    bstk = {}
    for nm in ["bfu_low", "bfu_lowT", "bfu_high", "bfu_highT",
               "bfd_low", "bfd_lowT", "bfd_high", "bfd_highT"]:
        bstk[nm] = dram(nm, (6, 64, 64 * 64), kind="ExternalInput").ap()

    outT = dram("outT", (D, TH), kind="ExternalOutput").ap()
    WexpP = pdram("WexpP", (NSLAB, 128, HF), BF16).ap()
    WexpB = pdram("WexpB", (1, HF)).ap()
    WconP = pdram("WconP", (HF, D), BF16).ap()
    bcSp = pdram("bcSp", (HPC, 96, L), BF16).ap()
    dtxSp = pdram("dtxSp", (HPC // 2, 128, L), BF16).ap()
    xhSp = pdram("xhSp", (HPC // 2, 128, L), BF16).ap()
    xnhSp = pdram("xnhSp", (NSLAB, 128, TH), BF16).ap()
    ffnSc = pdram("ffnSc", (NSLAB, 128, TH)).ap()
    x1Sc = pdram("x1Sc", (NSLAB, 128, TH)).ap()
    b1bnc = dram("b1bnc", (1, HF)).ap()
    cin = nc.dram_tensor("cin", [HPC // 2, 128, L], BF16)
    cout = nc.dram_tensor("cout", [2, HPC // 2, 128, L], BF16)
    if probe:
        hrowsD = dram("hrowsD", (128, TH), kind="ExternalOutput").ap()
        cinD = dram("cinD", (HPC // 2, 128, L), BF16, kind="ExternalOutput").ap()
        coutD = dram("coutD", (2, HPC // 2, 128, L), BF16, kind="ExternalOutput").ap()
        bcolD = dram("bcolD", (128, NSLAB + HF // 128 + HPC * NSH + 80),
                     kind="ExternalOutput").ap()

    dbg_out = {}
    if dbg:
        for nm, shp in [("dxn", (128, L)), ("duc", (128, L)), ("dxh", (128, L)),
                        ("ddtx", (128, L)), ("dbc", (96, L)), ("ddt", (128, L)),
                        ("dy", (128, L)), ("dx1", (128, TH)), ("dffn", (128, TH)),
                        ("dW1", (128, HF)), ("dWc", (128, D)), ("drows", (8, L))]:
            dbg_out[nm] = dram(nm, shp, kind="ExternalOutput").ap()

    S = lambda name: Ctr(es, nc, name)
    sW, sXs, sSq, sV, sT, sEv, sSp = S("sW"), S("sXs"), S("sSq"), S("sV"), S("sT"), S("sEv"), S("sSp")
    sBC, sAB, sVs, sCC, sBk, sOut = S("sBC"), S("sAB"), S("sVs"), S("sCC"), S("sBk"), S("sOut")
    sBb = [S("sBb0"), S("sBb1")]
    sCb = [S("sCb0"), S("sCb1")]

    sb = lambda name, shape, dt=F32: nc.sbuf_tensor(name, list(shape), dt)

    ones = es.enter_context(sb("ones", (128, 1)))
    hrows = es.enter_context(sb("hrows", (128, TH)))
    colbuf = es.enter_context(sb("colbuf", (128, 80)))
    b1col = es.enter_context(sb("b1col", (128, HF // 128)))
    b2col = es.enter_context(sb("b2col", (128, NSLAB)))
    ascl = es.enter_context(sb("ascl", (128, HPC * NSH)))
    R_M, R_RLN, R_GU, R_T0, R_T1, R_T2 = 0, 1, 2, 3, 4, 5
    rows_ref = {}
    ROWMAP = {0: (0, 0), 1: (64, 0), 2: (96, 0), 3: (32, 0),
              4: (32, 1), 5: (64, 1), 6: (0, 1), 7: (32, 2)}

    def ROW(r, c0=0, c1=L):
        p, c = ROWMAP[r]
        return rows_ref["t"][p:p + 1, c * L + c0:c * L + c1]
    C_W1, C_W2, C_KER, C_DWB, C_PWB, C_OPB, C_DTB = 0, 8, 16, 48, 56, 60, 68

    nc.vector.memset(ones[:], 1.0)

    nc.sync.dma_start(out=colbuf[:, C_W1:C_W1 + 8], in_=w1col).then_inc(*sW.inc(16))
    nc.sync.dma_start(out=colbuf[:, C_W2:C_W2 + 8], in_=w2col).then_inc(*sW.inc(16))
    nc.sync.dma_start(out=colbuf[:, C_KER:C_KER + 32], in_=kercol).then_inc(*sW.inc(16))
    nc.sync.dma_start(out=colbuf[:, C_DWB:C_DWB + 8], in_=dwbcol).then_inc(*sW.inc(16))
    nc.sync.dma_start(out=colbuf[:, C_PWB:C_PWB + 4], in_=pwbc).then_inc(*sW.inc(16))
    nc.sync.dma_start(out=colbuf[:, C_OPB:C_OPB + 8], in_=opbc).then_inc(*sW.inc(16))
    nc.sync.dma_start(out=colbuf[:, C_DTB:C_DTB + 1], in_=dtb2).then_inc(*sW.inc(16))
    sW_const = sW.n

    atmp = es.enter_context(sb("atmp", (HPC, DS)))
    nc.sync.dma_start(out=atmp[:], in_=alog8).then_inc(*sW.inc(16))
    nc.scalar.wait_ge(sW.h, sW.n)
    nc.scalar.activation(out=atmp[:], in_=atmp[:], func=AF.Exp).then_inc(*sSq.inc())
    nc.vector.wait_ge(sSq.h, sSq.n)
    nc.vector.tensor_scalar_mul(atmp[:], atmp[:], -1.0).then_inc(*sV.inc())
    aeo = es.enter_context(sb("aeo", (HPC, DS)))
    nc.vector.wait_ge(sV.h, sV.n)
    nc.vector.tensor_copy(out=aeo[:, 0:NSH], in_=atmp[:, 0:DS:2]).then_inc(*sV.inc())
    nc.vector.wait_ge(sV.h, sV.n)
    nc.vector.tensor_copy(out=aeo[:, NSH:DS], in_=atmp[:, 1:DS:2]).then_inc(*sV.inc())
    nc.gpsimd.wait_ge(sV.h, sV.n)
    for h in range(HPC):
        src0 = aeo[h:h + 1, 0:NSH].unsqueeze(1).broadcast_to((1, 64, NSH))
        src1 = aeo[h:h + 1, NSH:DS].unsqueeze(1).broadcast_to((1, 64, NSH))
        nc.gpsimd.dma_start(out=ascl[0:64, h * NSH:(h + 1) * NSH], in_=src0).then_inc(*sW.inc(16))
        nc.gpsimd.dma_start(out=ascl[64:128, h * NSH:(h + 1) * NSH], in_=src1).then_inc(*sW.inc(16))

    psA = es.enter_context(nc.psum_tensor("psA", [128, 512], F32))
    psB = es.enter_context(nc.psum_tensor("psB", [128, 512], F32))
    psC = es.enter_context(nc.psum_tensor("psC", [128, 512], F32))
    psD = es.enter_context(nc.psum_tensor("psD", [128, 512], F32))
    psS = es.enter_context(nc.psum_tensor("psS", [64, 512], F32))

    pid = nc.partition_id([Pool])
    parity = nc.gpsimd.compute_val(pid % 2)

    def make_compose(stg, Rbuf):
        def compose(dst, step_stack, steps, init_stack, init_idx):
                nc.sync.wait_ge(sT.h, sT.n)
                nc.sync.dma_start(out=Rbuf[:],
                                  in_=init_stack[init_idx]) \
                    .then_inc(*sXs.inc(16))
                cur, other = Rbuf, dst
                for s in steps:
                    nc.sync.wait_ge(sT.h, sT.n)
                    nc.sync.dma_start(out=stg[:],
                                      in_=step_stack[s]) \
                        .then_inc(*sXs.inc(16))
                    nc.tensor.wait_ge(sXs.h, sXs.n)
                    nc.tensor.wait_ge(sV.h, sV.n)
                    for grp in range(8):
                        nc.tensor.wait_ge(sEv.h, sEv.n)
                        for jb in range(8):
                            blk = grp * 8 + jb
                            mm = nc.tensor.matmul(psA[0:64, jb * 64:(jb + 1) * 64],
                                                  stg[:, blk * 64:(blk + 1) * 64],
                                                  cur[:, blk * 64:(blk + 1) * 64],
                                                  start=True, stop=True)
                            if jb == 7:
                                mm.then_inc(*sT.inc())
                        nc.scalar.wait_ge(sT.h, sT.n)
                        nc.scalar.copy(out=other[0:64, grp * 512:(grp + 1) * 512],
                                       in_=psA[0:64, 0:512]).then_inc(*sEv.inc())
                    cur, other = other, cur
                if cur is not dst:
                    nc.vector.wait_ge(sEv.h, sEv.n)
                    nc.vector.tensor_copy(out=dst[:], in_=cur[:]).then_inc(*sV.inc())
        return compose

    with sb("DupT", (64, 4096)) as DupT, sb("HupT", (64, 4096)) as HupT:
        with sb("stg", (64, 4096)) as stg, sb("Rbuf", (64, 4096)) as Rbuf:
            compose = make_compose(stg, Rbuf)
            compose(DupT, bstk["bfu_low"], [4, 3, 2, 1, 0], bstk["bfu_lowT"], 5)
            compose(HupT, bstk["bfu_high"], [4, 3, 2, 1, 0], bstk["bfu_highT"], 5)
            nc.sync.wait_ge(sEv.h, sEv.n)
            nc.sync.wait_ge(sV.h, sV.n)

        with sb("V2", (128, 32 * DP1)) as V2, sb("Eb", (64, DP1)) as Eb, \
             sb("Vsh", (64, DP1)) as Vsh, sb("wst", (128, 64), BF16) as wst, \
             sb("bst", (1, 64)) as bst:
            for ih in range(64):
                pe = ih % 2
                nc.sync.wait_ge(sT.h, sT.n)
                nc.sync.dma_start(out=Eb[:],
                                  in_=EXPc[ih * 64:(ih + 1) * 64, :]).then_inc(*sXs.inc(16))
                nc.tensor.wait_ge(sXs.h, sXs.n)
                for nch in range(3):
                    n0 = nch * 512
                    n1 = min(n0 + 512, DP1)
                    nc.tensor.wait_ge(sEv.h, sEv.n)
                    nc.tensor.matmul(psB[0:64, 0:n1 - n0],
                                     DupT[:, ih * 64:(ih + 1) * 64],
                                     Eb[:, n0:n1],
                                     start=True, stop=True).then_inc(*sT.inc())
                    nc.scalar.wait_ge(sT.h, sT.n)
                    nc.scalar.copy(out=V2[pe * 64:pe * 64 + 64,
                                          (ih // 2) * DP1 + n0:(ih // 2) * DP1 + n1],
                                   in_=psB[0:64, 0:n1 - n0]).then_inc(*sEv.inc())
            nc.gpsimd.wait_ge(sEv.h, sEv.n)
            for jl in range(64):
                srcA = V2[jl:jl + 1, :].rearrange("p (g c) -> p g c", g=32)
                srcB = V2[64 + jl:64 + jl + 1, :].rearrange("p (g c) -> p g c", g=32)
                dstA = bass.AP(Vsh[:].tensor, 0, [[2 * DP1, 32], [1, DP1]])
                dstB = bass.AP(Vsh[:].tensor, DP1, [[2 * DP1, 32], [1, DP1]])
                nc.gpsimd.wait_ge(sT.h, sT.n)
                nc.gpsimd.dma_start(out=dstA, in_=srcA).then_inc(*sSp.inc(16))
                nc.gpsimd.dma_start(out=dstB, in_=srcB).then_inc(*sSp.inc(16))
                nc.tensor.wait_ge(sSp.h, sSp.n)
                for dch in range(9):
                    d0 = dch * 128
                    dl = 128 if dch < 8 else 1
                    nc.tensor.wait_ge(sEv.h, sEv.n)
                    nc.tensor.matmul(psC[0:dl, 0:64],
                                     Vsh[:, d0:d0 + dl],
                                     HupT[:, jl * 64:(jl + 1) * 64],
                                     start=True, stop=True).then_inc(*sT.inc())
                    nc.scalar.wait_ge(sT.h, sT.n)
                    if dch < 8:
                        nc.scalar.copy(out=wst[:, 0:64], in_=psC[0:128, 0:64]).then_inc(*sEv.inc())
                        nc.gpsimd.wait_ge(sEv.h, sEv.n)
                        nc.gpsimd.dma_start(out=WexpP[dch, :, jl * 64:(jl + 1) * 64],
                                            in_=wst[:, 0:64]).then_inc(*sSp.inc(16))
                    else:
                        nc.scalar.copy(out=bst[:], in_=psC[0:1, 0:64]).then_inc(*sEv.inc())
                        nc.gpsimd.wait_ge(sEv.h, sEv.n)
                        nc.gpsimd.dma_start(out=WexpB[0:1, jl * 64:(jl + 1) * 64],
                                            in_=bst[:]).then_inc(*sSp.inc(16))
                    nc.scalar.wait_ge(sSp.h, sSp.n)

    with sb("Hdn", (64, 4096)) as Hdn, sb("Ddn", (64, 4096)) as Ddn:
        with sb("stg2", (64, 4096)) as stg, sb("Rbuf2", (64, 4096)) as Rbuf:
            compose = make_compose(stg, Rbuf)
            compose(Hdn, bstk["bfd_highT"], [1, 2, 3, 4, 5], bstk["bfd_high"], 0)
            compose(Ddn, bstk["bfd_lowT"], [1, 2, 3, 4, 5], bstk["bfd_low"], 0)
            nc.sync.wait_ge(sEv.h, sEv.n)
            nc.sync.wait_ge(sV.h, sV.n)
        with sb("U2", (128, 32 * D)) as U2, sb("Wcb", (64, 2 * D)) as Wcb, \
             sb("Ush", (64, 2 * D)) as Ush, sb("wst2", (64, D), BF16) as wst2:
            for jl in range(64):
                pe = jl % 2
                nc.sync.wait_ge(sT.h, sT.n)
                nc.sync.dma_start(out=Wcb[:, pe * D:(pe + 1) * D],
                                  in_=WCS[jl * 64:(jl + 1) * 64, :]).then_inc(*sXs.inc(16))
                nc.tensor.wait_ge(sXs.h, sXs.n)
                for nch in range(2):
                    n0 = nch * 512
                    nc.tensor.wait_ge(sEv.h, sEv.n)
                    nc.tensor.matmul(psB[0:64, 0:512],
                                     Hdn[:, jl * 64:(jl + 1) * 64],
                                     Wcb[:, pe * D + n0:pe * D + n0 + 512],
                                     start=True, stop=True).then_inc(*sT.inc())
                    nc.scalar.wait_ge(sT.h, sT.n)
                    nc.scalar.copy(out=U2[pe * 64:pe * 64 + 64,
                                          (jl // 2) * D + n0:(jl // 2) * D + n0 + 512],
                                   in_=psB[0:64, 0:512]).then_inc(*sEv.inc())
            nc.gpsimd.wait_ge(sEv.h, sEv.n)
            for ih in range(64):
                pe = ih % 2
                srcA = U2[ih:ih + 1, :].rearrange("p (g c) -> p g c", g=32)
                srcB = U2[64 + ih:64 + ih + 1, :].rearrange("p (g c) -> p g c", g=32)
                dstA = bass.AP(Ush[:].tensor, pe * D, [[2 * (2 * D), 32], [1, D]])
                dstB = bass.AP(Ush[:].tensor, (2 * D) + pe * D, [[2 * (2 * D), 32], [1, D]])
                nc.gpsimd.wait_ge(sT.h, sT.n)
                nc.gpsimd.dma_start(out=dstA, in_=srcA).then_inc(*sSp.inc(16))
                nc.gpsimd.dma_start(out=dstB, in_=srcB).then_inc(*sSp.inc(16))
                nc.tensor.wait_ge(sSp.h, sSp.n)
                for nch in range(2):
                    n0 = nch * 512
                    nc.tensor.wait_ge(sEv.h, sEv.n)
                    nc.tensor.matmul(psB[0:64, 0:512],
                                     Ddn[:, ih * 64:(ih + 1) * 64],
                                     Ush[:, pe * D + n0:pe * D + n0 + 512],
                                     start=True, stop=True).then_inc(*sT.inc())
                    nc.scalar.wait_ge(sT.h, sT.n)
                    nc.scalar.copy(out=wst2[0:64, n0:n0 + 512],
                                   in_=psB[0:64, 0:512]).then_inc(*sEv.inc())
                nc.gpsimd.wait_ge(sEv.h, sEv.n)
                dstw = bass.AP(WconP.tensor, ih * D, [[64 * D, 64], [1, D]])
                nc.gpsimd.dma_start(out=dstw, in_=wst2[0:64, :]).then_inc(*sSp.inc(16))
                nc.scalar.wait_ge(sSp.h, sSp.n)

    nc.sync.wait_ge(sSp.h, sSp.n)
    nc.sync.dma_start(out=b1bnc, in_=WexpB).then_inc(*sW.inc(16))
    nc.sync.wait_ge(sW.h, sW.n)
    nc.sync.dma_start(out=b1col[:], in_=bass.AP(b1bnc.tensor, 0, [[1, 128], [128, HF // 128]]),
                      allow_slow_non_contiguous=True).then_inc(*sW.inc(16))
    nc.sync.dma_start(out=b2col[:], in_=bcncol).then_inc(*sW.inc(16))
    with sb("tmpc", (128, NSLAB + HF // 128)) as tmpc:
        nc.sync.dma_start(out=tmpc[:, 0:NSLAB], in_=dbcol).then_inc(*sW.inc(16))
        nc.sync.dma_start(out=tmpc[:, NSLAB:], in_=ubcol).then_inc(*sW.inc(16))
        nc.vector.wait_ge(sW.h, sW.n)
        nc.vector.tensor_add(b2col[:], b2col[:], tmpc[:, 0:NSLAB])
        nc.vector.tensor_add(b1col[:], b1col[:], tmpc[:, NSLAB:]).then_inc(*sV.inc())

    if dbg:
        nc.gpsimd.wait_ge(sSp.h, sSp.n)
        nc.gpsimd.dma_start(out=dbg_out["dW1"], in_=WexpP[0]).then_inc(*sOut.inc(16))
        nc.gpsimd.dma_start(out=dbg_out["dWc"], in_=WconP[0:128, :]).then_inc(*sOut.inc(16))

    def finish():
        nc.gpsimd.wait_ge(sOut.h, sOut.n)
        es.close()
        return nc

    if stage < 1:
        return finish()

    esF = ExitStack()
    rows_ref["t"] = esF.enter_context(sb("rows", (128, 3 * L)))
    with sb("xsl", (128, 2 * L)) as xsl, sb("sqt", (128, L)) as sqt:
        for k in range(NSLAB):
            pe = k % 2
            nc.sync.wait_ge(sSq.h, sSq.n)
            nc.sync.wait_ge(sT.h, sT.n)
            nc.sync.dma_start(out=xsl[:, pe * L:(pe + 1) * L],
                              in_=xT[k * 128:(k + 1) * 128, :]).then_inc(*sXs.inc(16))
            nc.scalar.wait_ge(sXs.h, sXs.n)
            nc.scalar.activation(out=sqt[:], in_=xsl[:, pe * L:(pe + 1) * L],
                                 func=AF.Square).then_inc(*sSq.inc())
            nc.tensor.wait_ge(sSq.h, sSq.n)
            for nch in range(8):
                nc.tensor.wait_ge(sEv.h, sEv.n)
                nc.tensor.matmul(psS[0:1, :], ones[:],
                                 xsl[:, pe * L + nch * 512:pe * L + (nch + 1) * 512],
                                 start=True, stop=True)
                nc.tensor.matmul(psA[0:1, :], ones[:],
                                 sqt[:, nch * 512:(nch + 1) * 512],
                                 start=True, stop=True).then_inc(*sT.inc())
                nc.scalar.wait_ge(sT.h, sT.n)
                nc.scalar.copy(out=ROW(6, nch * 512, (nch + 1) * 512), in_=psS[0:1, :])
                nc.scalar.copy(out=ROW(7, nch * 512, (nch + 1) * 512),
                               in_=psA[0:1, :]).then_inc(*sEv.inc())
            nc.vector.wait_ge(sEv.h, sEv.n)
            if k == 0:
                nc.vector.tensor_scalar_mul(ROW(R_M), ROW(6), 1.0 / D)
                nc.vector.tensor_scalar_mul(ROW(R_T0), ROW(7), 1.0 / D)
            else:
                nc.vector.scalar_tensor_tensor(out=ROW(R_M), in0=ROW(6),
                                               scalar=1.0 / D, in1=ROW(R_M),
                                               op0=ALU.mult, op1=ALU.add)
                nc.vector.scalar_tensor_tensor(out=ROW(R_T0), in0=ROW(7),
                                               scalar=1.0 / D, in1=ROW(R_T0),
                                               op0=ALU.mult, op1=ALU.add)
            nc.vector.engine_nop().then_inc(*sV.inc())
            nc.sync.wait_ge(sV.h, sV.n)
        nc.vector.wait_ge(sV.h, sV.n)
        for opf in [
            lambda: nc.vector.tensor_mul(ROW(R_T1), ROW(R_M), ROW(R_M)),
            lambda: nc.vector.tensor_sub(ROW(R_T1), ROW(R_T0), ROW(R_T1)),
            lambda: nc.vector.tensor_scalar_add(ROW(R_T2), ROW(R_T1), 1e-5),
            lambda: nc.vector.reciprocal(ROW(R_T0), ROW(R_T2)),
        ]:
            opf().then_inc(*sV.inc())
            nc.vector.wait_ge(sV.h, sV.n)
        nc.scalar.wait_ge(sV.h, sV.n)
        nc.scalar.activation(out=ROW(R_RLN), in_=ROW(R_T0),
                             func=AF.Sqrt).then_inc(*sSq.inc())
        nc.vector.wait_ge(sSq.h, sSq.n)
        for opf in [
            lambda: nc.vector.tensor_mul(ROW(R_T2), ROW(R_T1), ROW(R_T0)),
            lambda: nc.vector.tensor_scalar_add(ROW(R_T2), ROW(R_T2), 1e-6),
            lambda: nc.vector.reciprocal(ROW(R_T2), ROW(R_T2)),
        ]:
            opf().then_inc(*sV.inc())
            nc.vector.wait_ge(sV.h, sV.n)
        nc.vector.engine_nop().then_inc(*sV.inc())
        nc.scalar.wait_ge(sV.h, sV.n)
        nc.scalar.activation(out=ROW(R_T2), in_=ROW(R_T2),
                             func=AF.Sqrt).then_inc(*sSq.inc())
        nc.vector.wait_ge(sSq.h, sSq.n)
        nc.vector.tensor_mul(ROW(R_GU), ROW(R_RLN),
                             ROW(R_T2)).then_inc(*sV.inc())
    if dbg:
        nc.gpsimd.wait_ge(sV.h, sV.n)
        nc.gpsimd.dma_start(out=dbg_out["drows"][0:1, :], in_=ROW(R_M)).then_inc(*sOut.inc(16))
        nc.gpsimd.dma_start(out=dbg_out["drows"][1:2, :], in_=ROW(R_RLN)).then_inc(*sOut.inc(16))
        nc.gpsimd.dma_start(out=dbg_out["drows"][2:3, :], in_=ROW(R_GU)).then_inc(*sOut.inc(16))

    if stage < 2:
        esF.close()
        return finish()
    nc.gpsimd.wait_ge(sV.h, sV.n)
    for case in nc.gpsimd.Switch(parity, 2):
        o = case * TH
        nc.gpsimd.dma_start(out=hrows[0:1, :], in_=ROW(R_M, o, o + TH)).then_inc(sBk.h, 16)
        nc.gpsimd.dma_start(out=hrows[32:33, :], in_=ROW(R_RLN, o, o + TH)).then_inc(sBk.h, 16)
    sBk.n += 32

    if stage < 3:
        esF.close()
        return finish()

    with sb("ucsl", (128, NSLAB * L), BF16) as ucsl:
      with sb("mB", (128, L), BF16) as mB, sb("rlnB", (128, L), BF16) as rlnB, sb("guB", (128, L), BF16) as guB, \
         sb("xsl2", (128, L)) as xsl2, sb("t1t", (128, L)) as t1t, \
         sb("xnsl", (128, L), BF16) as xnsl, sb("upad", (128, L + KC - 1)) as upad:
        nc.sync.wait_ge(sV.h, sV.n)
        nc.gpsimd.wait_ge(sV.h, sV.n)
        nc.gpsimd.dma_start(out=mB[:],
                            in_=ROW(R_M).unsqueeze(1).broadcast_to((1, 128, L))) \
            .then_inc(*sW.inc(16))
        for rsrc, rdst in [(R_RLN, rlnB), (R_GU, guB)]:
            nc.gpsimd.dma_start(out=rdst[:],
                                in_=ROW(rsrc).unsqueeze(1).broadcast_to((1, 128, L))) \
                .then_inc(*sW.inc(16))
        nc.vector.wait_ge(sW.h, sW.n)
        nc.vector.memset(upad[:, 0:KC - 1], 0.0)
        for k in range(NSLAB):
            pe = k % 2
            nc.sync.wait_ge(sV.h, sV.n)
            nc.sync.dma_start(out=xsl2[:],
                              in_=xT[k * 128:(k + 1) * 128, :]).then_inc(*sXs.inc(16))
            nc.vector.wait_ge(sXs.h, sXs.n)
            nc.vector.wait_ge(sSp.h, sSp.n)
            xs = xsl2[:]
            nc.vector.tensor_sub(t1t[:], xs, mB[:])
            nc.vector.tensor_mul(xnsl[:], t1t[:], rlnB[:])
            nc.vector.scalar_tensor_tensor(out=upad[:, KC - 1:], in0=t1t[:],
                                           scalar=colbuf[:, C_W1 + k:C_W1 + k + 1], in1=guB[:],
                                           op0=ALU.mult, op1=ALU.mult)
            nc.vector.tensor_scalar_mul(t1t[:], upad[:, 0:L],
                                        colbuf[:, C_KER + k * 4:C_KER + k * 4 + 1])
            for kk in range(1, KC):
                dst = ucsl[:, k * L:(k + 1) * L] if kk == KC - 1 else t1t[:]
                nc.vector.scalar_tensor_tensor(out=dst, in0=upad[:, kk:kk + L],
                                               scalar=colbuf[:, C_KER + k * 4 + kk:C_KER + k * 4 + kk + 1],
                                               in1=t1t[:], op0=ALU.mult, op1=ALU.add)
            nc.vector.engine_nop().then_inc(*sV.inc())
            for case in nc.gpsimd.Switch(parity, 2):
                o = case * TH
                nc.gpsimd.wait_ge(sV.h, sV.n)
                nc.gpsimd.dma_start(out=xnhSp[k], in_=xnsl[:, o:o + TH]).then_inc(sSp.h, 16)
            sSp.n += 16
            if dbg and k == 0:
                nc.gpsimd.wait_ge(sV.h, sV.n)
                nc.gpsimd.dma_start(out=dbg_out["dxn"], in_=xnsl[:]).then_inc(*sOut.inc(16))
                nc.gpsimd.dma_start(out=dbg_out["duc"], in_=ucsl[:, 0:L]).then_inc(*sOut.inc(16))
            if dbg:
                nc.vector.wait_ge(sOut.h, sOut.n)

      if True:
        nc.gpsimd.wait_ge(sSp.h, sSp.n)
        with sb("pwTs", (128, NSLAB * MYC), BF16) as pwTs, sb("pwbe", (128, 4)) as pwbe, \
             sb("dwbf", (128, NSLAB), BF16) as dwbf, \
             sb("xhs", (128, (HPC // 2) * L), BF16) as xhs:
            nc.gpsimd.wait_ge(sW.h, sW_const)
            nc.gpsimd.dma_start(out=pwTs[:], in_=_slabs(pwT)) \
                .then_inc(*sSp.inc(16))
            nc.gpsimd.dma_start(out=dwbf[:], in_=colbuf[:, C_DWB:C_DWB + 8]).then_inc(*sSp.inc(16))
            nc.tensor.wait_ge(sSp.h, sSp.n)
            for m in range(4):
                for k in range(NSLAB):
                    mm = nc.tensor.matmul(psA[0:128, m:m + 1],
                                          pwTs[:, k * MYC + m * 128:k * MYC + (m + 1) * 128],
                                          dwbf[:, k:k + 1],
                                          start=(k == 0), stop=(k == NSLAB - 1))
                    if k == NSLAB - 1:
                        mm.then_inc(*sT.inc())
            nc.scalar.wait_ge(sT.h, sT.n)
            for m in range(4):
                nc.scalar.activation(out=pwbe[:, m:m + 1], in_=psA[0:128, m:m + 1],
                                     func=AF.Identity, bias=colbuf[:, C_PWB + m:C_PWB + m + 1])
            nc.scalar.activation(out=pwbe[:, 0:4], in_=pwbe[:, 0:4], func=AF.Copy) \
                .then_inc(*sSq.inc())
            nc.tensor.wait_ge(sV.h, sV.n)
            nc.tensor.wait_ge(sSq.h, sSq.n)
            for m in range(4):
                psm = [psA, psB, psC, psD][m]
                for nch in range(8):
                    nc.tensor.wait_ge(sEv.h, sEv.n)
                    for k in range(NSLAB):
                        mm = nc.tensor.matmul(psm[:, 0:512],
                                              pwTs[:, k * MYC + m * 128:k * MYC + (m + 1) * 128],
                                              ucsl[:, k * L + nch * 512:k * L + (nch + 1) * 512],
                                              start=(k == 0), stop=(k == NSLAB - 1))
                        if k == NSLAB - 1:
                            mm.then_inc(*sT.inc())
                    nc.scalar.wait_ge(sT.h, sT.n)
                    nc.scalar.activation(out=xhs[:, m * L + nch * 512:m * L + (nch + 1) * 512],
                                         in_=psm[:, 0:512], func=AF.Identity,
                                         bias=pwbe[:, m:m + 1]).then_inc(*sEv.inc())
            nc.gpsimd.wait_ge(sEv.h, sEv.n)
            for j in range(HPC // 2):
                nc.gpsimd.dma_start(out=xhSp[j], in_=xhs[:, j * L:(j + 1) * L]).then_inc(*sSp.inc(16))
            if dbg:
                nc.gpsimd.dma_start(out=dbg_out["dxh"], in_=xhs[:, 0:L]).then_inc(*sOut.inc(16))

            with sb("xpTs", (128, 320), BF16) as xpTs, sb("bct", (96, L), BF16) as bct, \
                 sb("dtxt", (128, L), BF16) as dtxt:
                nc.gpsimd.dma_start(out=xpTs[:], in_=xpT).then_inc(*sSp.inc(16))
                nc.tensor.wait_ge(sSp.h, sSp.n)
                for h in range(HPC):
                    sl, po = h // 2, (h % 2) * 64
                    wo = (h % 2) * 160
                    for nch in range(8):
                        nc.tensor.wait_ge(sEv.h, sEv.n)
                        nc.tensor.matmul(psC[0:64, 0:512], xpTs[:, wo:wo + 64],
                                         xhs[:, sl * L + nch * 512:sl * L + (nch + 1) * 512],
                                         start=True, stop=True)
                        nc.tensor.matmul(psD[0:96, 0:512], xpTs[:, wo + 64:wo + 160],
                                         xhs[:, sl * L + nch * 512:sl * L + (nch + 1) * 512],
                                         start=True, stop=True).then_inc(*sT.inc())
                        nc.scalar.wait_ge(sT.h, sT.n)
                        nc.scalar.copy(out=dtxt[po:po + 64, nch * 512:(nch + 1) * 512],
                                       in_=psC[0:64, 0:512])
                        nc.scalar.copy(out=bct[:, nch * 512:(nch + 1) * 512],
                                       in_=psD[0:96, 0:512]).then_inc(*sEv.inc())
                    nc.gpsimd.wait_ge(sEv.h, sEv.n)
                    nc.gpsimd.dma_start(out=bcSp[h], in_=bct[:]).then_inc(*sSp.inc(16))
                    if h % 2 == 1:
                        nc.gpsimd.dma_start(out=dtxSp[h // 2], in_=dtxt[:]).then_inc(*sSp.inc(16))
                    if dbg and h == 0:
                        nc.gpsimd.dma_start(out=dbg_out["dbc"], in_=bct[:]).then_inc(*sOut.inc(16))
                    if dbg and h == 1:
                        nc.gpsimd.dma_start(out=dbg_out["ddtx"], in_=dtxt[:]).then_inc(*sOut.inc(16))
                    if dbg:
                        nc.scalar.wait_ge(sOut.h, sOut.n)
                    nc.scalar.wait_ge(sSp.h, sSp.n)

    esF.close()
    if stage < 5:
        return finish()

    with sb("ystg", (64, L), BF16) as ystg, sb("yfold", (64, L)) as yfold, \
         sb("dtpTs", (128, 128), BF16) as dtpTs, \
         sb("dtx2", (128, L), BF16) as dtx2, sb("dtp", (128, L)) as dtp, \
         sb("zsp", (128, 1024)) as zsp, sb("dt2", (128, L)) as dt2, \
         sb("xh2", (128, L), BF16) as xh2, sb("w2t", (128, L)) as w2t, \
         sb("abar", (128, L)) as abar, sb("inp", (128, L)) as inpt, \
         sb("hst", (128, L)) as hst, sb("y2t", (128, L)) as y2t, \
         sb("bcl", (96, L), BF16) as bcl, \
         sb("Bb", (128, 2 * L), BF16) as Bb, sb("Cb", (128, 2 * L), BF16) as Cb:
        nc.gpsimd.dma_start(out=dtpTs[:], in_=dtpT).then_inc(*sW.inc(16))
        b_free = {0: 0, 1: 0}
        c_free = {0: 0, 1: 0}
        a_free = 0
        sT_pair = sT.n
        sAB_head = sAB.n
        sBC_head = sBC.n
        w2_done = 0
        for h in range(HPC):
            pr, po = h // 2, (h % 2) * 64
            if h % 2 == 0:
                nc.sync.wait_ge(sT.h, sT_pair)
                if h == 0:
                    nc.sync.wait_ge(sW.h, sW.n)
                    nc.sync.wait_ge(sSp.h, sSp.n)
                nc.sync.dma_start(out=dtx2[:], in_=dtxSp[pr]).then_inc(*sXs.inc(16))
                nc.tensor.wait_ge(sXs.h, sXs.n)
                for nch in range(8):
                    nc.tensor.wait_ge(sSq.h, sSq.n)
                    nc.tensor.matmul(psA[0:64, 0:512], dtpTs[:, 0:64],
                                     dtx2[:, nch * 512:(nch + 1) * 512],
                                     start=True, stop=True)
                    nc.tensor.matmul(psB[0:64, 0:512], dtpTs[:, 64:128],
                                     dtx2[:, nch * 512:(nch + 1) * 512],
                                     start=True, stop=True).then_inc(*sT.inc())
                    nc.scalar.wait_ge(sT.h, sT.n)
                    nc.scalar.wait_ge(sV.h, sV.n)
                    zpe = (nch % 2) * 512
                    zs = zsp[:, zpe:zpe + 512]
                    nc.scalar.activation(out=zsp[0:64, zpe:zpe + 512],
                                         in_=psA[0:64, 0:512], func=AF.Identity,
                                         bias=colbuf[0:64, C_DTB:C_DTB + 1])
                    nc.scalar.activation(out=zsp[64:128, zpe:zpe + 512],
                                         in_=psB[0:64, 0:512], func=AF.Identity,
                                         bias=colbuf[64:128, C_DTB:C_DTB + 1])
                    ab = dtp[:, nch * 512:(nch + 1) * 512]
                    nc.scalar.activation(out=ab, in_=zs, func=AF.Abs)
                    nc.scalar.activation(out=ab, in_=ab, func=AF.Exp, scale=-1.0)
                    nc.scalar.activation(out=ab, in_=ab, func=AF.Ln, bias=1.0).then_inc(*sSq.inc())
                    nc.vector.wait_ge(sSq.h, sSq.n)
                    nc.vector.scalar_tensor_tensor(out=ab, in0=zs, scalar=0.0, in1=ab,
                                                   op0=ALU.max, op1=ALU.add).then_inc(*sV.inc())
                sT_pair = sT.n
            nc.sync.wait_ge(sV.h, sV.n)
            nc.sync.wait_ge(sAB.h, sAB_head)
            nc.sync.wait_ge(sVs.h, w2_done)
            for sx in (sBb[0], sBb[1], sCb[0], sCb[1]):
                nc.sync.wait_ge(sx.h, sx.n)
            nc.sync.dma_start(out=dt2[0:64, :], in_=dtp[po:po + 64, :]).then_inc(*sXs.inc(16))
            nc.sync.dma_start(out=dt2[64:128, :], in_=dtp[po:po + 64, :]).then_inc(*sXs.inc(16))
            nc.sync.dma_start(out=xh2[0:64, :], in_=xhSp[pr][po:po + 64, :]).then_inc(*sXs.inc(16))
            nc.sync.dma_start(out=xh2[64:128, :], in_=xhSp[pr][po:po + 64, :]).then_inc(*sXs.inc(16))
            nc.sync.dma_start(out=bcl[:], in_=bcSp[h]).then_inc(*sXs.inc(16))
            sXs_tiles = sXs.n
            nc.vector.wait_ge(sXs.h, sXs_tiles)
            nc.vector.tensor_mul(w2t[:], dt2[:], xh2[:]).then_inc(*sVs.inc())
            w2_done = sVs.n
            for shi in range(NSH):
                pe = shi % 2
                abv = abar[:]
                nc.gpsimd.wait_ge(sXs.h, sXs_tiles)
                nc.gpsimd.wait_ge(sVs.h, b_free[pe])
                nc.gpsimd.dma_start(out=Bb[:, pe * L:(pe + 1) * L],
                                    in_=bcl[2 * shi:2 * shi + 2, :].unsqueeze(1)
                                    .broadcast_to((2, 64, L))).then_inc(*sBb[pe].inc(16))
                B_rdy = sBb[pe].n
                nc.gpsimd.wait_ge(sVs.h, c_free[pe])
                nc.gpsimd.dma_start(out=Cb[:, pe * L:(pe + 1) * L],
                                    in_=bcl[DS + 2 * shi:DS + 2 * shi + 2, :].unsqueeze(1)
                                    .broadcast_to((2, 64, L))).then_inc(*sCb[pe].inc(16))
                C_rdy = sCb[pe].n
                nc.scalar.wait_ge(sXs.h, sXs_tiles)
                nc.scalar.wait_ge(sVs.h, a_free)
                nc.scalar.activation(out=abv, in_=dt2[:], func=AF.Exp,
                                     scale=ascl[:, h * NSH + shi:h * NSH + shi + 1]) \
                    .then_inc(*sAB.inc())
                A_rdy = sAB.n
                nc.vector.wait_ge(sBb[pe].h, B_rdy)
                nc.vector.tensor_mul(inpt[:], w2t[:], Bb[:, pe * L:(pe + 1) * L]) \
                    .then_inc(*sVs.inc())
                b_free[pe] = sVs.n
                nc.vector.wait_ge(sAB.h, A_rdy)
                nc.vector.tensor_tensor_scan(out=hst[:], data0=abv, data1=inpt[:],
                                             initial=0.0, op0=ALU.mult, op1=ALU.add) \
                    .then_inc(*sVs.inc())
                a_free = sVs.n
                nc.vector.wait_ge(sVs.h, a_free)
                nc.vector.wait_ge(sCb[pe].h, C_rdy)
                nc.vector.tensor_mul(hst[:], hst[:], Cb[:, pe * L:(pe + 1) * L]) \
                    .then_inc(*sVs.inc())
                c_free[pe] = sVs.n
                nc.vector.wait_ge(sVs.h, c_free[pe])
                if shi == 0:
                    nc.vector.wait_ge(sSp.h, sSp.n)
                    nc.vector.tensor_copy(out=y2t[:], in_=hst[:]).then_inc(*sVs.inc())
                else:
                    nc.vector.tensor_add(y2t[:], y2t[:], hst[:]).then_inc(*sVs.inc())
            nc.gpsimd.wait_ge(sVs.h, sVs.n)
            nc.gpsimd.dma_start(out=yfold[:], in_=y2t[64:128, :]).then_inc(*sSp.inc(16))
            nc.vector.wait_ge(sSp.h, sSp.n)
            nc.vector.tensor_add(ystg[:], y2t[0:64, :], yfold[:]).then_inc(*sVs.inc())
            nc.gpsimd.wait_ge(sVs.h, sVs.n)
            nc.gpsimd.dma_start(out=cin.ap()[pr][po:po + 64, :], in_=ystg[:]).then_inc(*sSp.inc(16))
            sAB_head = sAB.n
            sBC_head = sBC.n
        nc.gpsimd.wait_ge(sSp.h, sSp.n)

    if stage < 6:
        return finish()

    nc.gpsimd.wait_ge(sSp.h, sSp.n)
    nc.gpsimd.collective_compute(
        "AllGather", ALU.bypass,
        replica_groups=[[0, 1], [2, 3], [4, 5], [6, 7]],
        ins=[cin.ap().opt()], outs=[cout.ap().opt()],
    ).then_inc(*sCC.inc())

    if True:
        with sb("W1s", (128, NSLAB * HF), BF16) as W1s, sb("Wcs", (128, 32 * D), BF16) as Wcs, \
             sb("xnh", (128, NSLAB * 512), BF16) as xnh, \
             sb("h1t", (128, 32 * 512), BF16) as h1t, sb("ffo", (128, 512)) as ffo:
            nc.sync.wait_ge(sVs.h, sVs.n)
            nc.sync.wait_ge(sAB.h, sAB.n)
            nc.sync.wait_ge(sSp.h, sSp.n)
            for k in range(NSLAB):
                nc.sync.dma_start(out=W1s[:, k * HF:(k + 1) * HF], in_=WexpP[k]).then_inc(*sXs.inc(16))
            nc.sync.dma_start(out=Wcs[:], in_=_slabs(WconP)) \
                .then_inc(*sXs.inc(16))
            for st_ in range(TH // 512):
                for k in range(NSLAB):
                    nc.sync.wait_ge(sT.h, sT.n)
                    nc.sync.dma_start(out=xnh[:, k * 512:(k + 1) * 512],
                                      in_=xnhSp[k][:, st_ * 512:(st_ + 1) * 512]).then_inc(*sXs.inc(16))
                nc.tensor.wait_ge(sXs.h, sXs.n)
                nc.tensor.wait_ge(sV.h, sV.n)
                for m in range(32):
                    psm = [psA, psB][m % 2]
                    nc.tensor.wait_ge(sEv.h, sEv.n - 1 if m >= 2 else 0)
                    for k in range(NSLAB):
                        mm = nc.tensor.matmul(psm[:, 0:512],
                                              W1s[:, k * HF + m * 128:k * HF + (m + 1) * 128],
                                              xnh[:, k * 512:(k + 1) * 512],
                                              start=(k == 0), stop=(k == NSLAB - 1))
                        if k == NSLAB - 1:
                            mm.then_inc(*sT.inc())
                    nc.scalar.wait_ge(sT.h, sT.n)
                    nc.scalar.activation(out=h1t[:, m * 512:(m + 1) * 512], in_=psm[:, 0:512],
                                         func=AF.Gelu, bias=b1col[:, m:m + 1]).then_inc(*sEv.inc())
                for m in range(NSLAB):
                    nc.tensor.wait_ge(sEv.h, sEv.n)
                    nc.tensor.wait_ge(sSp.h, sSp.n)
                    for k in range(32):
                        mm = nc.tensor.matmul(psC[:, 0:512],
                                              Wcs[:, k * D + m * 128:k * D + (m + 1) * 128],
                                              h1t[:, k * 512:(k + 1) * 512],
                                              start=(k == 0), stop=(k == 31))
                        if k == 31:
                            mm.then_inc(*sT.inc())
                    nc.scalar.wait_ge(sT.h, sT.n)
                    nc.scalar.activation(out=ffo[:], in_=psC[:, 0:512], func=AF.Identity,
                                         bias=b2col[:, m:m + 1]).then_inc(*sEv.inc())
                    nc.gpsimd.wait_ge(sEv.h, sEv.n)
                    nc.gpsimd.dma_start(out=ffnSc[m][:, st_ * 512:(st_ + 1) * 512],
                                        in_=ffo[:]).then_inc(*sSp.inc(16))
            if dbg:
                nc.gpsimd.wait_ge(sSp.h, sSp.n)
                nc.gpsimd.dma_start(out=dbg_out["dffn"], in_=ffnSc[0]).then_inc(*sOut.inc(16))

        with sb("mhB", (128, TH)) as mhB, sb("rlhB", (128, TH)) as rlhB, \
             sb("sq2", (128, 512)) as sq2, sb("x1w", (128, 1024)) as x1w:
          with sb("yall", (128, NSLAB * TH), BF16) as yall, \
               sb("opTs", (128, NSLAB * D), BF16) as opTs, sb("xht", (128, TH)) as xht:
            nc.gpsimd.wait_ge(sCC.h, sCC.n)
            for case in nc.gpsimd.Switch(parity, 2):
                o = case * TH
                for slot in range(2):
                    for j in range(HPC // 2):
                        nc.gpsimd.dma_start(
                            out=yall[:, (slot * 4 + j) * TH:(slot * 4 + j + 1) * TH],
                            in_=cout.ap()[slot, j][:, o:o + TH]).then_inc(sBk.h, 16)
            sBk.n += 128
            nc.gpsimd.dma_start(out=opTs[:], in_=_slabs(opT)).then_inc(*sSp.inc(16))
            nc.gpsimd.wait_ge(sBk.h, sBk.n)
            nc.gpsimd.dma_start(out=mhB[:], in_=hrows[0:1, :].unsqueeze(1).broadcast_to((1, 128, TH))) \
                .then_inc(*sW.inc(16))
            nc.gpsimd.dma_start(out=rlhB[:], in_=hrows[32:33, :].unsqueeze(1).broadcast_to((1, 128, TH))) \
                .then_inc(*sW.inc(16))
            nc.tensor.wait_ge(sSp.h, sSp.n)
            nc.tensor.wait_ge(sBk.h, sBk.n)
            nc.vector.wait_ge(sW.h, sW.n)
            nc.sync.wait_ge(sSp.h, sSp.n)
            for m in range(NSLAB):
                nc.sync.wait_ge(sV.h, sV.n)
                nc.sync.dma_start(out=xht[:],
                                  in_=xTh[m * 128:(m + 1) * 128, :]).then_inc(*sXs.inc(16))
                for nch in range(TH // 512):
                    psm = [psA, psB][nch % 2]
                    pe = nch % 2
                    nc.tensor.wait_ge(sEv.h, sEv.n)
                    nc.tensor.wait_ge(sSp.h, sSp.n)
                    for k in range(NSLAB):
                        mm = nc.tensor.matmul(psm[:, 0:512],
                                              opTs[:, k * D + m * 128:k * D + (m + 1) * 128],
                                              yall[:, k * TH + nch * 512:k * TH + (nch + 1) * 512],
                                              start=(k == 0), stop=(k == NSLAB - 1))
                        if k == NSLAB - 1:
                            mm.then_inc(*sT.inc())
                    nc.scalar.wait_ge(sT.h, sT.n)
                    x1v = x1w[:, pe * 512:(pe + 1) * 512]
                    nc.scalar.activation(out=x1v, in_=psm[:, 0:512], func=AF.Identity,
                                         bias=colbuf[:, C_OPB + m:C_OPB + m + 1]).then_inc(*sEv.inc())
                    nc.vector.wait_ge(sEv.h, sEv.n)
                    nc.vector.wait_ge(sXs.h, sXs.n)
                    xv = xht[:, nch * 512:(nch + 1) * 512]
                    nc.vector.tensor_sub(sq2[:], xv, mhB[:, nch * 512:(nch + 1) * 512])
                    nc.vector.tensor_mul(sq2[:], sq2[:], rlhB[:, nch * 512:(nch + 1) * 512])
                    nc.vector.tensor_add(x1v, x1v, sq2[:]).then_inc(*sV.inc())
                    nc.gpsimd.wait_ge(sV.h, sV.n)
                    nc.gpsimd.dma_start(out=x1Sc[m][:, nch * 512:(nch + 1) * 512],
                                        in_=x1v).then_inc(*sSp.inc(16))
            if dbg:
                nc.gpsimd.wait_ge(sSp.h, sSp.n)
                nc.gpsimd.dma_start(out=dbg_out["dx1"], in_=x1Sc[0]).then_inc(*sOut.inc(16))
          nc.scalar.wait_ge(sV.h, sV.n)
          nc.sync.wait_ge(sSp.h, sSp.n)
          if True:
            for nch in range(TH // 512):
                for k in range(NSLAB):
                    pe2 = k % 2
                    nc.sync.wait_ge(sSq.h, sSq.n)
                    nc.sync.dma_start(out=x1w[:, pe2 * 512:(pe2 + 1) * 512],
                                      in_=x1Sc[k][:, nch * 512:(nch + 1) * 512]).then_inc(*sXs.inc(16))
                    nc.scalar.wait_ge(sXs.h, sXs.n)
                    nc.scalar.wait_ge(sT.h, sT.n) if k == 0 else None
                    nc.scalar.activation(out=sq2[:], in_=x1w[:, pe2 * 512:(pe2 + 1) * 512],
                                         func=AF.Square).then_inc(*sSq.inc())
                    nc.tensor.wait_ge(sSq.h, sSq.n)
                    mm = nc.tensor.matmul(psS[0:1, 0:512], ones[:], sq2[:],
                                          start=(k == 0), stop=(k == NSLAB - 1))
                    if k == NSLAB - 1:
                        mm.then_inc(*sT.inc())
                nc.scalar.wait_ge(sT.h, sT.n)
                nc.scalar.copy(out=hrows[64:65, nch * 512:(nch + 1) * 512],
                               in_=psS[0:1, 0:512]).then_inc(*sSq.inc())
            nc.vector.wait_ge(sSq.h, sSq.n)
            nc.vector.tensor_scalar(out=hrows[64:65, :], in0=hrows[64:65, :],
                                    scalar1=1.0 / D, scalar2=1e-6,
                                    op0=ALU.mult, op1=ALU.add).then_inc(*sV.inc())
            nc.vector.wait_ge(sV.h, sV.n)
            nc.vector.reciprocal(hrows[64:65, :], hrows[64:65, :]).then_inc(*sV.inc())
            nc.scalar.wait_ge(sV.h, sV.n)
            nc.scalar.activation(out=hrows[64:65, :], in_=hrows[64:65, :], func=AF.Sqrt) \
                .then_inc(*sSq.inc())
            nc.gpsimd.wait_ge(sSq.h, sSq.n)
            nc.gpsimd.dma_start(out=rlhB[:], in_=hrows[64:65, :].unsqueeze(1).broadcast_to((1, 128, TH))) \
                .then_inc(*sW.inc(16))

            with sb("gTs", (128, NSLAB * DI), BF16) as gTs, sb("uTs", (128, NSLAB * DI), BF16) as uTs, \
                 sb("dTs", (128, 16 * D), BF16) as dTs, sb("h2t", (128, NSLAB * 512), BF16) as h2t, \
                 sb("gsl", (128, 16 * 512), BF16) as gsl, sb("msl", (128, 16 * 512), BF16) as msl, \
                 sb("fres", (128, 2 * 512)) as fres, sb("xres", (128, 2 * 512)) as xres:
                nc.gpsimd.wait_ge(sV.h, sV.n)
                nc.gpsimd.dma_start(out=gTs[:], in_=_slabs(gT)) \
                    .then_inc(*sSp.inc(16))
                nc.gpsimd.dma_start(out=uTs[:], in_=_slabs(uT)) \
                    .then_inc(*sSp.inc(16))
                nc.gpsimd.dma_start(out=dTs[:], in_=_slabs(dTw)) \
                    .then_inc(*sSp.inc(16))
                nc.vector.wait_ge(sW.h, sW.n)
                nc.tensor.wait_ge(sSp.h, sSp.n)
                for st_ in range(TH // 512):
                    for k in range(NSLAB):
                        pe2 = k % 2
                        nc.sync.wait_ge(sV.h, sV.n)
                        nc.sync.dma_start(out=x1w[:, pe2 * 512:(pe2 + 1) * 512],
                                          in_=x1Sc[k][:, st_ * 512:(st_ + 1) * 512]) \
                            .then_inc(*sXs.inc(16))
                        nc.vector.wait_ge(sXs.h, sXs.n)
                        nc.vector.scalar_tensor_tensor(
                            out=h2t[:, k * 512:(k + 1) * 512],
                            in0=x1w[:, pe2 * 512:(pe2 + 1) * 512],
                            scalar=colbuf[:, C_W2 + k:C_W2 + k + 1],
                            in1=rlhB[:, st_ * 512:(st_ + 1) * 512],
                            op0=ALU.mult, op1=ALU.mult).then_inc(*sV.inc())
                    nc.vector.engine_nop().then_inc(*sV.inc())
                    nc.tensor.wait_ge(sV.h, sV.n)
                    for m in range(16):
                        psm = [psA, psB][m % 2]
                        nc.tensor.wait_ge(sEv.h, sEv.n - 1 if m >= 2 else 0)
                        for k in range(NSLAB):
                            mm = nc.tensor.matmul(psm[:, 0:512],
                                                  gTs[:, k * DI + m * 128:k * DI + (m + 1) * 128],
                                                  h2t[:, k * 512:(k + 1) * 512],
                                                  start=(k == 0), stop=(k == NSLAB - 1))
                            if k == NSLAB - 1:
                                mm.then_inc(*sT.inc())
                        nc.scalar.wait_ge(sT.h, sT.n)
                        nc.scalar.activation(out=gsl[:, m * 512:(m + 1) * 512],
                                             in_=psm[:, 0:512], func=AF.Silu).then_inc(*sEv.inc())
                    for m in range(16):
                        psm = [psC, psD][m % 2]
                        nc.tensor.wait_ge(sV.h, sV.n - 1 if m >= 2 else 0)
                        for k in range(NSLAB):
                            mm = nc.tensor.matmul(psm[:, 0:512],
                                                  uTs[:, k * DI + m * 128:k * DI + (m + 1) * 128],
                                                  h2t[:, k * 512:(k + 1) * 512],
                                                  start=(k == 0), stop=(k == NSLAB - 1))
                            if k == NSLAB - 1:
                                mm.then_inc(*sT.inc())
                        nc.vector.wait_ge(sT.h, sT.n)
                        nc.vector.wait_ge(sEv.h, sEv.n)
                        nc.vector.tensor_mul(msl[:, m * 512:(m + 1) * 512],
                                             psm[:, 0:512], gsl[:, m * 512:(m + 1) * 512]) \
                            .then_inc(*sV.inc())
                    for m in range(NSLAB):
                        pe = m % 2
                        nc.sync.wait_ge(sOut.h, sOut.n)
                        nc.sync.dma_start(out=fres[:, pe * 512:(pe + 1) * 512],
                                          in_=ffnSc[m][:, st_ * 512:(st_ + 1) * 512]) \
                            .then_inc(*sXs.inc(16))
                        sXs_ffn = sXs.n
                        psm = [psA, psB][m % 2]
                        nc.tensor.wait_ge(sV.h, sV.n)
                        nc.tensor.wait_ge(sEv.h, sEv.n)
                        for k in range(16):
                            mm = nc.tensor.matmul(psm[:, 0:512],
                                                  dTs[:, k * D + m * 128:k * D + (m + 1) * 128],
                                                  msl[:, k * 512:(k + 1) * 512],
                                                  start=(k == 0), stop=(k == 15))
                            if k == 15:
                                mm.then_inc(*sT.inc())
                        nc.scalar.wait_ge(sXs.h, sXs_ffn)
                        nc.scalar.wait_ge(sT.h, sT.n)
                        nc.sync.wait_ge(sV.h, sV.n)
                        nc.sync.dma_start(out=xres[:, pe * 512:(pe + 1) * 512],
                                          in_=xTh[m * 128:(m + 1) * 128, st_ * 512:(st_ + 1) * 512]) \
                            .then_inc(*sXs.inc(16))
                        nc.sync.wait_ge(sV.h, sV.n)
                        nc.sync.dma_start(out=x1w[:, pe * 512:(pe + 1) * 512],
                                          in_=x1Sc[m][:, st_ * 512:(st_ + 1) * 512]) \
                            .then_inc(*sXs.inc(16))
                        sXs_x1 = sXs.n
                        ov = fres[:, pe * 512:(pe + 1) * 512]
                        nc.vector.wait_ge(sXs.h, sXs_ffn)
                        nc.vector.wait_ge(sXs.h, sXs_x1)
                        nc.vector.wait_ge(sT.h, sT.n)
                        nc.vector.tensor_add(ov, ov, x1w[:, pe * 512:(pe + 1) * 512])
                        nc.vector.tensor_add(ov, ov, psm[:, 0:512]).then_inc(*sEv.inc())
                        nc.vector.wait_ge(sXs.h, sXs.n)
                        nc.vector.tensor_add(ov, ov, xres[:, pe * 512:(pe + 1) * 512]).then_inc(*sV.inc())
                        nc.gpsimd.wait_ge(sV.h, sV.n)
                        nc.gpsimd.dma_start(out=outT[m * 128:(m + 1) * 128, st_ * 512:(st_ + 1) * 512],
                                            in_=ov).then_inc(*sOut.inc(16))
    if probe:
        nc.gpsimd.wait_ge(sV.h, sV.n)
        nc.gpsimd.wait_ge(sSq.h, sSq.n)
        nc.gpsimd.wait_ge(sCC.h, sCC.n)
        nc.gpsimd.dma_start(out=hrowsD, in_=hrows[:]).then_inc(*sOut.inc(16))
        o1 = HF // 128
        o2 = o1 + NSLAB
        o3 = o2 + HPC * NSH
        nc.gpsimd.dma_start(out=bcolD[:, 0:o1], in_=b1col[:]).then_inc(*sOut.inc(16))
        nc.gpsimd.dma_start(out=bcolD[:, o1:o2], in_=b2col[:]).then_inc(*sOut.inc(16))
        nc.gpsimd.dma_start(out=bcolD[:, o2:o3], in_=ascl[:]).then_inc(*sOut.inc(16))
        nc.gpsimd.dma_start(out=bcolD[:, o3:o3 + 80], in_=colbuf[:]).then_inc(*sOut.inc(16))
        for j in range(HPC // 2):
            nc.gpsimd.dma_start(out=cinD[j], in_=cin.ap()[j]).then_inc(*sOut.inc(16))
            nc.gpsimd.dma_start(out=coutD[0, j], in_=cout.ap()[0, j]).then_inc(*sOut.inc(16))
            nc.gpsimd.dma_start(out=coutD[1, j], in_=cout.ap()[1, j]).then_inc(*sOut.inc(16))
    nc.gpsimd.wait_ge(sOut.h, sOut.n)
    es.close()
    return nc


_NC_CACHE = {}


def kernel(**inp):
    x = np.asarray(inp["L0_attn_x"], np.float32)
    n1w = np.asarray(inp["L0_attn_norm1_w"], np.float32)
    n2w = np.asarray(inp["L0_attn_norm2_w"], np.float32)
    dwk = np.asarray(inp["L0_attn_dw_kernel"], np.float32)
    dwb = np.asarray(inp["L0_attn_dw_bias"], np.float32)
    pww = np.asarray(inp["L0_attn_pw_w"], np.float32)
    pwb = np.asarray(inp["L0_attn_pw_b"], np.float32)
    xpw = np.asarray(inp["L0_attn_x_proj_w"], np.float32)
    dtw = np.asarray(inp["L0_attn_dt_proj_w"], np.float32)
    dtb = np.asarray(inp["L0_attn_dt_proj_b"], np.float32)
    alog = np.asarray(inp["L0_attn_A_log"], np.float32)
    opw = np.asarray(inp["L0_attn_out_proj_w"], np.float32)
    opb = np.asarray(inp["L0_attn_out_proj_b"], np.float32)
    gw = np.asarray(inp["L0_attn_gate_w"], np.float32)
    uw = np.asarray(inp["L0_attn_up_w"], np.float32)
    dw = np.asarray(inp["L0_attn_down_w"], np.float32)
    Wexp = np.asarray(inp["L0_ffn_W_exp"], np.float32)
    bexp = np.asarray(inp["L0_ffn_b_exp"], np.float32)
    upw_bf = np.asarray(inp["L0_ffn_up_weights"], np.float32)
    upb = np.asarray(inp["L0_ffn_up_bias"], np.float32)
    dnw_bf = np.asarray(inp["L0_ffn_down_weights"], np.float32)
    Wcon = np.asarray(inp["L0_ffn_W_con"], np.float32)
    bcon = np.asarray(inp["L0_ffn_b_con"], np.float32)
    dnb = np.asarray(inp["L0_ffn_down_bias"], np.float32)

    u_lowN, u_lowT, u_highN, u_highT = _stage_mats(upw_bf)
    d_lowN, d_lowT, d_highN, d_highT = _stage_mats(dnw_bf)
    ub_perm = np.ascontiguousarray(upb.reshape(64, 64).T).reshape(-1)

    common = dict(
        w1col=_col(n1w), w2col=_col(n2w),
        kercol=np.ascontiguousarray(
            dwk[:, 0, :].reshape(NSLAB, 128, KC).transpose(1, 0, 2).reshape(128, NSLAB * KC)),
        dwbcol=_col(dwb),
        xpT=_blkdiag2(xpw.T), dtpT=_blkdiag2(dtw.T),
        dtb2=np.ascontiguousarray(np.concatenate([dtb, dtb])[:, None]),
        opT=np.ascontiguousarray(opw.T), opbc=_col(opb),
        gT=np.ascontiguousarray(gw.T), uT=np.ascontiguousarray(uw.T),
        dTw=np.ascontiguousarray(dw.T),
        EXPc=np.ascontiguousarray(np.concatenate([Wexp.T, bexp[:, None]], axis=1)),
        WCS=np.ascontiguousarray(Wcon.reshape(64, 64, D).transpose(1, 0, 2).reshape(HF, D)),
        ubcol=_col(ub_perm), bcncol=_col(bcon), dbcol=_col(dnb),
        bfu_low=u_lowN, bfu_lowT=u_lowT, bfu_high=u_highN, bfu_highT=u_highT,
        bfd_low=d_lowN, bfd_lowT=d_lowT, bfd_high=d_highN, bfd_highT=d_highT,
    )

    in_maps = []
    for c in range(8):
        b, hg = c // 2, c % 2
        xbT = np.ascontiguousarray(x[b].T)
        m = dict(common)
        m["xT"] = xbT
        m["xTh"] = np.ascontiguousarray(xbT[:, hg * TH:(hg + 1) * TH])
        m["pwT"] = np.ascontiguousarray(pww.T[:, hg * MYC:(hg + 1) * MYC])
        m["pwbc"] = _col(pwb[hg * MYC:(hg + 1) * MYC])
        m["alog8"] = np.ascontiguousarray(alog[hg * HPC:(hg + 1) * HPC])
        in_maps.append(m)

    key = ("nc_dbg" if DEBUG else "nc") + str(STAGE) + ("p" if PROBE else "")
    if key not in _NC_CACHE:
        _NC_CACHE[key] = build_nc(dbg=DEBUG, stage=STAGE, probe=PROBE)
    nc = _NC_CACHE[key]
    res = run_bass_kernel_spmd(nc, in_maps, core_ids=list(range(8)))
    out = np.empty((B, L, D), np.float32)
    for c in range(8):
        b, th = c // 2, c % 2
        out[b, th * TH:(th + 1) * TH, :] = res.results[c]["outT"].T
    if DEBUG or PROBE:
        kernel.last_results = res
    return out


# version 04 (final)
# speedup vs baseline: 2.0459x; 1.1889x over previous
import numpy as np
from contextlib import ExitStack
import concourse.bass as bass
import concourse.mybir as mybir
from concourse.bass_utils import run_bass_kernel_spmd

F32 = mybir.dt.float32
BF16 = mybir.dt.bfloat16
AF = mybir.ActivationFunctionType
ALU = mybir.AluOpType
Pool = mybir.EngineType.Pool

B, L, D = 4, 4096, 1024
H, DH, DS, DTR = 16, 64, 48, 64
KC = 4
DI = 2048
HF = 4096
HPC = 8
MYC = 512
TH = 2048
NSH = DS // 2
NSLAB = D // 128
DP1 = D + 1
DEBUG = False
PROBE = False
STAGE = 9
SER_BCAST = False
SER_ONEFLIGHT = False


def _stage_mats(weights):
    w = np.asarray(weights, np.float32)
    lowN = np.zeros((6, 64, 64, 64), np.float32)
    highN = np.zeros((6, 64, 64, 64), np.float32)
    for s in range(12):
        s2 = 1 << s
        ws = w[s].reshape(HF // (2 * s2), s2, 2, 2)
        for blk in range(HF // (2 * s2)):
            for r in range(s2):
                for half in range(2):
                    i = blk * 2 * s2 + half * s2 + r
                    j0 = blk * 2 * s2 + r
                    j1 = j0 + s2
                    if s < 6:
                        lowN[s, i // 64, i % 64, j0 % 64] = ws[blk, r, half, 0]
                        lowN[s, i // 64, i % 64, j1 % 64] = ws[blk, r, half, 1]
                    else:
                        l = r % 64
                        highN[s - 6, l, i // 64, j0 // 64] = ws[blk, r, half, 0]
                        highN[s - 6, l, i // 64, j1 // 64] = ws[blk, r, half, 1]
    lowT = lowN.transpose(0, 1, 3, 2)
    highT = highN.transpose(0, 1, 3, 2)
    def arr(a):
        return np.ascontiguousarray(a.transpose(0, 2, 1, 3).reshape(6, 64, 64 * 64))
    return arr(lowN), arr(lowT), arr(highN), arr(highT)


def _blkdiag2(a):
    r, c = a.shape
    out = np.zeros((2 * r, 2 * c), np.float32)
    out[0:r, 0:c] = a
    out[r:2 * r, c:2 * c] = a
    return out


def _col(v, p=128):
    v = np.asarray(v, np.float32)
    return np.ascontiguousarray(v.reshape(-1, p).T)


def _slabs(ap2d, p=128):
    rows, C = ap2d.shape
    Sn = rows // p
    return bass.AP(ap2d.tensor, ap2d.offset, [[C, p], [p * C, Sn], [1, C]])


class Ctr:
    def __init__(self, es, nc, name):
        self.h = es.enter_context(nc.semaphore(name))
        self.n = 0

    def inc(self, k=1):
        self.n += k
        return self.h, k


def build_nc(dbg=False, stage=9, probe=False):
    nc = bass.Bass("TRN2", target_bir_lowering=False, debug=False)
    es = ExitStack()

    def dram(name, shape, dt=F32, kind="Internal"):
        return nc.dram_tensor(name, list(shape), dt, kind=kind)

    def pdram(name, shape, dt=F32):
        return nc.dram_tensor(name, list(shape), dt,
                              kind="ExternalOutput" if probe else "Internal")

    xT = dram("xT", (D, L), kind="ExternalInput").ap()
    xTh = dram("xTh", (D, TH), kind="ExternalInput").ap()
    w1col = dram("w1col", (128, NSLAB), kind="ExternalInput").ap()
    w2col = dram("w2col", (128, NSLAB), kind="ExternalInput").ap()
    kercol = dram("kercol", (128, NSLAB * KC), kind="ExternalInput").ap()
    dwbcol = dram("dwbcol", (128, NSLAB), kind="ExternalInput").ap()
    pwT = dram("pwT", (D, MYC), kind="ExternalInput").ap()
    pwbc = dram("pwbc", (128, MYC // 128), kind="ExternalInput").ap()
    xpT = dram("xpT", (128, 320), kind="ExternalInput").ap()
    dtpT = dram("dtpT", (128, 128), kind="ExternalInput").ap()
    dtb2 = dram("dtb2", (128, 1), kind="ExternalInput").ap()
    alog8 = dram("alog8", (HPC, DS), kind="ExternalInput").ap()
    opT = dram("opT", (D, D), kind="ExternalInput").ap()
    opbc = dram("opbc", (128, NSLAB), kind="ExternalInput").ap()
    gT = dram("gT", (D, DI), kind="ExternalInput").ap()
    uT = dram("uT", (D, DI), kind="ExternalInput").ap()
    dTw = dram("dTw", (DI, D), kind="ExternalInput").ap()
    EXPc = dram("EXPc", (HF, DP1), kind="ExternalInput").ap()
    WCS = dram("WCS", (HF, D), kind="ExternalInput").ap()
    ubcol = dram("ubcol", (128, HF // 128), kind="ExternalInput").ap()
    bcncol = dram("bcncol", (128, NSLAB), kind="ExternalInput").ap()
    dbcol = dram("dbcol", (128, NSLAB), kind="ExternalInput").ap()
    EselI = dram("EselI", (36, 256), BF16, kind="ExternalInput").ap()
    FoldI = dram("FoldI", (128, 64), kind="ExternalInput").ap()
    bstk = {}
    for nm in ["bfu_low", "bfu_lowT", "bfu_high", "bfu_highT",
               "bfd_low", "bfd_lowT", "bfd_high", "bfd_highT"]:
        bstk[nm] = dram(nm, (6, 64, 64 * 64), kind="ExternalInput").ap()

    outT = dram("outT", (D, TH), kind="ExternalOutput").ap()
    WexpP = pdram("WexpP", (NSLAB, 128, HF), BF16).ap()
    WexpB = pdram("WexpB", (1, HF)).ap()
    WconP = pdram("WconP", (HF, D), BF16).ap()
    bcSp = pdram("bcSp", (HPC, 96, L), BF16).ap()
    dtxSp = pdram("dtxSp", (HPC // 2, 128, L), BF16).ap()
    xhSp = pdram("xhSp", (HPC // 2, 128, L), BF16).ap()
    xnhSp = pdram("xnhSp", (NSLAB, 128, TH), BF16).ap()
    ffnSc = pdram("ffnSc", (NSLAB, 128, TH)).ap()
    x1Sc = pdram("x1Sc", (NSLAB, 128, TH)).ap()
    b1bnc = dram("b1bnc", (1, HF)).ap()
    cin = nc.dram_tensor("cin", [HPC // 2, 128, L], BF16)
    cout = nc.dram_tensor("cout", [2, HPC // 2, 128, L], BF16)
    if probe:
        hrowsD = dram("hrowsD", (128, TH), kind="ExternalOutput").ap()
        cinD = dram("cinD", (HPC // 2, 128, L), BF16, kind="ExternalOutput").ap()
        coutD = dram("coutD", (2, HPC // 2, 128, L), BF16, kind="ExternalOutput").ap()
        bcolD = dram("bcolD", (128, NSLAB + HF // 128 + HPC * NSH + 80),
                     kind="ExternalOutput").ap()

    dbg_out = {}
    if dbg:
        for nm, shp in [("dxn", (128, L)), ("duc", (128, L)), ("dxh", (128, L)),
                        ("ddtx", (128, L)), ("dbc", (96, L)), ("ddt", (128, L)),
                        ("dy", (128, L)), ("dx1", (128, TH)), ("dffn", (128, TH)),
                        ("dW1", (128, HF)), ("dWc", (128, D)), ("drows", (8, L))]:
            dbg_out[nm] = dram(nm, shp, kind="ExternalOutput").ap()

    S = lambda name: Ctr(es, nc, name)
    sW, sXs, sSq, sV, sT, sEv, sSp = S("sW"), S("sXs"), S("sSq"), S("sV"), S("sT"), S("sEv"), S("sSp")
    sBC, sAB, sVs, sCC, sBk, sOut = S("sBC"), S("sAB"), S("sVs"), S("sCC"), S("sBk"), S("sOut")
    sTB, sBE = S("sTB"), S("sBE")
    sSt = [S("sSt0"), S("sSt1")]

    sb = lambda name, shape, dt=F32: nc.sbuf_tensor(name, list(shape), dt)

    ones = es.enter_context(sb("ones", (128, 1)))
    hrows = es.enter_context(sb("hrows", (128, TH)))
    colbuf = es.enter_context(sb("colbuf", (128, 80)))
    b1col = es.enter_context(sb("b1col", (128, HF // 128)))
    b2col = es.enter_context(sb("b2col", (128, NSLAB)))
    ascl = es.enter_context(sb("ascl", (128, HPC * NSH)))
    R_M, R_RLN, R_GU, R_T0, R_T1, R_T2 = 0, 1, 2, 3, 4, 5
    rows_ref = {}
    ROWMAP = {0: (0, 0), 1: (64, 0), 2: (96, 0), 3: (32, 0),
              4: (32, 1), 5: (64, 1), 6: (0, 1), 7: (32, 2)}

    def ROW(r, c0=0, c1=L):
        p, c = ROWMAP[r]
        return rows_ref["t"][p:p + 1, c * L + c0:c * L + c1]
    C_W1, C_W2, C_KER, C_DWB, C_PWB, C_OPB, C_DTB = 0, 8, 16, 48, 56, 60, 68

    nc.vector.memset(ones[:], 1.0)
    Fold = es.enter_context(sb("Fold", (128, 64)))
    Esel = es.enter_context(sb("Esel", (36, 256), BF16))
    nc.sync.dma_start(out=Fold[:], in_=FoldI).then_inc(*sW.inc(16))
    nc.sync.dma_start(out=Esel[:], in_=EselI).then_inc(*sW.inc(16))

    nc.sync.dma_start(out=colbuf[:, C_W1:C_W1 + 8], in_=w1col).then_inc(*sW.inc(16))
    nc.sync.dma_start(out=colbuf[:, C_W2:C_W2 + 8], in_=w2col).then_inc(*sW.inc(16))
    nc.sync.dma_start(out=colbuf[:, C_KER:C_KER + 32], in_=kercol).then_inc(*sW.inc(16))
    nc.sync.dma_start(out=colbuf[:, C_DWB:C_DWB + 8], in_=dwbcol).then_inc(*sW.inc(16))
    nc.sync.dma_start(out=colbuf[:, C_PWB:C_PWB + 4], in_=pwbc).then_inc(*sW.inc(16))
    nc.sync.dma_start(out=colbuf[:, C_OPB:C_OPB + 8], in_=opbc).then_inc(*sW.inc(16))
    nc.sync.dma_start(out=colbuf[:, C_DTB:C_DTB + 1], in_=dtb2).then_inc(*sW.inc(16))
    sW_const = sW.n

    atmp = es.enter_context(sb("atmp", (HPC, DS)))
    nc.sync.dma_start(out=atmp[:], in_=alog8).then_inc(*sW.inc(16))
    nc.scalar.wait_ge(sW.h, sW.n)
    nc.scalar.activation(out=atmp[:], in_=atmp[:], func=AF.Exp).then_inc(*sSq.inc())
    nc.vector.wait_ge(sSq.h, sSq.n)
    nc.vector.tensor_scalar_mul(atmp[:], atmp[:], -1.0).then_inc(*sV.inc())
    aeo = es.enter_context(sb("aeo", (HPC, DS)))
    nc.vector.wait_ge(sV.h, sV.n)
    nc.vector.tensor_copy(out=aeo[:, 0:NSH], in_=atmp[:, 0:DS:2]).then_inc(*sV.inc())
    nc.vector.wait_ge(sV.h, sV.n)
    nc.vector.tensor_copy(out=aeo[:, NSH:DS], in_=atmp[:, 1:DS:2]).then_inc(*sV.inc())
    nc.gpsimd.wait_ge(sV.h, sV.n)
    for h in range(HPC):
        src0 = aeo[h:h + 1, 0:NSH].unsqueeze(1).broadcast_to((1, 64, NSH))
        src1 = aeo[h:h + 1, NSH:DS].unsqueeze(1).broadcast_to((1, 64, NSH))
        nc.gpsimd.dma_start(out=ascl[0:64, h * NSH:(h + 1) * NSH], in_=src0).then_inc(*sW.inc(16))
        nc.gpsimd.dma_start(out=ascl[64:128, h * NSH:(h + 1) * NSH], in_=src1).then_inc(*sW.inc(16))

    psA = es.enter_context(nc.psum_tensor("psA", [128, 512], F32))
    psB = es.enter_context(nc.psum_tensor("psB", [128, 512], F32))
    psC = es.enter_context(nc.psum_tensor("psC", [128, 512], F32))
    psD = es.enter_context(nc.psum_tensor("psD", [128, 512], F32))
    psS = es.enter_context(nc.psum_tensor("psS", [64, 512], F32))

    pid = nc.partition_id([Pool])
    parity = nc.gpsimd.compute_val(pid % 2)

    def make_compose(stg, Rbuf):
        def compose(dst, step_stack, steps, init_stack, init_idx):
                nc.sync.wait_ge(sT.h, sT.n)
                nc.sync.dma_start(out=Rbuf[:],
                                  in_=init_stack[init_idx]) \
                    .then_inc(*sXs.inc(16))
                cur, other = Rbuf, dst
                for s in steps:
                    nc.sync.wait_ge(sT.h, sT.n)
                    nc.sync.dma_start(out=stg[:],
                                      in_=step_stack[s]) \
                        .then_inc(*sXs.inc(16))
                    nc.tensor.wait_ge(sXs.h, sXs.n)
                    nc.tensor.wait_ge(sV.h, sV.n)
                    for grp in range(8):
                        nc.tensor.wait_ge(sEv.h, sEv.n)
                        for jb in range(8):
                            blk = grp * 8 + jb
                            mm = nc.tensor.matmul(psA[0:64, jb * 64:(jb + 1) * 64],
                                                  stg[:, blk * 64:(blk + 1) * 64],
                                                  cur[:, blk * 64:(blk + 1) * 64],
                                                  start=True, stop=True)
                            if jb == 7:
                                mm.then_inc(*sT.inc())
                        nc.scalar.wait_ge(sT.h, sT.n)
                        nc.scalar.copy(out=other[0:64, grp * 512:(grp + 1) * 512],
                                       in_=psA[0:64, 0:512]).then_inc(*sEv.inc())
                    cur, other = other, cur
                if cur is not dst:
                    nc.vector.wait_ge(sEv.h, sEv.n)
                    nc.vector.tensor_copy(out=dst[:], in_=cur[:]).then_inc(*sV.inc())
        return compose

    with sb("DupT", (64, 4096)) as DupT, sb("HupT", (64, 4096)) as HupT:
        with sb("stg", (64, 4096)) as stg, sb("Rbuf", (64, 4096)) as Rbuf:
            compose = make_compose(stg, Rbuf)
            compose(DupT, bstk["bfu_low"], [4, 3, 2, 1, 0], bstk["bfu_lowT"], 5)
            compose(HupT, bstk["bfu_high"], [4, 3, 2, 1, 0], bstk["bfu_highT"], 5)
            nc.sync.wait_ge(sEv.h, sEv.n)
            nc.sync.wait_ge(sV.h, sV.n)

        with sb("V2", (128, 32 * DP1)) as V2, sb("Eb", (64, DP1)) as Eb, \
             sb("Vsh", (64, DP1)) as Vsh, sb("wst", (128, 64), BF16) as wst, \
             sb("bst", (1, 64)) as bst:
            for ih in range(64):
                pe = ih % 2
                nc.sync.wait_ge(sT.h, sT.n)
                nc.sync.dma_start(out=Eb[:],
                                  in_=EXPc[ih * 64:(ih + 1) * 64, :]).then_inc(*sXs.inc(16))
                nc.tensor.wait_ge(sXs.h, sXs.n)
                for nch in range(3):
                    n0 = nch * 512
                    n1 = min(n0 + 512, DP1)
                    nc.tensor.wait_ge(sEv.h, sEv.n)
                    nc.tensor.matmul(psB[0:64, 0:n1 - n0],
                                     DupT[:, ih * 64:(ih + 1) * 64],
                                     Eb[:, n0:n1],
                                     start=True, stop=True).then_inc(*sT.inc())
                    nc.scalar.wait_ge(sT.h, sT.n)
                    nc.scalar.copy(out=V2[pe * 64:pe * 64 + 64,
                                          (ih // 2) * DP1 + n0:(ih // 2) * DP1 + n1],
                                   in_=psB[0:64, 0:n1 - n0]).then_inc(*sEv.inc())
            nc.gpsimd.wait_ge(sEv.h, sEv.n)
            for jl in range(64):
                srcA = V2[jl:jl + 1, :].rearrange("p (g c) -> p g c", g=32)
                srcB = V2[64 + jl:64 + jl + 1, :].rearrange("p (g c) -> p g c", g=32)
                dstA = bass.AP(Vsh[:].tensor, 0, [[2 * DP1, 32], [1, DP1]])
                dstB = bass.AP(Vsh[:].tensor, DP1, [[2 * DP1, 32], [1, DP1]])
                nc.gpsimd.wait_ge(sT.h, sT.n)
                nc.gpsimd.dma_start(out=dstA, in_=srcA).then_inc(*sSp.inc(16))
                nc.gpsimd.dma_start(out=dstB, in_=srcB).then_inc(*sSp.inc(16))
                nc.tensor.wait_ge(sSp.h, sSp.n)
                for dch in range(9):
                    d0 = dch * 128
                    dl = 128 if dch < 8 else 1
                    nc.tensor.wait_ge(sEv.h, sEv.n)
                    nc.tensor.matmul(psC[0:dl, 0:64],
                                     Vsh[:, d0:d0 + dl],
                                     HupT[:, jl * 64:(jl + 1) * 64],
                                     start=True, stop=True).then_inc(*sT.inc())
                    nc.scalar.wait_ge(sT.h, sT.n)
                    if dch < 8:
                        nc.scalar.copy(out=wst[:, 0:64], in_=psC[0:128, 0:64]).then_inc(*sEv.inc())
                        nc.gpsimd.wait_ge(sEv.h, sEv.n)
                        nc.gpsimd.dma_start(out=WexpP[dch, :, jl * 64:(jl + 1) * 64],
                                            in_=wst[:, 0:64]).then_inc(*sSp.inc(16))
                    else:
                        nc.scalar.copy(out=bst[:], in_=psC[0:1, 0:64]).then_inc(*sEv.inc())
                        nc.gpsimd.wait_ge(sEv.h, sEv.n)
                        nc.gpsimd.dma_start(out=WexpB[0:1, jl * 64:(jl + 1) * 64],
                                            in_=bst[:]).then_inc(*sSp.inc(16))
                    nc.scalar.wait_ge(sSp.h, sSp.n)

    with sb("Hdn", (64, 4096)) as Hdn, sb("Ddn", (64, 4096)) as Ddn:
        with sb("stg2", (64, 4096)) as stg, sb("Rbuf2", (64, 4096)) as Rbuf:
            compose = make_compose(stg, Rbuf)
            compose(Hdn, bstk["bfd_highT"], [1, 2, 3, 4, 5], bstk["bfd_high"], 0)
            compose(Ddn, bstk["bfd_lowT"], [1, 2, 3, 4, 5], bstk["bfd_low"], 0)
            nc.sync.wait_ge(sEv.h, sEv.n)
            nc.sync.wait_ge(sV.h, sV.n)
        with sb("U2", (128, 32 * D)) as U2, sb("Wcb", (64, 2 * D)) as Wcb, \
             sb("Ush", (64, 2 * D)) as Ush, sb("wst2", (64, D), BF16) as wst2:
            for jl in range(64):
                pe = jl % 2
                nc.sync.wait_ge(sT.h, sT.n)
                nc.sync.dma_start(out=Wcb[:, pe * D:(pe + 1) * D],
                                  in_=WCS[jl * 64:(jl + 1) * 64, :]).then_inc(*sXs.inc(16))
                nc.tensor.wait_ge(sXs.h, sXs.n)
                for nch in range(2):
                    n0 = nch * 512
                    nc.tensor.wait_ge(sEv.h, sEv.n)
                    nc.tensor.matmul(psB[0:64, 0:512],
                                     Hdn[:, jl * 64:(jl + 1) * 64],
                                     Wcb[:, pe * D + n0:pe * D + n0 + 512],
                                     start=True, stop=True).then_inc(*sT.inc())
                    nc.scalar.wait_ge(sT.h, sT.n)
                    nc.scalar.copy(out=U2[pe * 64:pe * 64 + 64,
                                          (jl // 2) * D + n0:(jl // 2) * D + n0 + 512],
                                   in_=psB[0:64, 0:512]).then_inc(*sEv.inc())
            nc.gpsimd.wait_ge(sEv.h, sEv.n)
            for ih in range(64):
                pe = ih % 2
                srcA = U2[ih:ih + 1, :].rearrange("p (g c) -> p g c", g=32)
                srcB = U2[64 + ih:64 + ih + 1, :].rearrange("p (g c) -> p g c", g=32)
                dstA = bass.AP(Ush[:].tensor, pe * D, [[2 * (2 * D), 32], [1, D]])
                dstB = bass.AP(Ush[:].tensor, (2 * D) + pe * D, [[2 * (2 * D), 32], [1, D]])
                nc.gpsimd.wait_ge(sT.h, sT.n)
                nc.gpsimd.dma_start(out=dstA, in_=srcA).then_inc(*sSp.inc(16))
                nc.gpsimd.dma_start(out=dstB, in_=srcB).then_inc(*sSp.inc(16))
                nc.tensor.wait_ge(sSp.h, sSp.n)
                for nch in range(2):
                    n0 = nch * 512
                    nc.tensor.wait_ge(sEv.h, sEv.n)
                    nc.tensor.matmul(psB[0:64, 0:512],
                                     Ddn[:, ih * 64:(ih + 1) * 64],
                                     Ush[:, pe * D + n0:pe * D + n0 + 512],
                                     start=True, stop=True).then_inc(*sT.inc())
                    nc.scalar.wait_ge(sT.h, sT.n)
                    nc.scalar.copy(out=wst2[0:64, n0:n0 + 512],
                                   in_=psB[0:64, 0:512]).then_inc(*sEv.inc())
                nc.gpsimd.wait_ge(sEv.h, sEv.n)
                dstw = bass.AP(WconP.tensor, ih * D, [[64 * D, 64], [1, D]])
                nc.gpsimd.dma_start(out=dstw, in_=wst2[0:64, :]).then_inc(*sSp.inc(16))
                nc.scalar.wait_ge(sSp.h, sSp.n)

    nc.sync.wait_ge(sSp.h, sSp.n)
    nc.sync.dma_start(out=b1bnc, in_=WexpB).then_inc(*sW.inc(16))
    nc.sync.wait_ge(sW.h, sW.n)
    nc.sync.dma_start(out=b1col[:], in_=bass.AP(b1bnc.tensor, 0, [[1, 128], [128, HF // 128]]),
                      allow_slow_non_contiguous=True).then_inc(*sW.inc(16))
    nc.sync.dma_start(out=b2col[:], in_=bcncol).then_inc(*sW.inc(16))
    with sb("tmpc", (128, NSLAB + HF // 128)) as tmpc:
        nc.sync.dma_start(out=tmpc[:, 0:NSLAB], in_=dbcol).then_inc(*sW.inc(16))
        nc.sync.dma_start(out=tmpc[:, NSLAB:], in_=ubcol).then_inc(*sW.inc(16))
        nc.vector.wait_ge(sW.h, sW.n)
        nc.vector.tensor_add(b2col[:], b2col[:], tmpc[:, 0:NSLAB])
        nc.vector.tensor_add(b1col[:], b1col[:], tmpc[:, NSLAB:]).then_inc(*sV.inc())

    if dbg:
        nc.gpsimd.wait_ge(sSp.h, sSp.n)
        nc.gpsimd.dma_start(out=dbg_out["dW1"], in_=WexpP[0]).then_inc(*sOut.inc(16))
        nc.gpsimd.dma_start(out=dbg_out["dWc"], in_=WconP[0:128, :]).then_inc(*sOut.inc(16))

    def finish():
        nc.gpsimd.wait_ge(sOut.h, sOut.n)
        es.close()
        return nc

    if stage < 1:
        return finish()

    esF = ExitStack()
    rows_ref["t"] = esF.enter_context(sb("rows", (128, 3 * L)))
    with sb("xsl", (128, 2 * L)) as xsl, sb("sqt", (128, L)) as sqt:
        for k in range(NSLAB):
            pe = k % 2
            nc.sync.wait_ge(sSq.h, sSq.n)
            nc.sync.wait_ge(sT.h, sT.n)
            nc.sync.dma_start(out=xsl[:, pe * L:(pe + 1) * L],
                              in_=xT[k * 128:(k + 1) * 128, :]).then_inc(*sXs.inc(16))
            nc.scalar.wait_ge(sXs.h, sXs.n)
            nc.scalar.activation(out=sqt[:], in_=xsl[:, pe * L:(pe + 1) * L],
                                 func=AF.Square).then_inc(*sSq.inc())
            nc.tensor.wait_ge(sSq.h, sSq.n)
            for nch in range(8):
                nc.tensor.wait_ge(sEv.h, sEv.n)
                nc.tensor.matmul(psS[0:1, :], ones[:],
                                 xsl[:, pe * L + nch * 512:pe * L + (nch + 1) * 512],
                                 start=True, stop=True)
                nc.tensor.matmul(psA[0:1, :], ones[:],
                                 sqt[:, nch * 512:(nch + 1) * 512],
                                 start=True, stop=True).then_inc(*sT.inc())
                nc.scalar.wait_ge(sT.h, sT.n)
                nc.scalar.copy(out=ROW(6, nch * 512, (nch + 1) * 512), in_=psS[0:1, :])
                nc.scalar.copy(out=ROW(7, nch * 512, (nch + 1) * 512),
                               in_=psA[0:1, :]).then_inc(*sEv.inc())
            nc.vector.wait_ge(sEv.h, sEv.n)
            if k == 0:
                nc.vector.tensor_scalar_mul(ROW(R_M), ROW(6), 1.0 / D)
                nc.vector.tensor_scalar_mul(ROW(R_T0), ROW(7), 1.0 / D)
            else:
                nc.vector.scalar_tensor_tensor(out=ROW(R_M), in0=ROW(6),
                                               scalar=1.0 / D, in1=ROW(R_M),
                                               op0=ALU.mult, op1=ALU.add)
                nc.vector.scalar_tensor_tensor(out=ROW(R_T0), in0=ROW(7),
                                               scalar=1.0 / D, in1=ROW(R_T0),
                                               op0=ALU.mult, op1=ALU.add)
            nc.vector.engine_nop().then_inc(*sV.inc())
            nc.sync.wait_ge(sV.h, sV.n)
        nc.vector.wait_ge(sV.h, sV.n)
        for opf in [
            lambda: nc.vector.tensor_mul(ROW(R_T1), ROW(R_M), ROW(R_M)),
            lambda: nc.vector.tensor_sub(ROW(R_T1), ROW(R_T0), ROW(R_T1)),
            lambda: nc.vector.tensor_scalar_add(ROW(R_T2), ROW(R_T1), 1e-5),
            lambda: nc.vector.reciprocal(ROW(R_T0), ROW(R_T2)),
        ]:
            opf().then_inc(*sV.inc())
            nc.vector.wait_ge(sV.h, sV.n)
        nc.scalar.wait_ge(sV.h, sV.n)
        nc.scalar.activation(out=ROW(R_RLN), in_=ROW(R_T0),
                             func=AF.Sqrt).then_inc(*sSq.inc())
        nc.vector.wait_ge(sSq.h, sSq.n)
        for opf in [
            lambda: nc.vector.tensor_mul(ROW(R_T2), ROW(R_T1), ROW(R_T0)),
            lambda: nc.vector.tensor_scalar_add(ROW(R_T2), ROW(R_T2), 1e-6),
            lambda: nc.vector.reciprocal(ROW(R_T2), ROW(R_T2)),
        ]:
            opf().then_inc(*sV.inc())
            nc.vector.wait_ge(sV.h, sV.n)
        nc.vector.engine_nop().then_inc(*sV.inc())
        nc.scalar.wait_ge(sV.h, sV.n)
        nc.scalar.activation(out=ROW(R_T2), in_=ROW(R_T2),
                             func=AF.Sqrt).then_inc(*sSq.inc())
        nc.vector.wait_ge(sSq.h, sSq.n)
        nc.vector.tensor_mul(ROW(R_GU), ROW(R_RLN),
                             ROW(R_T2)).then_inc(*sV.inc())
    if dbg:
        nc.gpsimd.wait_ge(sV.h, sV.n)
        nc.gpsimd.dma_start(out=dbg_out["drows"][0:1, :], in_=ROW(R_M)).then_inc(*sOut.inc(16))
        nc.gpsimd.dma_start(out=dbg_out["drows"][1:2, :], in_=ROW(R_RLN)).then_inc(*sOut.inc(16))
        nc.gpsimd.dma_start(out=dbg_out["drows"][2:3, :], in_=ROW(R_GU)).then_inc(*sOut.inc(16))

    if stage < 2:
        esF.close()
        return finish()
    nc.gpsimd.wait_ge(sV.h, sV.n)
    for case in nc.gpsimd.Switch(parity, 2):
        o = case * TH
        nc.gpsimd.dma_start(out=hrows[0:1, :], in_=ROW(R_M, o, o + TH)).then_inc(sBk.h, 16)
        nc.gpsimd.dma_start(out=hrows[32:33, :], in_=ROW(R_RLN, o, o + TH)).then_inc(sBk.h, 16)
    sBk.n += 32

    if stage < 3:
        esF.close()
        return finish()

    with sb("ucsl", (128, NSLAB * L), BF16) as ucsl:
      with sb("mB", (128, L), BF16) as mB, sb("rlnB", (128, L), BF16) as rlnB, sb("guB", (128, L), BF16) as guB, \
         sb("xsl2", (128, L)) as xsl2, sb("t1t", (128, L)) as t1t, \
         sb("xnsl", (128, L), BF16) as xnsl, sb("upad", (128, L + KC - 1)) as upad:
        nc.sync.wait_ge(sV.h, sV.n)
        nc.gpsimd.wait_ge(sV.h, sV.n)
        nc.gpsimd.dma_start(out=mB[:],
                            in_=ROW(R_M).unsqueeze(1).broadcast_to((1, 128, L))) \
            .then_inc(*sW.inc(16))
        for rsrc, rdst in [(R_RLN, rlnB), (R_GU, guB)]:
            nc.gpsimd.dma_start(out=rdst[:],
                                in_=ROW(rsrc).unsqueeze(1).broadcast_to((1, 128, L))) \
                .then_inc(*sW.inc(16))
        nc.vector.wait_ge(sW.h, sW.n)
        nc.vector.memset(upad[:, 0:KC - 1], 0.0)
        for k in range(NSLAB):
            pe = k % 2
            nc.sync.wait_ge(sV.h, sV.n)
            nc.sync.dma_start(out=xsl2[:],
                              in_=xT[k * 128:(k + 1) * 128, :]).then_inc(*sXs.inc(16))
            nc.vector.wait_ge(sXs.h, sXs.n)
            nc.vector.wait_ge(sSp.h, sSp.n)
            xs = xsl2[:]
            nc.vector.tensor_sub(t1t[:], xs, mB[:])
            nc.vector.tensor_mul(xnsl[:], t1t[:], rlnB[:])
            nc.vector.scalar_tensor_tensor(out=upad[:, KC - 1:], in0=t1t[:],
                                           scalar=colbuf[:, C_W1 + k:C_W1 + k + 1], in1=guB[:],
                                           op0=ALU.mult, op1=ALU.mult)
            nc.vector.tensor_scalar_mul(t1t[:], upad[:, 0:L],
                                        colbuf[:, C_KER + k * 4:C_KER + k * 4 + 1])
            for kk in range(1, KC):
                dst = ucsl[:, k * L:(k + 1) * L] if kk == KC - 1 else t1t[:]
                nc.vector.scalar_tensor_tensor(out=dst, in0=upad[:, kk:kk + L],
                                               scalar=colbuf[:, C_KER + k * 4 + kk:C_KER + k * 4 + kk + 1],
                                               in1=t1t[:], op0=ALU.mult, op1=ALU.add)
            nc.vector.engine_nop().then_inc(*sV.inc())
            for case in nc.gpsimd.Switch(parity, 2):
                o = case * TH
                nc.gpsimd.wait_ge(sV.h, sV.n)
                nc.gpsimd.dma_start(out=xnhSp[k], in_=xnsl[:, o:o + TH]).then_inc(sSp.h, 16)
            sSp.n += 16
            if dbg and k == 0:
                nc.gpsimd.wait_ge(sV.h, sV.n)
                nc.gpsimd.dma_start(out=dbg_out["dxn"], in_=xnsl[:]).then_inc(*sOut.inc(16))
                nc.gpsimd.dma_start(out=dbg_out["duc"], in_=ucsl[:, 0:L]).then_inc(*sOut.inc(16))
            if dbg:
                nc.vector.wait_ge(sOut.h, sOut.n)

      if True:
        nc.gpsimd.wait_ge(sSp.h, sSp.n)
        with sb("pwTs", (128, NSLAB * MYC), BF16) as pwTs, sb("pwbe", (128, 4)) as pwbe, \
             sb("dwbf", (128, NSLAB), BF16) as dwbf, \
             sb("xhs", (128, (HPC // 2) * L), BF16) as xhs:
            nc.gpsimd.wait_ge(sW.h, sW_const)
            nc.gpsimd.dma_start(out=pwTs[:], in_=_slabs(pwT)) \
                .then_inc(*sSp.inc(16))
            nc.gpsimd.dma_start(out=dwbf[:], in_=colbuf[:, C_DWB:C_DWB + 8]).then_inc(*sSp.inc(16))
            nc.tensor.wait_ge(sSp.h, sSp.n)
            for m in range(4):
                for k in range(NSLAB):
                    mm = nc.tensor.matmul(psA[0:128, m:m + 1],
                                          pwTs[:, k * MYC + m * 128:k * MYC + (m + 1) * 128],
                                          dwbf[:, k:k + 1],
                                          start=(k == 0), stop=(k == NSLAB - 1))
                    if k == NSLAB - 1:
                        mm.then_inc(*sT.inc())
            nc.scalar.wait_ge(sT.h, sT.n)
            for m in range(4):
                nc.scalar.activation(out=pwbe[:, m:m + 1], in_=psA[0:128, m:m + 1],
                                     func=AF.Identity, bias=colbuf[:, C_PWB + m:C_PWB + m + 1])
            nc.scalar.activation(out=pwbe[:, 0:4], in_=pwbe[:, 0:4], func=AF.Copy) \
                .then_inc(*sSq.inc())
            nc.tensor.wait_ge(sV.h, sV.n)
            nc.tensor.wait_ge(sSq.h, sSq.n)
            for m in range(4):
                psm = [psA, psB, psC, psD][m]
                for nch in range(8):
                    nc.tensor.wait_ge(sEv.h, sEv.n)
                    for k in range(NSLAB):
                        mm = nc.tensor.matmul(psm[:, 0:512],
                                              pwTs[:, k * MYC + m * 128:k * MYC + (m + 1) * 128],
                                              ucsl[:, k * L + nch * 512:k * L + (nch + 1) * 512],
                                              start=(k == 0), stop=(k == NSLAB - 1))
                        if k == NSLAB - 1:
                            mm.then_inc(*sT.inc())
                    nc.scalar.wait_ge(sT.h, sT.n)
                    nc.scalar.activation(out=xhs[:, m * L + nch * 512:m * L + (nch + 1) * 512],
                                         in_=psm[:, 0:512], func=AF.Identity,
                                         bias=pwbe[:, m:m + 1]).then_inc(*sEv.inc())
            nc.gpsimd.wait_ge(sEv.h, sEv.n)
            for j in range(HPC // 2):
                nc.gpsimd.dma_start(out=xhSp[j], in_=xhs[:, j * L:(j + 1) * L]).then_inc(*sSp.inc(16))
            if dbg:
                nc.gpsimd.dma_start(out=dbg_out["dxh"], in_=xhs[:, 0:L]).then_inc(*sOut.inc(16))

            with sb("xpTs", (128, 320), BF16) as xpTs, sb("bct", (96, L), BF16) as bct, \
                 sb("dtxt", (128, L), BF16) as dtxt:
                nc.gpsimd.dma_start(out=xpTs[:], in_=xpT).then_inc(*sSp.inc(16))
                nc.tensor.wait_ge(sSp.h, sSp.n)
                for h in range(HPC):
                    sl, po = h // 2, (h % 2) * 64
                    wo = (h % 2) * 160
                    for nch in range(8):
                        nc.tensor.wait_ge(sEv.h, sEv.n)
                        nc.tensor.matmul(psC[0:64, 0:512], xpTs[:, wo:wo + 64],
                                         xhs[:, sl * L + nch * 512:sl * L + (nch + 1) * 512],
                                         start=True, stop=True)
                        nc.tensor.matmul(psD[0:96, 0:512], xpTs[:, wo + 64:wo + 160],
                                         xhs[:, sl * L + nch * 512:sl * L + (nch + 1) * 512],
                                         start=True, stop=True).then_inc(*sT.inc())
                        nc.scalar.wait_ge(sT.h, sT.n)
                        nc.scalar.copy(out=dtxt[po:po + 64, nch * 512:(nch + 1) * 512],
                                       in_=psC[0:64, 0:512])
                        nc.scalar.copy(out=bct[:, nch * 512:(nch + 1) * 512],
                                       in_=psD[0:96, 0:512]).then_inc(*sEv.inc())
                    nc.gpsimd.wait_ge(sEv.h, sEv.n)
                    nc.gpsimd.dma_start(out=bcSp[h], in_=bct[:]).then_inc(*sSp.inc(16))
                    if h % 2 == 1:
                        nc.gpsimd.dma_start(out=dtxSp[h // 2], in_=dtxt[:]).then_inc(*sSp.inc(16))
                    if dbg and h == 0:
                        nc.gpsimd.dma_start(out=dbg_out["dbc"], in_=bct[:]).then_inc(*sOut.inc(16))
                    if dbg and h == 1:
                        nc.gpsimd.dma_start(out=dbg_out["ddtx"], in_=dtxt[:]).then_inc(*sOut.inc(16))
                    if dbg:
                        nc.scalar.wait_ge(sOut.h, sOut.n)
                    nc.scalar.wait_ge(sSp.h, sSp.n)

    esF.close()
    if stage < 5:
        return finish()

    with sb("ystg", (64, L), BF16) as ystg, \
         sb("dtpTs", (128, 128), BF16) as dtpTs, \
         sb("dtx2", (128, L), BF16) as dtx2, sb("dtp", (128, L)) as dtp, \
         sb("zsp", (128, 1024)) as zsp, sb("dt2", (128, L)) as dt2, \
         sb("xh2", (128, L), BF16) as xh2, sb("w2t", (128, L)) as w2t, \
         sb("abar", (128, L)) as abar, sb("inp", (128, L)) as inpt, \
         sb("hst", (128, L)) as hst, sb("y2t", (128, L)) as y2t, \
         sb("bcl", (96, L), BF16) as bcl, sb("bcs", (36, L), BF16) as bcs, \
         sb("Bb", (128, 2 * L), BF16) as Bb, sb("Cb", (128, 2 * L), BF16) as Cb:
        nc.gpsimd.dma_start(out=dtpTs[:], in_=dtpT).then_inc(*sW.inc(16))
        bank_free = {0: 0, 1: 0}
        tb_used = {0: 0, 1: 0}
        fold_free = 0
        y2t_free = 0
        b_free = {0: 0, 1: 0}
        c_free = {0: 0, 1: 0}
        a_free = 0
        sT_pair = sT.n
        sAB_head = sAB.n
        sBC_head = sBC.n
        w2_done = 0
        for h in range(HPC):
            pr, po = h // 2, (h % 2) * 64
            if h % 2 == 0:
                nc.sync.wait_ge(sT.h, sT_pair)
                if h == 0:
                    nc.sync.wait_ge(sW.h, sW.n)
                    nc.sync.wait_ge(sSp.h, sSp.n)
                nc.sync.dma_start(out=dtx2[:], in_=dtxSp[pr]).then_inc(*sXs.inc(16))
                nc.tensor.wait_ge(sXs.h, sXs.n)
                for nch in range(8):
                    nc.tensor.wait_ge(sSq.h, sSq.n)
                    nc.tensor.matmul(psA[0:64, 0:512], dtpTs[:, 0:64],
                                     dtx2[:, nch * 512:(nch + 1) * 512],
                                     start=True, stop=True)
                    nc.tensor.matmul(psB[0:64, 0:512], dtpTs[:, 64:128],
                                     dtx2[:, nch * 512:(nch + 1) * 512],
                                     start=True, stop=True).then_inc(*sT.inc())
                    nc.scalar.wait_ge(sT.h, sT.n)
                    nc.scalar.wait_ge(sV.h, sV.n)
                    zpe = (nch % 2) * 512
                    zs = zsp[:, zpe:zpe + 512]
                    nc.scalar.activation(out=zsp[0:64, zpe:zpe + 512],
                                         in_=psA[0:64, 0:512], func=AF.Identity,
                                         bias=colbuf[0:64, C_DTB:C_DTB + 1])
                    nc.scalar.activation(out=zsp[64:128, zpe:zpe + 512],
                                         in_=psB[0:64, 0:512], func=AF.Identity,
                                         bias=colbuf[64:128, C_DTB:C_DTB + 1])
                    ab = dtp[:, nch * 512:(nch + 1) * 512]
                    nc.scalar.activation(out=ab, in_=zs, func=AF.Abs)
                    nc.scalar.activation(out=ab, in_=ab, func=AF.Exp, scale=-1.0)
                    nc.scalar.activation(out=ab, in_=ab, func=AF.Ln, bias=1.0).then_inc(*sSq.inc())
                    nc.vector.wait_ge(sSq.h, sSq.n)
                    nc.vector.scalar_tensor_tensor(out=ab, in0=zs, scalar=0.0, in1=ab,
                                                   op0=ALU.max, op1=ALU.add).then_inc(*sV.inc())
                sT_pair = sT.n
            nc.sync.wait_ge(sV.h, sV.n)
            nc.sync.wait_ge(sAB.h, sAB_head)
            nc.sync.wait_ge(sVs.h, w2_done)
            nc.sync.wait_ge(sSt[0].h, sSt[0].n)
            nc.sync.wait_ge(sSt[1].h, sSt[1].n)
            nc.sync.dma_start(out=dt2[0:64, :], in_=dtp[po:po + 64, :]).then_inc(*sXs.inc(16))
            nc.sync.dma_start(out=dt2[64:128, :], in_=dtp[po:po + 64, :]).then_inc(*sXs.inc(16))
            nc.sync.dma_start(out=xh2[0:64, :], in_=xhSp[pr][po:po + 64, :]).then_inc(*sXs.inc(16))
            nc.sync.dma_start(out=xh2[64:128, :], in_=xhSp[pr][po:po + 64, :]).then_inc(*sXs.inc(16))
            nc.sync.dma_start(out=bcl[:], in_=bcSp[h]).then_inc(*sXs.inc(16))
            sXs_tiles = sXs.n
            nc.vector.wait_ge(sXs.h, sXs_tiles)
            nc.vector.tensor_mul(w2t[:], dt2[:], xh2[:]).then_inc(*sVs.inc())
            w2_done = sVs.n
            for shi in range(NSH):
                pe = shi % 2
                base = 32 * pe
                abv = abar[:]
                nc.gpsimd.wait_ge(sXs.h, sXs_tiles)
                nc.gpsimd.wait_ge(sTB.h, tb_used[pe])
                nc.gpsimd.dma_start(out=bcs[base:base + 2, :],
                                    in_=bcl[2 * shi:2 * shi + 2, :]).then_inc(*sSt[pe].inc(16))
                nc.gpsimd.dma_start(out=bcs[base + 2:base + 4, :],
                                    in_=bcl[DS + 2 * shi:DS + 2 * shi + 2, :]).then_inc(*sSt[pe].inc(16))
                st_rdy = sSt[pe].n
                for half, (co, dst, vfree) in enumerate(
                        [(0, Bb, b_free[pe]), (128, Cb, c_free[pe])]):
                    for c in range(8):
                        ps = psC if c % 2 == 0 else psD
                        nc.tensor.wait_ge(sSt[pe].h, st_rdy)
                        nc.tensor.wait_ge(sBE.h, bank_free[c % 2])
                        nc.tensor.wait_ge(sVs.h, vfree)
                        nc.tensor.matmul(ps[0:128, 0:512],
                                         Esel[base:base + 4, co:co + 128],
                                         bcs[base:base + 4, c * 512:(c + 1) * 512],
                                         start=True, stop=True).then_inc(*sTB.inc())
                        nc.scalar.wait_ge(sTB.h, sTB.n)
                        nc.scalar.activation(
                            out=dst[:, pe * L + c * 512:pe * L + (c + 1) * 512],
                            in_=ps[0:128, 0:512], func=AF.Copy).then_inc(*sBE.inc())
                        bank_free[c % 2] = sBE.n
                    if half == 0:
                        B_rdy = sBE.n
                    else:
                        C_rdy = sBE.n
                tb_used[pe] = sTB.n
                nc.scalar.wait_ge(sXs.h, sXs_tiles)
                nc.scalar.wait_ge(sVs.h, a_free)
                nc.scalar.activation(out=abv, in_=dt2[:], func=AF.Exp,
                                     scale=ascl[:, h * NSH + shi:h * NSH + shi + 1]) \
                    .then_inc(*sAB.inc())
                A_rdy = sAB.n
                nc.vector.wait_ge(sBE.h, B_rdy)
                nc.vector.tensor_mul(inpt[:], w2t[:], Bb[:, pe * L:(pe + 1) * L]) \
                    .then_inc(*sVs.inc())
                b_free[pe] = sVs.n
                nc.vector.wait_ge(sAB.h, A_rdy)
                nc.vector.tensor_tensor_scan(out=hst[:], data0=abv, data1=inpt[:],
                                             initial=0.0, op0=ALU.mult, op1=ALU.add) \
                    .then_inc(*sVs.inc())
                a_free = sVs.n
                nc.vector.wait_ge(sVs.h, a_free)
                nc.vector.wait_ge(sBE.h, C_rdy)
                nc.vector.tensor_mul(hst[:], hst[:], Cb[:, pe * L:(pe + 1) * L]) \
                    .then_inc(*sVs.inc())
                c_free[pe] = sVs.n
                nc.vector.wait_ge(sVs.h, c_free[pe])
                if shi == 0:
                    nc.vector.wait_ge(sTB.h, y2t_free)
                    nc.vector.tensor_copy(out=y2t[:], in_=hst[:]).then_inc(*sVs.inc())
                else:
                    nc.vector.tensor_add(y2t[:], y2t[:], hst[:]).then_inc(*sVs.inc())
            for c in range(8):
                nc.tensor.wait_ge(sVs.h, sVs.n)
                nc.tensor.wait_ge(sBE.h, fold_free)
                nc.tensor.matmul(psS[0:64, 0:512], Fold[:],
                                 y2t[:, c * 512:(c + 1) * 512],
                                 start=True, stop=True).then_inc(*sTB.inc())
                nc.scalar.wait_ge(sTB.h, sTB.n)
                nc.scalar.wait_ge(sSp.h, sSp.n)
                nc.scalar.activation(out=ystg[:, c * 512:(c + 1) * 512],
                                     in_=psS[0:64, 0:512], func=AF.Copy).then_inc(*sBE.inc())
                fold_free = sBE.n
            nc.gpsimd.wait_ge(sBE.h, sBE.n)
            nc.gpsimd.dma_start(out=cin.ap()[pr][po:po + 64, :], in_=ystg[:]).then_inc(*sSp.inc(16))
            sAB_head = sAB.n
            y2t_free = sTB.n
        nc.gpsimd.wait_ge(sSp.h, sSp.n)

    if stage < 6:
        return finish()

    nc.gpsimd.wait_ge(sSp.h, sSp.n)
    nc.gpsimd.collective_compute(
        "AllGather", ALU.bypass,
        replica_groups=[[0, 1], [2, 3], [4, 5], [6, 7]],
        ins=[cin.ap().opt()], outs=[cout.ap().opt()],
    ).then_inc(*sCC.inc())

    if True:
        with sb("W1s", (128, NSLAB * HF), BF16) as W1s, sb("Wcs", (128, 32 * D), BF16) as Wcs, \
             sb("xnh", (128, NSLAB * 512), BF16) as xnh, \
             sb("h1t", (128, 32 * 512), BF16) as h1t, sb("ffo", (128, 512)) as ffo:
            nc.sync.wait_ge(sVs.h, sVs.n)
            nc.sync.wait_ge(sAB.h, sAB.n)
            nc.sync.wait_ge(sSp.h, sSp.n)
            for k in range(NSLAB):
                nc.sync.dma_start(out=W1s[:, k * HF:(k + 1) * HF], in_=WexpP[k]).then_inc(*sXs.inc(16))
            nc.sync.dma_start(out=Wcs[:], in_=_slabs(WconP)) \
                .then_inc(*sXs.inc(16))
            for st_ in range(TH // 512):
                for k in range(NSLAB):
                    nc.sync.wait_ge(sT.h, sT.n)
                    nc.sync.dma_start(out=xnh[:, k * 512:(k + 1) * 512],
                                      in_=xnhSp[k][:, st_ * 512:(st_ + 1) * 512]).then_inc(*sXs.inc(16))
                nc.tensor.wait_ge(sXs.h, sXs.n)
                nc.tensor.wait_ge(sV.h, sV.n)
                for m in range(32):
                    psm = [psA, psB][m % 2]
                    nc.tensor.wait_ge(sEv.h, sEv.n - 1 if m >= 2 else 0)
                    for k in range(NSLAB):
                        mm = nc.tensor.matmul(psm[:, 0:512],
                                              W1s[:, k * HF + m * 128:k * HF + (m + 1) * 128],
                                              xnh[:, k * 512:(k + 1) * 512],
                                              start=(k == 0), stop=(k == NSLAB - 1))
                        if k == NSLAB - 1:
                            mm.then_inc(*sT.inc())
                    nc.scalar.wait_ge(sT.h, sT.n)
                    nc.scalar.activation(out=h1t[:, m * 512:(m + 1) * 512], in_=psm[:, 0:512],
                                         func=AF.Gelu, bias=b1col[:, m:m + 1]).then_inc(*sEv.inc())
                for m in range(NSLAB):
                    nc.tensor.wait_ge(sEv.h, sEv.n)
                    nc.tensor.wait_ge(sSp.h, sSp.n)
                    for k in range(32):
                        mm = nc.tensor.matmul(psC[:, 0:512],
                                              Wcs[:, k * D + m * 128:k * D + (m + 1) * 128],
                                              h1t[:, k * 512:(k + 1) * 512],
                                              start=(k == 0), stop=(k == 31))
                        if k == 31:
                            mm.then_inc(*sT.inc())
                    nc.scalar.wait_ge(sT.h, sT.n)
                    nc.scalar.activation(out=ffo[:], in_=psC[:, 0:512], func=AF.Identity,
                                         bias=b2col[:, m:m + 1]).then_inc(*sEv.inc())
                    nc.gpsimd.wait_ge(sEv.h, sEv.n)
                    nc.gpsimd.dma_start(out=ffnSc[m][:, st_ * 512:(st_ + 1) * 512],
                                        in_=ffo[:]).then_inc(*sSp.inc(16))
            if dbg:
                nc.gpsimd.wait_ge(sSp.h, sSp.n)
                nc.gpsimd.dma_start(out=dbg_out["dffn"], in_=ffnSc[0]).then_inc(*sOut.inc(16))

        with sb("mhB", (128, TH)) as mhB, sb("rlhB", (128, TH)) as rlhB, \
             sb("sq2", (128, 512)) as sq2, sb("x1w", (128, 1024)) as x1w:
          with sb("yall", (128, NSLAB * TH), BF16) as yall, \
               sb("opTs", (128, NSLAB * D), BF16) as opTs, sb("xht", (128, TH)) as xht:
            nc.gpsimd.wait_ge(sCC.h, sCC.n)
            for case in nc.gpsimd.Switch(parity, 2):
                o = case * TH
                for slot in range(2):
                    for j in range(HPC // 2):
                        nc.gpsimd.dma_start(
                            out=yall[:, (slot * 4 + j) * TH:(slot * 4 + j + 1) * TH],
                            in_=cout.ap()[slot, j][:, o:o + TH]).then_inc(sBk.h, 16)
            sBk.n += 128
            nc.gpsimd.dma_start(out=opTs[:], in_=_slabs(opT)).then_inc(*sSp.inc(16))
            nc.gpsimd.wait_ge(sBk.h, sBk.n)
            nc.gpsimd.dma_start(out=mhB[:], in_=hrows[0:1, :].unsqueeze(1).broadcast_to((1, 128, TH))) \
                .then_inc(*sW.inc(16))
            nc.gpsimd.dma_start(out=rlhB[:], in_=hrows[32:33, :].unsqueeze(1).broadcast_to((1, 128, TH))) \
                .then_inc(*sW.inc(16))
            nc.tensor.wait_ge(sSp.h, sSp.n)
            nc.tensor.wait_ge(sBk.h, sBk.n)
            nc.vector.wait_ge(sW.h, sW.n)
            nc.sync.wait_ge(sSp.h, sSp.n)
            for m in range(NSLAB):
                nc.sync.wait_ge(sV.h, sV.n)
                nc.sync.dma_start(out=xht[:],
                                  in_=xTh[m * 128:(m + 1) * 128, :]).then_inc(*sXs.inc(16))
                for nch in range(TH // 512):
                    psm = [psA, psB][nch % 2]
                    pe = nch % 2
                    nc.tensor.wait_ge(sEv.h, sEv.n)
                    nc.tensor.wait_ge(sSp.h, sSp.n)
                    for k in range(NSLAB):
                        mm = nc.tensor.matmul(psm[:, 0:512],
                                              opTs[:, k * D + m * 128:k * D + (m + 1) * 128],
                                              yall[:, k * TH + nch * 512:k * TH + (nch + 1) * 512],
                                              start=(k == 0), stop=(k == NSLAB - 1))
                        if k == NSLAB - 1:
                            mm.then_inc(*sT.inc())
                    nc.scalar.wait_ge(sT.h, sT.n)
                    x1v = x1w[:, pe * 512:(pe + 1) * 512]
                    nc.scalar.activation(out=x1v, in_=psm[:, 0:512], func=AF.Identity,
                                         bias=colbuf[:, C_OPB + m:C_OPB + m + 1]).then_inc(*sEv.inc())
                    nc.vector.wait_ge(sEv.h, sEv.n)
                    nc.vector.wait_ge(sXs.h, sXs.n)
                    xv = xht[:, nch * 512:(nch + 1) * 512]
                    nc.vector.tensor_sub(sq2[:], xv, mhB[:, nch * 512:(nch + 1) * 512])
                    nc.vector.tensor_mul(sq2[:], sq2[:], rlhB[:, nch * 512:(nch + 1) * 512])
                    nc.vector.tensor_add(x1v, x1v, sq2[:]).then_inc(*sV.inc())
                    nc.gpsimd.wait_ge(sV.h, sV.n)
                    nc.gpsimd.dma_start(out=x1Sc[m][:, nch * 512:(nch + 1) * 512],
                                        in_=x1v).then_inc(*sSp.inc(16))
            if dbg:
                nc.gpsimd.wait_ge(sSp.h, sSp.n)
                nc.gpsimd.dma_start(out=dbg_out["dx1"], in_=x1Sc[0]).then_inc(*sOut.inc(16))
          nc.scalar.wait_ge(sV.h, sV.n)
          nc.sync.wait_ge(sSp.h, sSp.n)
          if True:
            for nch in range(TH // 512):
                for k in range(NSLAB):
                    pe2 = k % 2
                    nc.sync.wait_ge(sSq.h, sSq.n)
                    nc.sync.dma_start(out=x1w[:, pe2 * 512:(pe2 + 1) * 512],
                                      in_=x1Sc[k][:, nch * 512:(nch + 1) * 512]).then_inc(*sXs.inc(16))
                    nc.scalar.wait_ge(sXs.h, sXs.n)
                    nc.scalar.wait_ge(sT.h, sT.n) if k == 0 else None
                    nc.scalar.activation(out=sq2[:], in_=x1w[:, pe2 * 512:(pe2 + 1) * 512],
                                         func=AF.Square).then_inc(*sSq.inc())
                    nc.tensor.wait_ge(sSq.h, sSq.n)
                    mm = nc.tensor.matmul(psS[0:1, 0:512], ones[:], sq2[:],
                                          start=(k == 0), stop=(k == NSLAB - 1))
                    if k == NSLAB - 1:
                        mm.then_inc(*sT.inc())
                nc.scalar.wait_ge(sT.h, sT.n)
                nc.scalar.copy(out=hrows[64:65, nch * 512:(nch + 1) * 512],
                               in_=psS[0:1, 0:512]).then_inc(*sSq.inc())
            nc.vector.wait_ge(sSq.h, sSq.n)
            nc.vector.tensor_scalar(out=hrows[64:65, :], in0=hrows[64:65, :],
                                    scalar1=1.0 / D, scalar2=1e-6,
                                    op0=ALU.mult, op1=ALU.add).then_inc(*sV.inc())
            nc.vector.wait_ge(sV.h, sV.n)
            nc.vector.reciprocal(hrows[64:65, :], hrows[64:65, :]).then_inc(*sV.inc())
            nc.scalar.wait_ge(sV.h, sV.n)
            nc.scalar.activation(out=hrows[64:65, :], in_=hrows[64:65, :], func=AF.Sqrt) \
                .then_inc(*sSq.inc())
            nc.gpsimd.wait_ge(sSq.h, sSq.n)
            nc.gpsimd.dma_start(out=rlhB[:], in_=hrows[64:65, :].unsqueeze(1).broadcast_to((1, 128, TH))) \
                .then_inc(*sW.inc(16))

            with sb("gTs", (128, NSLAB * DI), BF16) as gTs, sb("uTs", (128, NSLAB * DI), BF16) as uTs, \
                 sb("dTs", (128, 16 * D), BF16) as dTs, sb("h2t", (128, NSLAB * 512), BF16) as h2t, \
                 sb("gsl", (128, 16 * 512), BF16) as gsl, sb("msl", (128, 16 * 512), BF16) as msl, \
                 sb("fres", (128, 2 * 512)) as fres, sb("xres", (128, 2 * 512)) as xres:
                nc.gpsimd.wait_ge(sV.h, sV.n)
                nc.gpsimd.dma_start(out=gTs[:], in_=_slabs(gT)) \
                    .then_inc(*sSp.inc(16))
                nc.gpsimd.dma_start(out=uTs[:], in_=_slabs(uT)) \
                    .then_inc(*sSp.inc(16))
                nc.gpsimd.dma_start(out=dTs[:], in_=_slabs(dTw)) \
                    .then_inc(*sSp.inc(16))
                nc.vector.wait_ge(sW.h, sW.n)
                nc.tensor.wait_ge(sSp.h, sSp.n)
                for st_ in range(TH // 512):
                    for k in range(NSLAB):
                        pe2 = k % 2
                        nc.sync.wait_ge(sV.h, sV.n)
                        nc.sync.dma_start(out=x1w[:, pe2 * 512:(pe2 + 1) * 512],
                                          in_=x1Sc[k][:, st_ * 512:(st_ + 1) * 512]) \
                            .then_inc(*sXs.inc(16))
                        nc.vector.wait_ge(sXs.h, sXs.n)
                        nc.vector.scalar_tensor_tensor(
                            out=h2t[:, k * 512:(k + 1) * 512],
                            in0=x1w[:, pe2 * 512:(pe2 + 1) * 512],
                            scalar=colbuf[:, C_W2 + k:C_W2 + k + 1],
                            in1=rlhB[:, st_ * 512:(st_ + 1) * 512],
                            op0=ALU.mult, op1=ALU.mult).then_inc(*sV.inc())
                    nc.vector.engine_nop().then_inc(*sV.inc())
                    nc.tensor.wait_ge(sV.h, sV.n)
                    for m in range(16):
                        psm = [psA, psB][m % 2]
                        nc.tensor.wait_ge(sEv.h, sEv.n - 1 if m >= 2 else 0)
                        for k in range(NSLAB):
                            mm = nc.tensor.matmul(psm[:, 0:512],
                                                  gTs[:, k * DI + m * 128:k * DI + (m + 1) * 128],
                                                  h2t[:, k * 512:(k + 1) * 512],
                                                  start=(k == 0), stop=(k == NSLAB - 1))
                            if k == NSLAB - 1:
                                mm.then_inc(*sT.inc())
                        nc.scalar.wait_ge(sT.h, sT.n)
                        nc.scalar.activation(out=gsl[:, m * 512:(m + 1) * 512],
                                             in_=psm[:, 0:512], func=AF.Silu).then_inc(*sEv.inc())
                    for m in range(16):
                        psm = [psC, psD][m % 2]
                        nc.tensor.wait_ge(sV.h, sV.n - 1 if m >= 2 else 0)
                        for k in range(NSLAB):
                            mm = nc.tensor.matmul(psm[:, 0:512],
                                                  uTs[:, k * DI + m * 128:k * DI + (m + 1) * 128],
                                                  h2t[:, k * 512:(k + 1) * 512],
                                                  start=(k == 0), stop=(k == NSLAB - 1))
                            if k == NSLAB - 1:
                                mm.then_inc(*sT.inc())
                        nc.vector.wait_ge(sT.h, sT.n)
                        nc.vector.wait_ge(sEv.h, sEv.n)
                        nc.vector.tensor_mul(msl[:, m * 512:(m + 1) * 512],
                                             psm[:, 0:512], gsl[:, m * 512:(m + 1) * 512]) \
                            .then_inc(*sV.inc())
                    for m in range(NSLAB):
                        pe = m % 2
                        nc.sync.wait_ge(sOut.h, sOut.n)
                        nc.sync.dma_start(out=fres[:, pe * 512:(pe + 1) * 512],
                                          in_=ffnSc[m][:, st_ * 512:(st_ + 1) * 512]) \
                            .then_inc(*sXs.inc(16))
                        sXs_ffn = sXs.n
                        psm = [psA, psB][m % 2]
                        nc.tensor.wait_ge(sV.h, sV.n)
                        nc.tensor.wait_ge(sEv.h, sEv.n)
                        for k in range(16):
                            mm = nc.tensor.matmul(psm[:, 0:512],
                                                  dTs[:, k * D + m * 128:k * D + (m + 1) * 128],
                                                  msl[:, k * 512:(k + 1) * 512],
                                                  start=(k == 0), stop=(k == 15))
                            if k == 15:
                                mm.then_inc(*sT.inc())
                        nc.scalar.wait_ge(sXs.h, sXs_ffn)
                        nc.scalar.wait_ge(sT.h, sT.n)
                        nc.sync.wait_ge(sV.h, sV.n)
                        nc.sync.dma_start(out=xres[:, pe * 512:(pe + 1) * 512],
                                          in_=xTh[m * 128:(m + 1) * 128, st_ * 512:(st_ + 1) * 512]) \
                            .then_inc(*sXs.inc(16))
                        nc.sync.wait_ge(sV.h, sV.n)
                        nc.sync.dma_start(out=x1w[:, pe * 512:(pe + 1) * 512],
                                          in_=x1Sc[m][:, st_ * 512:(st_ + 1) * 512]) \
                            .then_inc(*sXs.inc(16))
                        sXs_x1 = sXs.n
                        ov = fres[:, pe * 512:(pe + 1) * 512]
                        nc.vector.wait_ge(sXs.h, sXs_ffn)
                        nc.vector.wait_ge(sXs.h, sXs_x1)
                        nc.vector.wait_ge(sT.h, sT.n)
                        nc.vector.tensor_add(ov, ov, x1w[:, pe * 512:(pe + 1) * 512])
                        nc.vector.tensor_add(ov, ov, psm[:, 0:512]).then_inc(*sEv.inc())
                        nc.vector.wait_ge(sXs.h, sXs.n)
                        nc.vector.tensor_add(ov, ov, xres[:, pe * 512:(pe + 1) * 512]).then_inc(*sV.inc())
                        nc.gpsimd.wait_ge(sV.h, sV.n)
                        nc.gpsimd.dma_start(out=outT[m * 128:(m + 1) * 128, st_ * 512:(st_ + 1) * 512],
                                            in_=ov).then_inc(*sOut.inc(16))
    if probe:
        nc.gpsimd.wait_ge(sV.h, sV.n)
        nc.gpsimd.wait_ge(sSq.h, sSq.n)
        nc.gpsimd.wait_ge(sCC.h, sCC.n)
        nc.gpsimd.dma_start(out=hrowsD, in_=hrows[:]).then_inc(*sOut.inc(16))
        o1 = HF // 128
        o2 = o1 + NSLAB
        o3 = o2 + HPC * NSH
        nc.gpsimd.dma_start(out=bcolD[:, 0:o1], in_=b1col[:]).then_inc(*sOut.inc(16))
        nc.gpsimd.dma_start(out=bcolD[:, o1:o2], in_=b2col[:]).then_inc(*sOut.inc(16))
        nc.gpsimd.dma_start(out=bcolD[:, o2:o3], in_=ascl[:]).then_inc(*sOut.inc(16))
        nc.gpsimd.dma_start(out=bcolD[:, o3:o3 + 80], in_=colbuf[:]).then_inc(*sOut.inc(16))
        for j in range(HPC // 2):
            nc.gpsimd.dma_start(out=cinD[j], in_=cin.ap()[j]).then_inc(*sOut.inc(16))
            nc.gpsimd.dma_start(out=coutD[0, j], in_=cout.ap()[0, j]).then_inc(*sOut.inc(16))
            nc.gpsimd.dma_start(out=coutD[1, j], in_=cout.ap()[1, j]).then_inc(*sOut.inc(16))
    nc.gpsimd.wait_ge(sOut.h, sOut.n)
    es.close()
    return nc


_NC_CACHE = {}


def kernel(**inp):
    x = np.asarray(inp["L0_attn_x"], np.float32)
    n1w = np.asarray(inp["L0_attn_norm1_w"], np.float32)
    n2w = np.asarray(inp["L0_attn_norm2_w"], np.float32)
    dwk = np.asarray(inp["L0_attn_dw_kernel"], np.float32)
    dwb = np.asarray(inp["L0_attn_dw_bias"], np.float32)
    pww = np.asarray(inp["L0_attn_pw_w"], np.float32)
    pwb = np.asarray(inp["L0_attn_pw_b"], np.float32)
    xpw = np.asarray(inp["L0_attn_x_proj_w"], np.float32)
    dtw = np.asarray(inp["L0_attn_dt_proj_w"], np.float32)
    dtb = np.asarray(inp["L0_attn_dt_proj_b"], np.float32)
    alog = np.asarray(inp["L0_attn_A_log"], np.float32)
    opw = np.asarray(inp["L0_attn_out_proj_w"], np.float32)
    opb = np.asarray(inp["L0_attn_out_proj_b"], np.float32)
    gw = np.asarray(inp["L0_attn_gate_w"], np.float32)
    uw = np.asarray(inp["L0_attn_up_w"], np.float32)
    dw = np.asarray(inp["L0_attn_down_w"], np.float32)
    Wexp = np.asarray(inp["L0_ffn_W_exp"], np.float32)
    bexp = np.asarray(inp["L0_ffn_b_exp"], np.float32)
    upw_bf = np.asarray(inp["L0_ffn_up_weights"], np.float32)
    upb = np.asarray(inp["L0_ffn_up_bias"], np.float32)
    dnw_bf = np.asarray(inp["L0_ffn_down_weights"], np.float32)
    Wcon = np.asarray(inp["L0_ffn_W_con"], np.float32)
    bcon = np.asarray(inp["L0_ffn_b_con"], np.float32)
    dnb = np.asarray(inp["L0_ffn_down_bias"], np.float32)

    u_lowN, u_lowT, u_highN, u_highT = _stage_mats(upw_bf)
    d_lowN, d_lowT, d_highN, d_highT = _stage_mats(dnw_bf)
    ub_perm = np.ascontiguousarray(upb.reshape(64, 64).T).reshape(-1)

    import ml_dtypes
    eselH = np.zeros((36, 256), np.float32)
    for _b in (0, 32):
        eselH[_b + 0, 0:64] = 1.0
        eselH[_b + 1, 64:128] = 1.0
        eselH[_b + 2, 128:192] = 1.0
        eselH[_b + 3, 192:256] = 1.0
    foldH = np.zeros((128, 64), np.float32)
    for _m in range(64):
        foldH[_m, _m] = 1.0
        foldH[64 + _m, _m] = 1.0

    common = dict(
        EselI=eselH.astype(ml_dtypes.bfloat16), FoldI=foldH,
        w1col=_col(n1w), w2col=_col(n2w),
        kercol=np.ascontiguousarray(
            dwk[:, 0, :].reshape(NSLAB, 128, KC).transpose(1, 0, 2).reshape(128, NSLAB * KC)),
        dwbcol=_col(dwb),
        xpT=_blkdiag2(xpw.T), dtpT=_blkdiag2(dtw.T),
        dtb2=np.ascontiguousarray(np.concatenate([dtb, dtb])[:, None]),
        opT=np.ascontiguousarray(opw.T), opbc=_col(opb),
        gT=np.ascontiguousarray(gw.T), uT=np.ascontiguousarray(uw.T),
        dTw=np.ascontiguousarray(dw.T),
        EXPc=np.ascontiguousarray(np.concatenate([Wexp.T, bexp[:, None]], axis=1)),
        WCS=np.ascontiguousarray(Wcon.reshape(64, 64, D).transpose(1, 0, 2).reshape(HF, D)),
        ubcol=_col(ub_perm), bcncol=_col(bcon), dbcol=_col(dnb),
        bfu_low=u_lowN, bfu_lowT=u_lowT, bfu_high=u_highN, bfu_highT=u_highT,
        bfd_low=d_lowN, bfd_lowT=d_lowT, bfd_high=d_highN, bfd_highT=d_highT,
    )

    in_maps = []
    for c in range(8):
        b, hg = c // 2, c % 2
        xbT = np.ascontiguousarray(x[b].T)
        m = dict(common)
        m["xT"] = xbT
        m["xTh"] = np.ascontiguousarray(xbT[:, hg * TH:(hg + 1) * TH])
        m["pwT"] = np.ascontiguousarray(pww.T[:, hg * MYC:(hg + 1) * MYC])
        m["pwbc"] = _col(pwb[hg * MYC:(hg + 1) * MYC])
        m["alog8"] = np.ascontiguousarray(alog[hg * HPC:(hg + 1) * HPC])
        in_maps.append(m)

    key = ("nc_dbg" if DEBUG else "nc") + str(STAGE) + ("p" if PROBE else "")
    if key not in _NC_CACHE:
        _NC_CACHE[key] = build_nc(dbg=DEBUG, stage=STAGE, probe=PROBE)
    nc = _NC_CACHE[key]
    res = run_bass_kernel_spmd(nc, in_maps, core_ids=list(range(8)))
    out = np.empty((B, L, D), np.float32)
    for c in range(8):
        b, th = c // 2, c % 2
        out[b, th * TH:(th + 1) * TH, :] = res.results[c]["outT"].T
    if DEBUG or PROBE:
        kernel.last_results = res
    return out


# version 05
# speedup vs baseline: 2.8286x; 1.3826x over previous
import numpy as np
from contextlib import ExitStack
import concourse.bass as bass
import concourse.mybir as mybir
from concourse.bass_utils import run_bass_kernel_spmd

F32 = mybir.dt.float32
BF16 = mybir.dt.bfloat16
AF = mybir.ActivationFunctionType
ALU = mybir.AluOpType
Pool = mybir.EngineType.Pool

B, L, D = 4, 4096, 1024
H, DH, DS, DTR = 16, 64, 48, 64
KC = 4
DI = 2048
HF = 4096
HPC = 8
MYC = 512
TH = 2048
NSH = DS // 2
NSLAB = D // 128
DP1 = D + 1
DEBUG = False
PROBE = False
STAGE = 9
SER_BCAST = False
SER_ONEFLIGHT = False
SER_MLP = True


def _stage_mats(weights):
    w = np.asarray(weights, np.float32)
    lowN = np.zeros((6, 64, 64, 64), np.float32)
    highN = np.zeros((6, 64, 64, 64), np.float32)
    for s in range(12):
        s2 = 1 << s
        ws = w[s].reshape(HF // (2 * s2), s2, 2, 2)
        for blk in range(HF // (2 * s2)):
            for r in range(s2):
                for half in range(2):
                    i = blk * 2 * s2 + half * s2 + r
                    j0 = blk * 2 * s2 + r
                    j1 = j0 + s2
                    if s < 6:
                        lowN[s, i // 64, i % 64, j0 % 64] = ws[blk, r, half, 0]
                        lowN[s, i // 64, i % 64, j1 % 64] = ws[blk, r, half, 1]
                    else:
                        l = r % 64
                        highN[s - 6, l, i // 64, j0 // 64] = ws[blk, r, half, 0]
                        highN[s - 6, l, i // 64, j1 // 64] = ws[blk, r, half, 1]
    lowT = lowN.transpose(0, 1, 3, 2)
    highT = highN.transpose(0, 1, 3, 2)
    def arr(a):
        return np.ascontiguousarray(a.transpose(0, 2, 1, 3).reshape(6, 64, 64 * 64))
    return arr(lowN), arr(lowT), arr(highN), arr(highT)


def _blkdiag2(a):
    r, c = a.shape
    out = np.zeros((2 * r, 2 * c), np.float32)
    out[0:r, 0:c] = a
    out[r:2 * r, c:2 * c] = a
    return out


def _slabmajor(a):
    a = np.asarray(a, np.float32)
    K, C = a.shape
    return np.ascontiguousarray(
        a.reshape(K // 128, 128, C).transpose(1, 0, 2).reshape(128, (K // 128) * C))


def _col(v, p=128):
    v = np.asarray(v, np.float32)
    return np.ascontiguousarray(v.reshape(-1, p).T)


def _slabs(ap2d, p=128):
    rows, C = ap2d.shape
    Sn = rows // p
    return bass.AP(ap2d.tensor, ap2d.offset, [[C, p], [p * C, Sn], [1, C]])


class Ctr:
    def __init__(self, es, nc, name):
        self.h = es.enter_context(nc.semaphore(name))
        self.n = 0

    def inc(self, k=1):
        self.n += k
        return self.h, k


def build_nc(dbg=False, stage=9, probe=False):
    nc = bass.Bass("TRN2", target_bir_lowering=False, debug=False)
    es = ExitStack()

    def dram(name, shape, dt=F32, kind="Internal"):
        return nc.dram_tensor(name, list(shape), dt, kind=kind)

    def pdram(name, shape, dt=F32):
        return nc.dram_tensor(name, list(shape), dt,
                              kind="ExternalOutput" if probe else "Internal")

    xT = dram("xT", (D, L), kind="ExternalInput").ap()
    xTh = dram("xTh", (D, TH), kind="ExternalInput").ap()
    w1col = dram("w1col", (128, NSLAB), kind="ExternalInput").ap()
    w2col = dram("w2col", (128, NSLAB), kind="ExternalInput").ap()
    kercol = dram("kercol", (128, NSLAB * KC), kind="ExternalInput").ap()
    dwbcol = dram("dwbcol", (128, NSLAB), kind="ExternalInput").ap()
    pwT = dram("pwT", (128, NSLAB * MYC), kind="ExternalInput").ap()
    pwbc = dram("pwbc", (128, MYC // 128), kind="ExternalInput").ap()
    xpT = dram("xpT", (128, 320), kind="ExternalInput").ap()
    dtpT = dram("dtpT", (128, 128), kind="ExternalInput").ap()
    dtb2 = dram("dtb2", (128, 1), kind="ExternalInput").ap()
    alog8 = dram("alog8", (HPC, DS), kind="ExternalInput").ap()
    opT = dram("opT", (128, NSLAB * D), kind="ExternalInput").ap()
    opbc = dram("opbc", (128, NSLAB), kind="ExternalInput").ap()
    gT = dram("gT", (128, NSLAB * DI), kind="ExternalInput").ap()
    uT = dram("uT", (128, NSLAB * DI), kind="ExternalInput").ap()
    dTw = dram("dTw", (128, 16 * D), kind="ExternalInput").ap()
    EXPc = dram("EXPc", (HF, DP1), BF16, kind="ExternalInput").ap()
    WCS = dram("WCS", (HF, D), BF16, kind="ExternalInput").ap()
    ubcol = dram("ubcol", (128, HF // 128), kind="ExternalInput").ap()
    bcncol = dram("bcncol", (128, NSLAB), kind="ExternalInput").ap()
    dbcol = dram("dbcol", (128, NSLAB), kind="ExternalInput").ap()
    EselI = dram("EselI", (36, 256), BF16, kind="ExternalInput").ap()
    FoldI = dram("FoldI", (128, 64), kind="ExternalInput").ap()
    bstk = {}
    for nm in ["bfu_low", "bfu_lowT", "bfu_high", "bfu_highT",
               "bfd_low", "bfd_lowT", "bfd_high", "bfd_highT"]:
        bstk[nm] = dram(nm, (6, 64, 64 * 64), BF16, kind="ExternalInput").ap()

    outT = dram("outT", (D, TH), kind="ExternalOutput").ap()
    WexpP = pdram("WexpP", (NSLAB, 128, HF), BF16).ap()
    WexpB = pdram("WexpB", (1, HF)).ap()
    WconP = pdram("WconP", (32, 128, D), BF16).ap()
    bcSp = pdram("bcSp", (HPC, 96, L), BF16).ap()
    dtxSp = pdram("dtxSp", (HPC // 2, 128, L), BF16).ap()
    xhSp = pdram("xhSp", (HPC // 2, 128, L), BF16).ap()
    xnhSp = pdram("xnhSp", (NSLAB, 128, TH), BF16).ap()
    ffnSc = pdram("ffnSc", (NSLAB, 128, TH)).ap()
    x1Sc = pdram("x1Sc", (NSLAB, 128, TH)).ap()
    b1bnc = dram("b1bnc", (1, HF)).ap()
    cin = nc.dram_tensor("cin", [HPC // 2, 128, L], BF16)
    cout = nc.dram_tensor("cout", [2, HPC // 2, 128, L], BF16)
    if probe:
        hrowsD = dram("hrowsD", (128, TH), kind="ExternalOutput").ap()
        cinD = dram("cinD", (HPC // 2, 128, L), BF16, kind="ExternalOutput").ap()
        coutD = dram("coutD", (2, HPC // 2, 128, L), BF16, kind="ExternalOutput").ap()
        bcolD = dram("bcolD", (128, NSLAB + HF // 128 + HPC * NSH + 80),
                     kind="ExternalOutput").ap()
        rlbD = dram("rlbD", (128, 2 * TH), kind="ExternalOutput").ap()

    dbg_out = {}
    if dbg:
        for nm, shp in [("dxn", (128, L)), ("duc", (128, L)), ("dxh", (128, L)),
                        ("ddtx", (128, L)), ("dbc", (96, L)), ("ddt", (128, L)),
                        ("dy", (128, L)), ("dx1", (128, TH)), ("dffn", (128, TH)),
                        ("dW1", (128, HF)), ("dWc", (128, D)), ("drows", (8, L))]:
            dbg_out[nm] = dram(nm, shp, kind="ExternalOutput").ap()

    S = lambda name: Ctr(es, nc, name)
    sW, sXs, sSq, sV, sT, sEv, sSp = S("sW"), S("sXs"), S("sSq"), S("sV"), S("sT"), S("sEv"), S("sSp")
    sBC, sAB, sVs, sCC, sBk, sOut = S("sBC"), S("sAB"), S("sVs"), S("sCC"), S("sBk"), S("sOut")
    sTB, sBE = S("sTB"), S("sBE")
    sSt = [S("sSt0"), S("sSt1")]
    sF = [S("sF0"), S("sF1")]

    sb = lambda name, shape, dt=F32: nc.sbuf_tensor(name, list(shape), dt)

    ones = es.enter_context(sb("ones", (128, 1)))
    hrows = es.enter_context(sb("hrows", (128, TH)))
    colbuf = es.enter_context(sb("colbuf", (128, 80)))
    b1col = es.enter_context(sb("b1col", (128, HF // 128)))
    b2col = es.enter_context(sb("b2col", (128, NSLAB)))
    ascl = es.enter_context(sb("ascl", (128, HPC * NSH)))
    bst = es.enter_context(sb("bst", (1, 128)))
    R_M, R_RLN, R_GU, R_T0, R_T1, R_T2 = 0, 1, 2, 3, 4, 5
    rows_ref = {}
    ROWMAP = {0: (0, 0), 1: (64, 0), 2: (0, 2), 3: (32, 0),
              4: (32, 1), 5: (64, 1), 6: (0, 1), 7: (32, 2)}

    def ROW(r, c0=0, c1=L):
        p, c = ROWMAP[r]
        return rows_ref["t"][p:p + 1, c * L + c0:c * L + c1]
    C_W1, C_W2, C_KER, C_DWB, C_PWB, C_OPB, C_DTB = 0, 8, 16, 48, 56, 60, 68

    nc.vector.memset(ones[:], 1.0)
    onesR = es.enter_context(sb("onesR", (65, 128)))
    nc.vector.memset(onesR[0:1, :], 1.0)
    nc.vector.memset(onesR[32:33, :], 1.0)
    nc.vector.memset(onesR[64:65, :], 1.0)
    Fold = es.enter_context(sb("Fold", (128, 64)))
    Esel = es.enter_context(sb("Esel", (36, 256), BF16))
    nc.sync.dma_start(out=Fold[:], in_=FoldI).then_inc(*sW.inc(16))
    nc.sync.dma_start(out=Esel[:], in_=EselI).then_inc(*sW.inc(16))

    nc.sync.dma_start(out=colbuf[:, C_W1:C_W1 + 8], in_=w1col).then_inc(*sW.inc(16))
    nc.sync.dma_start(out=colbuf[:, C_W2:C_W2 + 8], in_=w2col).then_inc(*sW.inc(16))
    nc.sync.dma_start(out=colbuf[:, C_KER:C_KER + 32], in_=kercol).then_inc(*sW.inc(16))
    nc.sync.dma_start(out=colbuf[:, C_DWB:C_DWB + 8], in_=dwbcol).then_inc(*sW.inc(16))
    nc.sync.dma_start(out=colbuf[:, C_PWB:C_PWB + 4], in_=pwbc).then_inc(*sW.inc(16))
    nc.sync.dma_start(out=colbuf[:, C_OPB:C_OPB + 8], in_=opbc).then_inc(*sW.inc(16))
    nc.sync.dma_start(out=colbuf[:, C_DTB:C_DTB + 1], in_=dtb2).then_inc(*sW.inc(16))
    sW_const = sW.n

    atmp = es.enter_context(sb("atmp", (HPC, DS)))
    nc.sync.dma_start(out=atmp[:], in_=alog8).then_inc(*sW.inc(16))
    nc.scalar.wait_ge(sW.h, sW.n)
    nc.scalar.activation(out=atmp[:], in_=atmp[:], func=AF.Exp).then_inc(*sSq.inc())
    nc.vector.wait_ge(sSq.h, sSq.n)
    nc.vector.tensor_scalar_mul(atmp[:], atmp[:], -1.0).then_inc(*sV.inc())
    aeo = es.enter_context(sb("aeo", (HPC, DS)))
    nc.vector.wait_ge(sV.h, sV.n)
    nc.vector.tensor_copy(out=aeo[:, 0:NSH], in_=atmp[:, 0:DS:2]).then_inc(*sV.inc())
    nc.vector.wait_ge(sV.h, sV.n)
    nc.vector.tensor_copy(out=aeo[:, NSH:DS], in_=atmp[:, 1:DS:2]).then_inc(*sV.inc())
    nc.gpsimd.wait_ge(sV.h, sV.n)
    for h in range(HPC):
        src0 = aeo[h:h + 1, 0:NSH].unsqueeze(1).broadcast_to((1, 64, NSH))
        src1 = aeo[h:h + 1, NSH:DS].unsqueeze(1).broadcast_to((1, 64, NSH))
        nc.gpsimd.dma_start(out=ascl[0:64, h * NSH:(h + 1) * NSH], in_=src0).then_inc(*sW.inc(16))
        nc.gpsimd.dma_start(out=ascl[64:128, h * NSH:(h + 1) * NSH], in_=src1).then_inc(*sW.inc(16))

    psA = es.enter_context(nc.psum_tensor("psA", [128, 512], F32))
    psB = es.enter_context(nc.psum_tensor("psB", [128, 512], F32))
    psC = es.enter_context(nc.psum_tensor("psC", [128, 512], F32))
    psD = es.enter_context(nc.psum_tensor("psD", [128, 512], F32))
    psS = es.enter_context(nc.psum_tensor("psS", [64, 512], F32))

    bb_free = {0: 0, 1: 0}

    def bcast_row(src, base, dst, width):
        for ci in range(width // 512):
            ps = psA if ci % 2 == 0 else psB
            nc.tensor.wait_ge(sBE.h, bb_free[ci % 2])
            nc.tensor.matmul(ps[0:128, 0:512], onesR[base:base + 1, :],
                             src[:, ci * 512:(ci + 1) * 512],
                             start=True, stop=True).then_inc(*sTB.inc())
            nc.scalar.wait_ge(sTB.h, sTB.n)
            nc.scalar.copy(out=dst[:, ci * 512:(ci + 1) * 512],
                           in_=ps[0:128, 0:512]).then_inc(*sBE.inc())
            bb_free[ci % 2] = sBE.n

    pid = nc.partition_id([Pool])
    parity = nc.gpsimd.compute_val(pid % 2)

    ev_bank = {0: 0, 1: 0}
    evC_bank = {0: 0, 1: 0}

    def compose(dst, stg2, Rb, step_stack, steps, init_stack, init_idx, ldst):
        nc.sync.wait_ge(sT.h, ldst["rb"])
        nc.sync.dma_start(out=Rb[0][:], in_=init_stack[init_idx]).then_inc(*sXs.inc(16))
        ld_init = sXs.n
        cur = 0
        for si, sidx in enumerate(steps):
            pe = si % 2
            nc.sync.wait_ge(sT.h, ldst["stg"][pe])
            nc.sync.dma_start(out=stg2[pe][:], in_=step_stack[sidx]).then_inc(*sXs.inc(16))
            ld = sXs.n
            obuf = dst if si == len(steps) - 1 else Rb[1 - cur]
            cbuf = Rb[cur]
            nc.tensor.wait_ge(sXs.h, ld if si else max(ld, ld_init))
            for grp in range(8):
                ps = psA if grp % 2 == 0 else psB
                nc.tensor.wait_ge(sEv.h, ev_bank[grp % 2])
                for jb in range(8):
                    blk = grp * 8 + jb
                    mm = nc.tensor.matmul(ps[0:64, jb * 64:(jb + 1) * 64],
                                          stg2[pe][:, blk * 64:(blk + 1) * 64],
                                          cbuf[:, blk * 64:(blk + 1) * 64],
                                          start=True, stop=True)
                    if jb == 7:
                        mm.then_inc(*sT.inc())
                nc.scalar.wait_ge(sT.h, sT.n)
                nc.scalar.copy(out=obuf[0:64, grp * 512:(grp + 1) * 512],
                               in_=ps[0:64, 0:512]).then_inc(*sEv.inc())
                ev_bank[grp % 2] = sEv.n
            ldst["stg"][pe] = sT.n
            cur = 1 - cur
        ldst["rb"] = sT.n

    with sb("stg0", (64, HF), BF16) as stgA, sb("stg1", (64, HF), BF16) as stgB, \
         sb("Rb0", (64, HF), BF16) as Rb0, sb("Rb1", (64, HF), BF16) as Rb1, \
         sb("W1sb", (128, NSLAB * HF), BF16) as W1sb:
        stg2, Rb = [stgA, stgB], [Rb0, Rb1]
        ldst = {"stg": {0: 0, 1: 0}, "rb": 0}
        with sb("DupT", (64, HF), BF16) as DupT, sb("HupT", (64, HF), BF16) as HupT:
            compose(DupT, stg2, Rb, bstk["bfu_low"], [4, 3, 2, 1, 0], bstk["bfu_lowT"], 5, ldst)
            compose(HupT, stg2, Rb, bstk["bfu_high"], [4, 3, 2, 1, 0], bstk["bfu_highT"], 5, ldst)

            with sb("V2", (128, 32 * DP1), BF16) as V2, sb("Eb2", (64, 2 * DP1), BF16) as Eb2, \
                 sb("Vsh0", (64, DP1), BF16) as Vsh0, sb("Vsh1", (64, DP1), BF16) as Vsh1:
                eb_free = {0: 0, 1: 0}
                gpar = 0
                for ih in range(64):
                    pe = ih % 2
                    nc.sync.wait_ge(sT.h, eb_free[pe])
                    nc.sync.dma_start(out=Eb2[:, pe * DP1:(pe + 1) * DP1],
                                      in_=EXPc[ih * 64:(ih + 1) * 64, :]).then_inc(*sXs.inc(16))
                    ld = sXs.n
                    for nch in range(3):
                        n0 = nch * 512
                        n1 = min(n0 + 512, DP1)
                        ps = psA if gpar % 2 == 0 else psB
                        nc.tensor.wait_ge(sXs.h, ld)
                        nc.tensor.wait_ge(sEv.h, ev_bank[gpar % 2])
                        nc.tensor.matmul(ps[0:64, 0:n1 - n0],
                                         DupT[:, ih * 64:(ih + 1) * 64],
                                         Eb2[:, pe * DP1 + n0:pe * DP1 + n1],
                                         start=True, stop=True).then_inc(*sT.inc())
                        nc.scalar.wait_ge(sT.h, sT.n)
                        nc.scalar.copy(out=V2[(ih % 2) * 64:(ih % 2) * 64 + 64,
                                              (ih // 2) * DP1 + n0:(ih // 2) * DP1 + n1],
                                       in_=ps[0:64, 0:n1 - n0]).then_inc(*sEv.inc())
                        ev_bank[gpar % 2] = sEv.n
                        gpar += 1
                    eb_free[pe] = sT.n
                ev_V2 = sEv.n
                bst_free = {0: 0, 1: 0}
                vsh_free = {0: 0, 1: 0}
                Vsh2 = [Vsh0, Vsh1]
                gparC = 0
                for jl in range(64):
                    pe = jl % 2
                    nc.gpsimd.wait_ge(sEv.h, ev_V2)
                    nc.gpsimd.wait_ge(sT.h, vsh_free[pe])
                    srcA = V2[jl:jl + 1, :].rearrange("p (g c) -> p g c", g=32)
                    srcB = V2[64 + jl:64 + jl + 1, :].rearrange("p (g c) -> p g c", g=32)
                    dstA = bass.AP(Vsh2[pe][:].tensor, 0, [[2 * DP1, 32], [1, DP1]])
                    dstB = bass.AP(Vsh2[pe][:].tensor, DP1, [[2 * DP1, 32], [1, DP1]])
                    nc.gpsimd.dma_start(out=dstA, in_=srcA).then_inc(*sSt[pe].inc(16))
                    nc.gpsimd.dma_start(out=dstB, in_=srcB).then_inc(*sSt[pe].inc(16))
                    sh_rdy = sSt[pe].n
                    for dch in range(9):
                        d0 = dch * 128
                        dl = 128 if dch < 8 else 1
                        ps = psC if gparC % 2 == 0 else psD
                        nc.tensor.wait_ge(sSt[pe].h, sh_rdy)
                        nc.tensor.wait_ge(sEv.h, evC_bank[gparC % 2])
                        nc.tensor.matmul(ps[0:dl, 0:64],
                                         Vsh2[pe][:, d0:d0 + dl],
                                         HupT[:, jl * 64:(jl + 1) * 64],
                                         start=True, stop=True).then_inc(*sT.inc())
                        nc.scalar.wait_ge(sT.h, sT.n)
                        if dch < 8:
                            nc.scalar.copy(out=W1sb[:, dch * HF + jl:dch * HF + HF:64],
                                           in_=ps[0:128, 0:64]).then_inc(*sEv.inc())
                        else:
                            nc.scalar.wait_ge(sF[pe].h, bst_free[pe])
                            nc.scalar.copy(out=bst[0:1, pe * 64:(pe + 1) * 64],
                                           in_=ps[0:1, 0:64]).then_inc(*sEv.inc())
                            nc.gpsimd.wait_ge(sEv.h, sEv.n)
                            nc.gpsimd.dma_start(out=WexpB[0:1, jl * 64:(jl + 1) * 64],
                                                in_=bst[0:1, pe * 64:(pe + 1) * 64]) \
                                .then_inc(*sF[pe].inc(16))
                            bst_free[pe] = sF[pe].n
                        evC_bank[gparC % 2] = sEv.n
                        gparC += 1
                    vsh_free[pe] = sT.n
                nc.gpsimd.wait_ge(sEv.h, sEv.n)
                for k in range(NSLAB):
                    nc.gpsimd.dma_start(out=WexpP[k], in_=W1sb[:, k * HF:(k + 1) * HF]) \
                        .then_inc(*sSp.inc(16))

        fence = sT.n
        ldst["rb"] = fence
        ldst["stg"] = {0: fence, 1: fence}
        with sb("Hdn", (64, HF), BF16) as Hdn, sb("Ddn", (64, HF), BF16) as Ddn:
            compose(Hdn, stg2, Rb, bstk["bfd_highT"], [1, 2, 3, 4, 5], bstk["bfd_high"], 0, ldst)
            compose(Ddn, stg2, Rb, bstk["bfd_lowT"], [1, 2, 3, 4, 5], bstk["bfd_low"], 0, ldst)
            with sb("U2", (128, 32 * D), BF16) as U2, sb("Wcb2", (64, 2 * D), BF16) as Wcb2, \
                 sb("Ush0", (64, D), BF16) as Ush0, sb("Ush1", (64, D), BF16) as Ush1, \
                 sb("Wcsb", (128, 2 * D), BF16) as Wcsb:
                wb_free = {0: fence, 1: fence}
                gpar = 0
                for jl in range(64):
                    pe = jl % 2
                    nc.sync.wait_ge(sT.h, wb_free[pe])
                    nc.sync.dma_start(out=Wcb2[:, pe * D:(pe + 1) * D],
                                      in_=WCS[jl * 64:(jl + 1) * 64, :]).then_inc(*sXs.inc(16))
                    ld = sXs.n
                    for nch in range(2):
                        n0 = nch * 512
                        ps = psA if gpar % 2 == 0 else psB
                        nc.tensor.wait_ge(sXs.h, ld)
                        nc.tensor.wait_ge(sEv.h, ev_bank[gpar % 2])
                        nc.tensor.matmul(ps[0:64, 0:512],
                                         Hdn[:, jl * 64:(jl + 1) * 64],
                                         Wcb2[:, pe * D + n0:pe * D + n0 + 512],
                                         start=True, stop=True).then_inc(*sT.inc())
                        nc.scalar.wait_ge(sT.h, sT.n)
                        nc.scalar.copy(out=U2[(jl % 2) * 64:(jl % 2) * 64 + 64,
                                              (jl // 2) * D + n0:(jl // 2) * D + n0 + 512],
                                       in_=ps[0:64, 0:512]).then_inc(*sEv.inc())
                        ev_bank[gpar % 2] = sEv.n
                        gpar += 1
                    wb_free[pe] = sT.n
                ev_U2 = sEv.n
                ush_free = {0: 0, 1: 0}
                Ush2 = [Ush0, Ush1]
                gparC = 0
                wcs_free = {0: 0, 1: 0}
                for ih in range(64):
                    pe = ih % 2
                    pe2 = (ih // 2) % 2
                    nc.gpsimd.wait_ge(sEv.h, ev_U2)
                    nc.gpsimd.wait_ge(sT.h, ush_free[pe])
                    srcA = U2[ih:ih + 1, :].rearrange("p (g c) -> p g c", g=32)
                    srcB = U2[64 + ih:64 + ih + 1, :].rearrange("p (g c) -> p g c", g=32)
                    dstA = bass.AP(Ush2[pe][:].tensor, 0, [[2 * D, 32], [1, D]])
                    dstB = bass.AP(Ush2[pe][:].tensor, D, [[2 * D, 32], [1, D]])
                    nc.gpsimd.dma_start(out=dstA, in_=srcA).then_inc(*sSt[pe].inc(16))
                    nc.gpsimd.dma_start(out=dstB, in_=srcB).then_inc(*sSt[pe].inc(16))
                    sh_rdy = sSt[pe].n
                    for nch in range(2):
                        n0 = nch * 512
                        ps = psC if gparC % 2 == 0 else psD
                        nc.tensor.wait_ge(sSt[pe].h, sh_rdy)
                        nc.tensor.wait_ge(sEv.h, evC_bank[gparC % 2])
                        nc.tensor.matmul(ps[0:64, 0:512],
                                         Ddn[:, ih * 64:(ih + 1) * 64],
                                         Ush2[pe][:, n0:n0 + 512],
                                         start=True, stop=True).then_inc(*sT.inc())
                        nc.scalar.wait_ge(sT.h, sT.n)
                        if nch == 0:
                            nc.scalar.wait_ge(sF[pe2].h, wcs_free[pe2])
                        nc.scalar.copy(out=Wcsb[(ih % 2) * 64:(ih % 2) * 64 + 64,
                                                pe2 * D + n0:pe2 * D + n0 + 512],
                                       in_=ps[0:64, 0:512]).then_inc(*sEv.inc())
                        evC_bank[gparC % 2] = sEv.n
                        gparC += 1
                    ush_free[pe] = sT.n
                    if ih % 2 == 1:
                        nc.gpsimd.wait_ge(sEv.h, sEv.n)
                        nc.gpsimd.dma_start(out=WconP[ih // 2],
                                            in_=Wcsb[:, pe2 * D:(pe2 + 1) * D]) \
                            .then_inc(*sF[pe2].inc(16))
                        wcs_free[pe2] = sF[pe2].n

    nc.sync.wait_ge(sSp.h, sSp.n)
    nc.sync.wait_ge(sF[0].h, sF[0].n)
    nc.sync.wait_ge(sF[1].h, sF[1].n)
    nc.sync.dma_start(out=b1bnc, in_=WexpB).then_inc(*sW.inc(16))
    nc.sync.wait_ge(sW.h, sW.n)
    nc.sync.dma_start(out=b1col[0:64, :], in_=bass.AP(b1bnc.tensor, 0, [[64, 64], [2, HF // 128]]),
                      allow_slow_non_contiguous=True).then_inc(*sW.inc(16))
    nc.sync.dma_start(out=b1col[64:128, :], in_=bass.AP(b1bnc.tensor, 1, [[64, 64], [2, HF // 128]]),
                      allow_slow_non_contiguous=True).then_inc(*sW.inc(16))
    nc.sync.dma_start(out=b2col[:], in_=bcncol).then_inc(*sW.inc(16))
    with sb("tmpc", (128, NSLAB + HF // 128)) as tmpc:
        nc.sync.dma_start(out=tmpc[:, 0:NSLAB], in_=dbcol).then_inc(*sW.inc(16))
        nc.sync.dma_start(out=tmpc[:, NSLAB:], in_=ubcol).then_inc(*sW.inc(16))
        nc.vector.wait_ge(sW.h, sW.n)
        nc.vector.tensor_add(b2col[:], b2col[:], tmpc[:, 0:NSLAB])
        nc.vector.tensor_add(b1col[:], b1col[:], tmpc[:, NSLAB:]).then_inc(*sV.inc())

    if dbg:
        nc.gpsimd.wait_ge(sSp.h, sSp.n)
        nc.gpsimd.dma_start(out=dbg_out["dW1"], in_=WexpP[0]).then_inc(*sOut.inc(16))
        nc.gpsimd.dma_start(out=dbg_out["dWc"], in_=WconP[0:128, :]).then_inc(*sOut.inc(16))

    def finish():
        nc.gpsimd.wait_ge(sOut.h, sOut.n)
        es.close()
        return nc

    if stage < 1:
        return finish()

    esF = ExitStack()
    rows_ref["t"] = esF.enter_context(sb("rows", (128, 3 * L)))
    with sb("xsl", (128, 2 * L)) as xsl, sb("sqt", (128, L)) as sqt:
        for k in range(NSLAB):
            pe = k % 2
            nc.sync.wait_ge(sSq.h, sSq.n)
            nc.sync.wait_ge(sT.h, sT.n)
            nc.sync.dma_start(out=xsl[:, pe * L:(pe + 1) * L],
                              in_=xT[k * 128:(k + 1) * 128, :]).then_inc(*sXs.inc(16))
            nc.scalar.wait_ge(sXs.h, sXs.n)
            nc.scalar.activation(out=sqt[:], in_=xsl[:, pe * L:(pe + 1) * L],
                                 func=AF.Square).then_inc(*sSq.inc())
            nc.tensor.wait_ge(sSq.h, sSq.n)
            for nch in range(8):
                nc.tensor.wait_ge(sEv.h, sEv.n)
                nc.tensor.matmul(psS[0:1, :], ones[:],
                                 xsl[:, pe * L + nch * 512:pe * L + (nch + 1) * 512],
                                 start=True, stop=True)
                nc.tensor.matmul(psA[0:1, :], ones[:],
                                 sqt[:, nch * 512:(nch + 1) * 512],
                                 start=True, stop=True).then_inc(*sT.inc())
                nc.scalar.wait_ge(sT.h, sT.n)
                nc.scalar.copy(out=ROW(6, nch * 512, (nch + 1) * 512), in_=psS[0:1, :])
                nc.scalar.copy(out=ROW(7, nch * 512, (nch + 1) * 512),
                               in_=psA[0:1, :]).then_inc(*sEv.inc())
            nc.vector.wait_ge(sEv.h, sEv.n)
            if k == 0:
                nc.vector.tensor_scalar_mul(ROW(R_M), ROW(6), 1.0 / D)
                nc.vector.tensor_scalar_mul(ROW(R_T0), ROW(7), 1.0 / D)
            else:
                nc.vector.scalar_tensor_tensor(out=ROW(R_M), in0=ROW(6),
                                               scalar=1.0 / D, in1=ROW(R_M),
                                               op0=ALU.mult, op1=ALU.add)
                nc.vector.scalar_tensor_tensor(out=ROW(R_T0), in0=ROW(7),
                                               scalar=1.0 / D, in1=ROW(R_T0),
                                               op0=ALU.mult, op1=ALU.add)
            nc.vector.engine_nop().then_inc(*sV.inc())
            nc.sync.wait_ge(sV.h, sV.n)
        nc.vector.wait_ge(sV.h, sV.n)
        for opf in [
            lambda: nc.vector.tensor_mul(ROW(R_T1), ROW(R_M), ROW(R_M)),
            lambda: nc.vector.tensor_sub(ROW(R_T1), ROW(R_T0), ROW(R_T1)),
            lambda: nc.vector.tensor_scalar_add(ROW(R_T2), ROW(R_T1), 1e-5),
            lambda: nc.vector.reciprocal(ROW(R_T0), ROW(R_T2)),
        ]:
            opf().then_inc(*sV.inc())
            nc.vector.wait_ge(sV.h, sV.n)
        nc.scalar.wait_ge(sV.h, sV.n)
        nc.scalar.activation(out=ROW(R_RLN), in_=ROW(R_T0),
                             func=AF.Sqrt).then_inc(*sSq.inc())
        nc.vector.wait_ge(sSq.h, sSq.n)
        for opf in [
            lambda: nc.vector.tensor_mul(ROW(R_T2), ROW(R_T1), ROW(R_T0)),
            lambda: nc.vector.tensor_scalar_add(ROW(R_T2), ROW(R_T2), 1e-6),
            lambda: nc.vector.reciprocal(ROW(R_T2), ROW(R_T2)),
        ]:
            opf().then_inc(*sV.inc())
            nc.vector.wait_ge(sV.h, sV.n)
        nc.vector.engine_nop().then_inc(*sV.inc())
        nc.scalar.wait_ge(sV.h, sV.n)
        nc.scalar.activation(out=ROW(R_T2), in_=ROW(R_T2),
                             func=AF.Sqrt).then_inc(*sSq.inc())
        nc.vector.wait_ge(sSq.h, sSq.n)
        nc.vector.tensor_mul(ROW(R_GU), ROW(R_RLN),
                             ROW(R_T2)).then_inc(*sV.inc())
    if dbg:
        nc.gpsimd.wait_ge(sV.h, sV.n)
        nc.gpsimd.dma_start(out=dbg_out["drows"][0:1, :], in_=ROW(R_M)).then_inc(*sOut.inc(16))
        nc.gpsimd.dma_start(out=dbg_out["drows"][1:2, :], in_=ROW(R_RLN)).then_inc(*sOut.inc(16))
        nc.gpsimd.dma_start(out=dbg_out["drows"][2:3, :], in_=ROW(R_GU)).then_inc(*sOut.inc(16))

    if stage < 2:
        esF.close()
        return finish()
    nc.gpsimd.wait_ge(sV.h, sV.n)
    for case in nc.gpsimd.Switch(parity, 2):
        o = case * TH
        nc.gpsimd.dma_start(out=hrows[0:1, :], in_=ROW(R_M, o, o + TH)).then_inc(sBk.h, 16)
        nc.gpsimd.dma_start(out=hrows[32:33, :], in_=ROW(R_RLN, o, o + TH)).then_inc(sBk.h, 16)
    sBk.n += 32

    if stage < 3:
        esF.close()
        return finish()

    with sb("ucsl", (128, NSLAB * L), BF16) as ucsl:
      with sb("mB", (128, L), BF16) as mB, sb("rlnB", (128, L), BF16) as rlnB, sb("guB", (128, L), BF16) as guB, \
         sb("xsl2", (128, L)) as xsl2, sb("t1t", (128, L)) as t1t, \
         sb("xnsl", (128, L), BF16) as xnsl, sb("upad", (128, L + KC - 1)) as upad:
        nc.sync.wait_ge(sV.h, sV.n)
        nc.tensor.wait_ge(sV.h, sV.n)
        nc.tensor.wait_ge(sEv.h, sEv.n)
        for rsrc, rdst in [(R_M, mB), (R_RLN, rlnB), (R_GU, guB)]:
            bcast_row(ROW(rsrc), ROWMAP[rsrc][0], rdst, L)
        nc.tensor.wait_ge(sBE.h, sBE.n)
        nc.vector.wait_ge(sBE.h, sBE.n)
        nc.vector.memset(upad[:, 0:KC - 1], 0.0)
        for k in range(NSLAB):
            pe = k % 2
            nc.sync.wait_ge(sV.h, sV.n)
            nc.sync.dma_start(out=xsl2[:],
                              in_=xT[k * 128:(k + 1) * 128, :]).then_inc(*sXs.inc(16))
            nc.vector.wait_ge(sXs.h, sXs.n)
            nc.vector.wait_ge(sSp.h, sSp.n)
            xs = xsl2[:]
            nc.vector.tensor_sub(t1t[:], xs, mB[:])
            nc.vector.tensor_mul(xnsl[:], t1t[:], rlnB[:])
            nc.vector.scalar_tensor_tensor(out=upad[:, KC - 1:], in0=t1t[:],
                                           scalar=colbuf[:, C_W1 + k:C_W1 + k + 1], in1=guB[:],
                                           op0=ALU.mult, op1=ALU.mult)
            nc.vector.tensor_scalar_mul(t1t[:], upad[:, 0:L],
                                        colbuf[:, C_KER + k * 4:C_KER + k * 4 + 1])
            for kk in range(1, KC):
                dst = ucsl[:, k * L:(k + 1) * L] if kk == KC - 1 else t1t[:]
                nc.vector.scalar_tensor_tensor(out=dst, in0=upad[:, kk:kk + L],
                                               scalar=colbuf[:, C_KER + k * 4 + kk:C_KER + k * 4 + kk + 1],
                                               in1=t1t[:], op0=ALU.mult, op1=ALU.add)
            nc.vector.engine_nop().then_inc(*sV.inc())
            for case in nc.gpsimd.Switch(parity, 2):
                o = case * TH
                nc.gpsimd.wait_ge(sV.h, sV.n)
                nc.gpsimd.dma_start(out=xnhSp[k], in_=xnsl[:, o:o + TH]).then_inc(sSp.h, 16)
            sSp.n += 16
            if dbg and k == 0:
                nc.gpsimd.wait_ge(sV.h, sV.n)
                nc.gpsimd.dma_start(out=dbg_out["dxn"], in_=xnsl[:]).then_inc(*sOut.inc(16))
                nc.gpsimd.dma_start(out=dbg_out["duc"], in_=ucsl[:, 0:L]).then_inc(*sOut.inc(16))
            if dbg:
                nc.vector.wait_ge(sOut.h, sOut.n)

      if True:
        nc.gpsimd.wait_ge(sSp.h, sSp.n)
        with sb("pwTs", (128, NSLAB * MYC), BF16) as pwTs, sb("pwbe", (128, 4)) as pwbe, \
             sb("dwbf", (128, NSLAB), BF16) as dwbf, \
             sb("xhs", (128, (HPC // 2) * L), BF16) as xhs:
            nc.gpsimd.wait_ge(sW.h, sW_const)
            nc.gpsimd.dma_start(out=pwTs[:], in_=pwT).then_inc(*sSp.inc(16))
            nc.gpsimd.dma_start(out=dwbf[:], in_=colbuf[:, C_DWB:C_DWB + 8]).then_inc(*sSp.inc(16))
            nc.tensor.wait_ge(sSp.h, sSp.n)
            for m in range(4):
                for k in range(NSLAB):
                    mm = nc.tensor.matmul(psA[0:128, m:m + 1],
                                          pwTs[:, k * MYC + m * 128:k * MYC + (m + 1) * 128],
                                          dwbf[:, k:k + 1],
                                          start=(k == 0), stop=(k == NSLAB - 1))
                    if k == NSLAB - 1:
                        mm.then_inc(*sT.inc())
            nc.scalar.wait_ge(sT.h, sT.n)
            for m in range(4):
                nc.scalar.activation(out=pwbe[:, m:m + 1], in_=psA[0:128, m:m + 1],
                                     func=AF.Identity, bias=colbuf[:, C_PWB + m:C_PWB + m + 1])
            nc.scalar.activation(out=pwbe[:, 0:4], in_=pwbe[:, 0:4], func=AF.Copy) \
                .then_inc(*sSq.inc())
            nc.tensor.wait_ge(sV.h, sV.n)
            nc.tensor.wait_ge(sSq.h, sSq.n)
            for m in range(4):
                psm = [psA, psB, psC, psD][m]
                for nch in range(8):
                    nc.tensor.wait_ge(sEv.h, sEv.n)
                    for k in range(NSLAB):
                        mm = nc.tensor.matmul(psm[:, 0:512],
                                              pwTs[:, k * MYC + m * 128:k * MYC + (m + 1) * 128],
                                              ucsl[:, k * L + nch * 512:k * L + (nch + 1) * 512],
                                              start=(k == 0), stop=(k == NSLAB - 1))
                        if k == NSLAB - 1:
                            mm.then_inc(*sT.inc())
                    nc.scalar.wait_ge(sT.h, sT.n)
                    nc.scalar.activation(out=xhs[:, m * L + nch * 512:m * L + (nch + 1) * 512],
                                         in_=psm[:, 0:512], func=AF.Identity,
                                         bias=pwbe[:, m:m + 1]).then_inc(*sEv.inc())
            nc.gpsimd.wait_ge(sEv.h, sEv.n)
            for j in range(HPC // 2):
                nc.gpsimd.dma_start(out=xhSp[j], in_=xhs[:, j * L:(j + 1) * L]).then_inc(*sSp.inc(16))
            if dbg:
                nc.gpsimd.dma_start(out=dbg_out["dxh"], in_=xhs[:, 0:L]).then_inc(*sOut.inc(16))

            with sb("xpTs", (128, 320), BF16) as xpTs, sb("bct", (96, L), BF16) as bct, \
                 sb("dtxt", (128, L), BF16) as dtxt:
                nc.gpsimd.dma_start(out=xpTs[:], in_=xpT).then_inc(*sSp.inc(16))
                nc.tensor.wait_ge(sSp.h, sSp.n)
                for h in range(HPC):
                    sl, po = h // 2, (h % 2) * 64
                    wo = (h % 2) * 160
                    for nch in range(8):
                        nc.tensor.wait_ge(sEv.h, sEv.n)
                        nc.tensor.matmul(psC[0:64, 0:512], xpTs[:, wo:wo + 64],
                                         xhs[:, sl * L + nch * 512:sl * L + (nch + 1) * 512],
                                         start=True, stop=True)
                        nc.tensor.matmul(psD[0:96, 0:512], xpTs[:, wo + 64:wo + 160],
                                         xhs[:, sl * L + nch * 512:sl * L + (nch + 1) * 512],
                                         start=True, stop=True).then_inc(*sT.inc())
                        nc.scalar.wait_ge(sT.h, sT.n)
                        nc.scalar.copy(out=dtxt[po:po + 64, nch * 512:(nch + 1) * 512],
                                       in_=psC[0:64, 0:512])
                        nc.scalar.copy(out=bct[:, nch * 512:(nch + 1) * 512],
                                       in_=psD[0:96, 0:512]).then_inc(*sEv.inc())
                    nc.gpsimd.wait_ge(sEv.h, sEv.n)
                    nc.gpsimd.dma_start(out=bcSp[h], in_=bct[:]).then_inc(*sSp.inc(16))
                    if h % 2 == 1:
                        nc.gpsimd.dma_start(out=dtxSp[h // 2], in_=dtxt[:]).then_inc(*sSp.inc(16))
                    if dbg and h == 0:
                        nc.gpsimd.dma_start(out=dbg_out["dbc"], in_=bct[:]).then_inc(*sOut.inc(16))
                    if dbg and h == 1:
                        nc.gpsimd.dma_start(out=dbg_out["ddtx"], in_=dtxt[:]).then_inc(*sOut.inc(16))
                    if dbg:
                        nc.scalar.wait_ge(sOut.h, sOut.n)
                    nc.scalar.wait_ge(sSp.h, sSp.n)

    esF.close()
    if stage < 5:
        return finish()

    with sb("ystg", (64, L), BF16) as ystg, \
         sb("dtpTs", (128, 128), BF16) as dtpTs, \
         sb("dtx2", (128, L), BF16) as dtx2, sb("dtp", (128, L)) as dtp, \
         sb("zsp", (128, 1024)) as zsp, sb("dt2", (128, L)) as dt2, \
         sb("xh2", (128, L), BF16) as xh2, sb("w2t", (128, L)) as w2t, \
         sb("abar", (128, L)) as abar, sb("inp", (128, L)) as inpt, \
         sb("hst", (128, L)) as hst, sb("y2t", (128, L)) as y2t, \
         sb("bcl", (96, L), BF16) as bcl, sb("bcs", (36, L), BF16) as bcs, \
         sb("Bb", (128, 2 * L), BF16) as Bb, sb("Cb", (128, 2 * L), BF16) as Cb:
        nc.gpsimd.dma_start(out=dtpTs[:], in_=dtpT).then_inc(*sW.inc(16))
        bank_free = {0: 0, 1: 0}
        tb_used = {0: 0, 1: 0}
        fold_free = 0
        y2t_free = 0
        b_free = {0: 0, 1: 0}
        c_free = {0: 0, 1: 0}
        a_free = 0
        sT_pair = sT.n
        sAB_head = sAB.n
        sBC_head = sBC.n
        w2_done = 0
        for h in range(HPC):
            pr, po = h // 2, (h % 2) * 64
            if h % 2 == 0:
                nc.sync.wait_ge(sT.h, sT_pair)
                if h == 0:
                    nc.sync.wait_ge(sW.h, sW.n)
                    nc.sync.wait_ge(sSp.h, sSp.n)
                nc.sync.dma_start(out=dtx2[:], in_=dtxSp[pr]).then_inc(*sXs.inc(16))
                nc.tensor.wait_ge(sXs.h, sXs.n)
                for nch in range(8):
                    nc.tensor.wait_ge(sSq.h, sSq.n)
                    nc.tensor.matmul(psA[0:64, 0:512], dtpTs[:, 0:64],
                                     dtx2[:, nch * 512:(nch + 1) * 512],
                                     start=True, stop=True)
                    nc.tensor.matmul(psB[0:64, 0:512], dtpTs[:, 64:128],
                                     dtx2[:, nch * 512:(nch + 1) * 512],
                                     start=True, stop=True).then_inc(*sT.inc())
                    nc.scalar.wait_ge(sT.h, sT.n)
                    nc.scalar.wait_ge(sV.h, sV.n)
                    zpe = (nch % 2) * 512
                    zs = zsp[:, zpe:zpe + 512]
                    nc.scalar.activation(out=zsp[0:64, zpe:zpe + 512],
                                         in_=psA[0:64, 0:512], func=AF.Identity,
                                         bias=colbuf[0:64, C_DTB:C_DTB + 1])
                    nc.scalar.activation(out=zsp[64:128, zpe:zpe + 512],
                                         in_=psB[0:64, 0:512], func=AF.Identity,
                                         bias=colbuf[64:128, C_DTB:C_DTB + 1])
                    ab = dtp[:, nch * 512:(nch + 1) * 512]
                    nc.scalar.activation(out=ab, in_=zs, func=AF.Abs)
                    nc.scalar.activation(out=ab, in_=ab, func=AF.Exp, scale=-1.0)
                    nc.scalar.activation(out=ab, in_=ab, func=AF.Ln, bias=1.0).then_inc(*sSq.inc())
                    nc.vector.wait_ge(sSq.h, sSq.n)
                    nc.vector.scalar_tensor_tensor(out=ab, in0=zs, scalar=0.0, in1=ab,
                                                   op0=ALU.max, op1=ALU.add).then_inc(*sV.inc())
                sT_pair = sT.n
            nc.sync.wait_ge(sV.h, sV.n)
            nc.sync.wait_ge(sAB.h, sAB_head)
            nc.sync.wait_ge(sVs.h, w2_done)
            nc.sync.wait_ge(sSt[0].h, sSt[0].n)
            nc.sync.wait_ge(sSt[1].h, sSt[1].n)
            nc.sync.dma_start(out=dt2[0:64, :], in_=dtp[po:po + 64, :]).then_inc(*sXs.inc(16))
            nc.sync.dma_start(out=dt2[64:128, :], in_=dtp[po:po + 64, :]).then_inc(*sXs.inc(16))
            nc.sync.dma_start(out=xh2[0:64, :], in_=xhSp[pr][po:po + 64, :]).then_inc(*sXs.inc(16))
            nc.sync.dma_start(out=xh2[64:128, :], in_=xhSp[pr][po:po + 64, :]).then_inc(*sXs.inc(16))
            nc.sync.dma_start(out=bcl[:], in_=bcSp[h]).then_inc(*sXs.inc(16))
            sXs_tiles = sXs.n
            nc.vector.wait_ge(sXs.h, sXs_tiles)
            nc.vector.tensor_mul(w2t[:], dt2[:], xh2[:]).then_inc(*sVs.inc())
            w2_done = sVs.n
            for shi in range(NSH):
                pe = shi % 2
                base = 32 * pe
                abv = abar[:]
                nc.gpsimd.wait_ge(sXs.h, sXs_tiles)
                nc.gpsimd.wait_ge(sTB.h, tb_used[pe])
                nc.gpsimd.dma_start(out=bcs[base:base + 2, :],
                                    in_=bcl[2 * shi:2 * shi + 2, :]).then_inc(*sSt[pe].inc(16))
                nc.gpsimd.dma_start(out=bcs[base + 2:base + 4, :],
                                    in_=bcl[DS + 2 * shi:DS + 2 * shi + 2, :]).then_inc(*sSt[pe].inc(16))
                st_rdy = sSt[pe].n
                for half, (co, dst, vfree) in enumerate(
                        [(0, Bb, b_free[pe]), (128, Cb, c_free[pe])]):
                    for c in range(8):
                        ps = psC if c % 2 == 0 else psD
                        nc.tensor.wait_ge(sSt[pe].h, st_rdy)
                        nc.tensor.wait_ge(sBE.h, bank_free[c % 2])
                        nc.tensor.wait_ge(sVs.h, vfree)
                        nc.tensor.matmul(ps[0:128, 0:512],
                                         Esel[base:base + 4, co:co + 128],
                                         bcs[base:base + 4, c * 512:(c + 1) * 512],
                                         start=True, stop=True).then_inc(*sTB.inc())
                        nc.scalar.wait_ge(sTB.h, sTB.n)
                        nc.scalar.activation(
                            out=dst[:, pe * L + c * 512:pe * L + (c + 1) * 512],
                            in_=ps[0:128, 0:512], func=AF.Copy).then_inc(*sBE.inc())
                        bank_free[c % 2] = sBE.n
                    if half == 0:
                        B_rdy = sBE.n
                    else:
                        C_rdy = sBE.n
                tb_used[pe] = sTB.n
                nc.scalar.wait_ge(sXs.h, sXs_tiles)
                nc.scalar.wait_ge(sVs.h, a_free)
                nc.scalar.activation(out=abv, in_=dt2[:], func=AF.Exp,
                                     scale=ascl[:, h * NSH + shi:h * NSH + shi + 1]) \
                    .then_inc(*sAB.inc())
                A_rdy = sAB.n
                nc.vector.wait_ge(sBE.h, B_rdy)
                nc.vector.tensor_mul(inpt[:], w2t[:], Bb[:, pe * L:(pe + 1) * L]) \
                    .then_inc(*sVs.inc())
                b_free[pe] = sVs.n
                nc.vector.wait_ge(sAB.h, A_rdy)
                nc.vector.tensor_tensor_scan(out=hst[:], data0=abv, data1=inpt[:],
                                             initial=0.0, op0=ALU.mult, op1=ALU.add) \
                    .then_inc(*sVs.inc())
                a_free = sVs.n
                nc.vector.wait_ge(sVs.h, a_free)
                nc.vector.wait_ge(sBE.h, C_rdy)
                nc.vector.tensor_mul(hst[:], hst[:], Cb[:, pe * L:(pe + 1) * L]) \
                    .then_inc(*sVs.inc())
                c_free[pe] = sVs.n
                nc.vector.wait_ge(sVs.h, c_free[pe])
                if shi == 0:
                    nc.vector.wait_ge(sTB.h, y2t_free)
                    nc.vector.tensor_copy(out=y2t[:], in_=hst[:]).then_inc(*sVs.inc())
                else:
                    nc.vector.tensor_add(y2t[:], y2t[:], hst[:]).then_inc(*sVs.inc())
            for c in range(8):
                nc.tensor.wait_ge(sVs.h, sVs.n)
                nc.tensor.wait_ge(sBE.h, fold_free)
                nc.tensor.matmul(psS[0:64, 0:512], Fold[:],
                                 y2t[:, c * 512:(c + 1) * 512],
                                 start=True, stop=True).then_inc(*sTB.inc())
                nc.scalar.wait_ge(sTB.h, sTB.n)
                nc.scalar.wait_ge(sSp.h, sSp.n)
                nc.scalar.activation(out=ystg[:, c * 512:(c + 1) * 512],
                                     in_=psS[0:64, 0:512], func=AF.Copy).then_inc(*sBE.inc())
                fold_free = sBE.n
            nc.gpsimd.wait_ge(sBE.h, sBE.n)
            nc.gpsimd.dma_start(out=cin.ap()[pr][po:po + 64, :], in_=ystg[:]).then_inc(*sSp.inc(16))
            sAB_head = sAB.n
            y2t_free = sTB.n
        nc.gpsimd.wait_ge(sSp.h, sSp.n)

    if stage < 6:
        return finish()

    nc.gpsimd.wait_ge(sSp.h, sSp.n)
    nc.gpsimd.collective_compute(
        "AllGather", ALU.bypass,
        replica_groups=[[0, 1], [2, 3], [4, 5], [6, 7]],
        ins=[cin.ap().opt()], outs=[cout.ap().opt()],
    ).then_inc(*sCC.inc())

    if True:
        with sb("W1s", (128, NSLAB * HF), BF16) as W1s, sb("Wcs", (128, 32 * D), BF16) as Wcs, \
             sb("xnh", (128, NSLAB * 512), BF16) as xnh, \
             sb("h1t", (128, 32 * 512), BF16) as h1t, sb("ffo", (128, 512)) as ffo:
            nc.sync.wait_ge(sVs.h, sVs.n)
            nc.sync.wait_ge(sAB.h, sAB.n)
            nc.sync.wait_ge(sSp.h, sSp.n)
            for k in range(NSLAB):
                nc.sync.dma_start(out=W1s[:, k * HF:(k + 1) * HF], in_=WexpP[k]).then_inc(*sXs.inc(16))
            for k in range(32):
                nc.sync.dma_start(out=Wcs[:, k * D:(k + 1) * D], in_=WconP[k]) \
                    .then_inc(*sXs.inc(16))
            for st_ in range(TH // 512):
                for k in range(NSLAB):
                    nc.sync.wait_ge(sT.h, sT.n)
                    nc.sync.dma_start(out=xnh[:, k * 512:(k + 1) * 512],
                                      in_=xnhSp[k][:, st_ * 512:(st_ + 1) * 512]).then_inc(*sXs.inc(16))
                nc.tensor.wait_ge(sXs.h, sXs.n)
                nc.tensor.wait_ge(sV.h, sV.n)
                for m in range(32):
                    psm = [psA, psB][m % 2]
                    nc.tensor.wait_ge(sEv.h, sEv.n - 1 if m >= 2 else 0)
                    for k in range(NSLAB):
                        mm = nc.tensor.matmul(psm[:, 0:512],
                                              W1s[:, k * HF + m * 128:k * HF + (m + 1) * 128],
                                              xnh[:, k * 512:(k + 1) * 512],
                                              start=(k == 0), stop=(k == NSLAB - 1))
                        if k == NSLAB - 1:
                            mm.then_inc(*sT.inc())
                    nc.scalar.wait_ge(sT.h, sT.n)
                    nc.scalar.activation(out=h1t[:, m * 512:(m + 1) * 512], in_=psm[:, 0:512],
                                         func=AF.Gelu, bias=b1col[:, m:m + 1]).then_inc(*sEv.inc())
                for m in range(NSLAB):
                    nc.tensor.wait_ge(sEv.h, sEv.n)
                    nc.tensor.wait_ge(sSp.h, sSp.n)
                    for k in range(32):
                        mm = nc.tensor.matmul(psC[:, 0:512],
                                              Wcs[:, k * D + m * 128:k * D + (m + 1) * 128],
                                              h1t[:, k * 512:(k + 1) * 512],
                                              start=(k == 0), stop=(k == 31))
                        if k == 31:
                            mm.then_inc(*sT.inc())
                    nc.scalar.wait_ge(sT.h, sT.n)
                    nc.scalar.activation(out=ffo[:], in_=psC[:, 0:512], func=AF.Identity,
                                         bias=b2col[:, m:m + 1]).then_inc(*sEv.inc())
                    nc.gpsimd.wait_ge(sEv.h, sEv.n)
                    nc.gpsimd.dma_start(out=ffnSc[m][:, st_ * 512:(st_ + 1) * 512],
                                        in_=ffo[:]).then_inc(*sSp.inc(16))
            if dbg:
                nc.gpsimd.wait_ge(sSp.h, sSp.n)
                nc.gpsimd.dma_start(out=dbg_out["dffn"], in_=ffnSc[0]).then_inc(*sOut.inc(16))

        with sb("mhB", (128, TH)) as mhB, sb("rlhB", (128, TH)) as rlhB, \
             sb("sq2", (128, 512)) as sq2, sb("x1w", (128, 1024)) as x1w:
          with sb("yall", (128, NSLAB * TH), BF16) as yall, \
               sb("opTs", (128, NSLAB * D), BF16) as opTs, sb("xht", (128, TH)) as xht:
            nc.gpsimd.wait_ge(sCC.h, sCC.n)
            for case in nc.gpsimd.Switch(parity, 2):
                o = case * TH
                for slot in range(2):
                    for j in range(HPC // 2):
                        nc.gpsimd.dma_start(
                            out=yall[:, (slot * 4 + j) * TH:(slot * 4 + j + 1) * TH],
                            in_=cout.ap()[slot, j][:, o:o + TH]).then_inc(sBk.h, 16)
            sBk.n += 128
            nc.gpsimd.dma_start(out=opTs[:], in_=opT).then_inc(*sSp.inc(16))
            nc.tensor.wait_ge(sBk.h, sBk.n)
            nc.tensor.wait_ge(sEv.h, sEv.n)
            bcast_row(hrows[0:1, :], 0, mhB, TH)
            bcast_row(hrows[32:33, :], 32, rlhB, TH)
            nc.tensor.wait_ge(sBE.h, sBE.n)
            nc.tensor.wait_ge(sSp.h, sSp.n)
            nc.vector.wait_ge(sBE.h, sBE.n)
            nc.sync.wait_ge(sSp.h, sSp.n)
            for m in range(NSLAB):
                nc.sync.wait_ge(sV.h, sV.n)
                nc.sync.dma_start(out=xht[:],
                                  in_=xTh[m * 128:(m + 1) * 128, :]).then_inc(*sXs.inc(16))
                for nch in range(TH // 512):
                    psm = [psA, psB][nch % 2]
                    pe = nch % 2
                    nc.tensor.wait_ge(sEv.h, sEv.n)
                    nc.tensor.wait_ge(sSp.h, sSp.n)
                    for k in range(NSLAB):
                        mm = nc.tensor.matmul(psm[:, 0:512],
                                              opTs[:, k * D + m * 128:k * D + (m + 1) * 128],
                                              yall[:, k * TH + nch * 512:k * TH + (nch + 1) * 512],
                                              start=(k == 0), stop=(k == NSLAB - 1))
                        if k == NSLAB - 1:
                            mm.then_inc(*sT.inc())
                    nc.scalar.wait_ge(sT.h, sT.n)
                    x1v = x1w[:, pe * 512:(pe + 1) * 512]
                    nc.scalar.activation(out=x1v, in_=psm[:, 0:512], func=AF.Identity,
                                         bias=colbuf[:, C_OPB + m:C_OPB + m + 1]).then_inc(*sEv.inc())
                    nc.vector.wait_ge(sEv.h, sEv.n)
                    nc.vector.wait_ge(sXs.h, sXs.n)
                    xv = xht[:, nch * 512:(nch + 1) * 512]
                    nc.vector.tensor_sub(sq2[:], xv, mhB[:, nch * 512:(nch + 1) * 512])
                    nc.vector.tensor_mul(sq2[:], sq2[:], rlhB[:, nch * 512:(nch + 1) * 512])
                    nc.vector.tensor_add(x1v, x1v, sq2[:]).then_inc(*sV.inc())
                    nc.gpsimd.wait_ge(sV.h, sV.n)
                    nc.gpsimd.dma_start(out=x1Sc[m][:, nch * 512:(nch + 1) * 512],
                                        in_=x1v).then_inc(*sSp.inc(16))
            if dbg:
                nc.gpsimd.wait_ge(sSp.h, sSp.n)
                nc.gpsimd.dma_start(out=dbg_out["dx1"], in_=x1Sc[0]).then_inc(*sOut.inc(16))
          nc.scalar.wait_ge(sV.h, sV.n)
          nc.sync.wait_ge(sSp.h, sSp.n)
          if True:
            for nch in range(TH // 512):
                for k in range(NSLAB):
                    pe2 = k % 2
                    nc.sync.wait_ge(sSq.h, sSq.n)
                    nc.sync.dma_start(out=x1w[:, pe2 * 512:(pe2 + 1) * 512],
                                      in_=x1Sc[k][:, nch * 512:(nch + 1) * 512]).then_inc(*sXs.inc(16))
                    nc.scalar.wait_ge(sXs.h, sXs.n)
                    nc.scalar.wait_ge(sT.h, sT.n) if k == 0 else None
                    nc.scalar.activation(out=sq2[:], in_=x1w[:, pe2 * 512:(pe2 + 1) * 512],
                                         func=AF.Square).then_inc(*sSq.inc())
                    nc.tensor.wait_ge(sSq.h, sSq.n)
                    mm = nc.tensor.matmul(psS[0:1, 0:512], ones[:], sq2[:],
                                          start=(k == 0), stop=(k == NSLAB - 1))
                    if k == NSLAB - 1:
                        mm.then_inc(*sT.inc())
                nc.scalar.wait_ge(sT.h, sT.n)
                nc.scalar.copy(out=hrows[64:65, nch * 512:(nch + 1) * 512],
                               in_=psS[0:1, 0:512]).then_inc(*sSq.inc())
            nc.vector.wait_ge(sSq.h, sSq.n)
            nc.vector.tensor_scalar(out=hrows[64:65, :], in0=hrows[64:65, :],
                                    scalar1=1.0 / D, scalar2=1e-6,
                                    op0=ALU.mult, op1=ALU.add).then_inc(*sV.inc())
            nc.vector.wait_ge(sV.h, sV.n)
            nc.vector.reciprocal(hrows[64:65, :], hrows[64:65, :]).then_inc(*sV.inc())
            nc.scalar.wait_ge(sV.h, sV.n)
            nc.scalar.activation(out=hrows[64:65, :], in_=hrows[64:65, :], func=AF.Sqrt) \
                .then_inc(*sSq.inc())
            nc.tensor.wait_ge(sSq.h, sSq.n)
            nc.tensor.wait_ge(sEv.h, sEv.n)
            bcast_row(hrows[64:65, :], 64, rlhB, TH)
            nc.tensor.wait_ge(sBE.h, sBE.n)
            nc.scalar.activation(out=sq2[:, 0:16], in_=sq2[:, 0:16],
                                 func=AF.Silu).then_inc(*sSq.inc())
            nc.scalar.wait_ge(sSq.h, sSq.n)

            with sb("gTs", (128, NSLAB * DI), BF16) as gTs, sb("uTs", (128, NSLAB * DI), BF16) as uTs, \
                 sb("dTs", (128, 16 * D), BF16) as dTs, sb("h2t", (128, NSLAB * 512), BF16) as h2t, \
                 sb("gsl", (128, 16 * 512), BF16) as gsl, sb("msl", (128, 16 * 512), BF16) as msl, \
                 sb("fres", (128, 2 * 512)) as fres, sb("xres", (128, 2 * 512)) as xres:
                nc.gpsimd.wait_ge(sV.h, sV.n)
                nc.gpsimd.dma_start(out=gTs[:], in_=gT).then_inc(*sSp.inc(16))
                nc.gpsimd.dma_start(out=uTs[:], in_=uT).then_inc(*sSp.inc(16))
                nc.gpsimd.dma_start(out=dTs[:], in_=dTw).then_inc(*sSp.inc(16))
                nc.vector.wait_ge(sBE.h, sBE.n)
                if probe:
                    nc.gpsimd.wait_ge(sBE.h, sBE.n)
                    nc.gpsimd.dma_start(out=rlbD[:, TH:], in_=rlhB[:]).then_inc(*sOut.inc(16))
                nc.tensor.wait_ge(sSp.h, sSp.n)
                if SER_MLP:
                    for eng in (nc.tensor, nc.scalar, nc.vector, nc.sync, nc.gpsimd):
                        for ctr in (sW, sXs, sSq, sV, sT, sEv, sSp, sBC, sAB, sVs,
                                    sCC, sBk, sOut, sTB, sBE, sSt[0], sSt[1], sF[0], sF[1]):
                            eng.wait_ge(ctr.h, ctr.n)
                for st_ in range(TH // 512):
                    for k in range(NSLAB):
                        pe2 = k % 2
                        nc.sync.wait_ge(sV.h, sV.n)
                        nc.sync.dma_start(out=x1w[:, pe2 * 512:(pe2 + 1) * 512],
                                          in_=x1Sc[k][:, st_ * 512:(st_ + 1) * 512]) \
                            .then_inc(*sXs.inc(16))
                        nc.vector.wait_ge(sXs.h, sXs.n)
                        nc.vector.scalar_tensor_tensor(
                            out=h2t[:, k * 512:(k + 1) * 512],
                            in0=x1w[:, pe2 * 512:(pe2 + 1) * 512],
                            scalar=colbuf[:, C_W2 + k:C_W2 + k + 1],
                            in1=rlhB[:, st_ * 512:(st_ + 1) * 512],
                            op0=ALU.mult, op1=ALU.mult).then_inc(*sV.inc())
                    nc.vector.engine_nop().then_inc(*sV.inc())
                    nc.tensor.wait_ge(sV.h, sV.n)
                    for m in range(16):
                        psm = [psA, psB][m % 2]
                        if SER_MLP:
                            nc.tensor.wait_ge(sEv.h, sEv.n)
                            nc.tensor.wait_ge(sV.h, sV.n)
                        nc.tensor.wait_ge(sEv.h, sEv.n - 1 if m >= 2 else 0)
                        for k in range(NSLAB):
                            mm = nc.tensor.matmul(psm[:, 0:512],
                                                  gTs[:, k * DI + m * 128:k * DI + (m + 1) * 128],
                                                  h2t[:, k * 512:(k + 1) * 512],
                                                  start=(k == 0), stop=(k == NSLAB - 1))
                            if k == NSLAB - 1:
                                mm.then_inc(*sT.inc())
                        nc.scalar.wait_ge(sT.h, sT.n)
                        nc.scalar.activation(out=gsl[:, m * 512:(m + 1) * 512],
                                             in_=psm[:, 0:512], func=AF.Silu).then_inc(*sEv.inc())
                    for m in range(16):
                        psm = [psC, psD][m % 2]
                        if SER_MLP:
                            nc.tensor.wait_ge(sEv.h, sEv.n)
                            nc.tensor.wait_ge(sV.h, sV.n)
                        nc.tensor.wait_ge(sV.h, sV.n - 1 if m >= 2 else 0)
                        for k in range(NSLAB):
                            mm = nc.tensor.matmul(psm[:, 0:512],
                                                  uTs[:, k * DI + m * 128:k * DI + (m + 1) * 128],
                                                  h2t[:, k * 512:(k + 1) * 512],
                                                  start=(k == 0), stop=(k == NSLAB - 1))
                            if k == NSLAB - 1:
                                mm.then_inc(*sT.inc())
                        nc.vector.wait_ge(sT.h, sT.n)
                        nc.vector.wait_ge(sEv.h, sEv.n)
                        nc.vector.tensor_mul(msl[:, m * 512:(m + 1) * 512],
                                             psm[:, 0:512], gsl[:, m * 512:(m + 1) * 512]) \
                            .then_inc(*sV.inc())
                    for m in range(NSLAB):
                        pe = m % 2
                        nc.sync.wait_ge(sOut.h, sOut.n)
                        nc.sync.dma_start(out=fres[:, pe * 512:(pe + 1) * 512],
                                          in_=ffnSc[m][:, st_ * 512:(st_ + 1) * 512]) \
                            .then_inc(*sXs.inc(16))
                        sXs_ffn = sXs.n
                        psm = [psA, psB][m % 2]
                        nc.tensor.wait_ge(sV.h, sV.n)
                        nc.tensor.wait_ge(sEv.h, sEv.n)
                        for k in range(16):
                            mm = nc.tensor.matmul(psm[:, 0:512],
                                                  dTs[:, k * D + m * 128:k * D + (m + 1) * 128],
                                                  msl[:, k * 512:(k + 1) * 512],
                                                  start=(k == 0), stop=(k == 15))
                            if k == 15:
                                mm.then_inc(*sT.inc())
                        nc.scalar.wait_ge(sXs.h, sXs_ffn)
                        nc.scalar.wait_ge(sT.h, sT.n)
                        nc.sync.wait_ge(sV.h, sV.n)
                        nc.sync.dma_start(out=xres[:, pe * 512:(pe + 1) * 512],
                                          in_=xTh[m * 128:(m + 1) * 128, st_ * 512:(st_ + 1) * 512]) \
                            .then_inc(*sXs.inc(16))
                        nc.sync.wait_ge(sV.h, sV.n)
                        nc.sync.dma_start(out=x1w[:, pe * 512:(pe + 1) * 512],
                                          in_=x1Sc[m][:, st_ * 512:(st_ + 1) * 512]) \
                            .then_inc(*sXs.inc(16))
                        sXs_x1 = sXs.n
                        ov = fres[:, pe * 512:(pe + 1) * 512]
                        nc.vector.wait_ge(sXs.h, sXs_ffn)
                        nc.vector.wait_ge(sXs.h, sXs_x1)
                        nc.vector.wait_ge(sT.h, sT.n)
                        nc.vector.tensor_add(ov, ov, x1w[:, pe * 512:(pe + 1) * 512])
                        nc.vector.tensor_add(ov, ov, psm[:, 0:512]).then_inc(*sEv.inc())
                        nc.vector.wait_ge(sXs.h, sXs.n)
                        nc.vector.tensor_add(ov, ov, xres[:, pe * 512:(pe + 1) * 512]).then_inc(*sV.inc())
                        nc.gpsimd.wait_ge(sV.h, sV.n)
                        nc.gpsimd.dma_start(out=outT[m * 128:(m + 1) * 128, st_ * 512:(st_ + 1) * 512],
                                            in_=ov).then_inc(*sOut.inc(16))
                if probe and st_ == TH // 512 - 1:
                    nc.gpsimd.dma_start(out=rlbD[:, 0:TH], in_=rlhB[:]).then_inc(*sOut.inc(16))
    if probe:
        nc.gpsimd.wait_ge(sV.h, sV.n)
        nc.gpsimd.wait_ge(sSq.h, sSq.n)
        nc.gpsimd.wait_ge(sCC.h, sCC.n)
        nc.gpsimd.dma_start(out=hrowsD, in_=hrows[:]).then_inc(*sOut.inc(16))
        o1 = HF // 128
        o2 = o1 + NSLAB
        o3 = o2 + HPC * NSH
        nc.gpsimd.dma_start(out=bcolD[:, 0:o1], in_=b1col[:]).then_inc(*sOut.inc(16))
        nc.gpsimd.dma_start(out=bcolD[:, o1:o2], in_=b2col[:]).then_inc(*sOut.inc(16))
        nc.gpsimd.dma_start(out=bcolD[:, o2:o3], in_=ascl[:]).then_inc(*sOut.inc(16))
        nc.gpsimd.dma_start(out=bcolD[:, o3:o3 + 80], in_=colbuf[:]).then_inc(*sOut.inc(16))
        for j in range(HPC // 2):
            nc.gpsimd.dma_start(out=cinD[j], in_=cin.ap()[j]).then_inc(*sOut.inc(16))
            nc.gpsimd.dma_start(out=coutD[0, j], in_=cout.ap()[0, j]).then_inc(*sOut.inc(16))
            nc.gpsimd.dma_start(out=coutD[1, j], in_=cout.ap()[1, j]).then_inc(*sOut.inc(16))
    nc.gpsimd.wait_ge(sOut.h, sOut.n)
    es.close()
    return nc


_NC_CACHE = {}


def kernel(**inp):
    x = np.asarray(inp["L0_attn_x"], np.float32)
    n1w = np.asarray(inp["L0_attn_norm1_w"], np.float32)
    n2w = np.asarray(inp["L0_attn_norm2_w"], np.float32)
    dwk = np.asarray(inp["L0_attn_dw_kernel"], np.float32)
    dwb = np.asarray(inp["L0_attn_dw_bias"], np.float32)
    pww = np.asarray(inp["L0_attn_pw_w"], np.float32)
    pwb = np.asarray(inp["L0_attn_pw_b"], np.float32)
    xpw = np.asarray(inp["L0_attn_x_proj_w"], np.float32)
    dtw = np.asarray(inp["L0_attn_dt_proj_w"], np.float32)
    dtb = np.asarray(inp["L0_attn_dt_proj_b"], np.float32)
    alog = np.asarray(inp["L0_attn_A_log"], np.float32)
    opw = np.asarray(inp["L0_attn_out_proj_w"], np.float32)
    opb = np.asarray(inp["L0_attn_out_proj_b"], np.float32)
    gw = np.asarray(inp["L0_attn_gate_w"], np.float32)
    uw = np.asarray(inp["L0_attn_up_w"], np.float32)
    dw = np.asarray(inp["L0_attn_down_w"], np.float32)
    Wexp = np.asarray(inp["L0_ffn_W_exp"], np.float32)
    bexp = np.asarray(inp["L0_ffn_b_exp"], np.float32)
    upw_bf = np.asarray(inp["L0_ffn_up_weights"], np.float32)
    upb = np.asarray(inp["L0_ffn_up_bias"], np.float32)
    dnw_bf = np.asarray(inp["L0_ffn_down_weights"], np.float32)
    Wcon = np.asarray(inp["L0_ffn_W_con"], np.float32)
    bcon = np.asarray(inp["L0_ffn_b_con"], np.float32)
    dnb = np.asarray(inp["L0_ffn_down_bias"], np.float32)

    u_lowN, u_lowT, u_highN, u_highT = _stage_mats(upw_bf)
    d_lowN, d_lowT, d_highN, d_highT = _stage_mats(dnw_bf)

    import ml_dtypes
    eselH = np.zeros((36, 256), np.float32)
    for _b in (0, 32):
        eselH[_b + 0, 0:64] = 1.0
        eselH[_b + 1, 64:128] = 1.0
        eselH[_b + 2, 128:192] = 1.0
        eselH[_b + 3, 192:256] = 1.0
    foldH = np.zeros((128, 64), np.float32)
    for _m in range(64):
        foldH[_m, _m] = 1.0
        foldH[64 + _m, _m] = 1.0

    common = dict(
        EselI=eselH.astype(ml_dtypes.bfloat16), FoldI=foldH,
        w1col=_col(n1w), w2col=_col(n2w),
        kercol=np.ascontiguousarray(
            dwk[:, 0, :].reshape(NSLAB, 128, KC).transpose(1, 0, 2).reshape(128, NSLAB * KC)),
        dwbcol=_col(dwb),
        xpT=_blkdiag2(xpw.T), dtpT=_blkdiag2(dtw.T),
        dtb2=np.ascontiguousarray(np.concatenate([dtb, dtb])[:, None]),
        opT=_slabmajor(opw.T), opbc=_col(opb),
        gT=_slabmajor(gw.T), uT=_slabmajor(uw.T),
        dTw=_slabmajor(dw.T),
        EXPc=np.ascontiguousarray(np.concatenate([Wexp.T, bexp[:, None]], axis=1)).astype(ml_dtypes.bfloat16),
        WCS=np.ascontiguousarray(Wcon.reshape(64, 64, D).transpose(1, 0, 2).reshape(HF, D)).astype(ml_dtypes.bfloat16),
        ubcol=_col(upb), bcncol=_col(bcon), dbcol=_col(dnb),
        bfu_low=u_lowN.astype(ml_dtypes.bfloat16), bfu_lowT=u_lowT.astype(ml_dtypes.bfloat16),
        bfu_high=u_highN.astype(ml_dtypes.bfloat16), bfu_highT=u_highT.astype(ml_dtypes.bfloat16),
        bfd_low=d_lowN.astype(ml_dtypes.bfloat16), bfd_lowT=d_lowT.astype(ml_dtypes.bfloat16),
        bfd_high=d_highN.astype(ml_dtypes.bfloat16), bfd_highT=d_highT.astype(ml_dtypes.bfloat16),
    )

    in_maps = []
    for c in range(8):
        b, hg = c // 2, c % 2
        xbT = np.ascontiguousarray(x[b].T)
        m = dict(common)
        m["xT"] = xbT
        m["xTh"] = np.ascontiguousarray(xbT[:, hg * TH:(hg + 1) * TH])
        m["pwT"] = _slabmajor(pww.T[:, hg * MYC:(hg + 1) * MYC])
        m["pwbc"] = _col(pwb[hg * MYC:(hg + 1) * MYC])
        m["alog8"] = np.ascontiguousarray(alog[hg * HPC:(hg + 1) * HPC])
        in_maps.append(m)

    key = ("nc_dbg" if DEBUG else "nc") + str(STAGE) + ("p" if PROBE else "")
    if key not in _NC_CACHE:
        _NC_CACHE[key] = build_nc(dbg=DEBUG, stage=STAGE, probe=PROBE)
    nc = _NC_CACHE[key]
    res = run_bass_kernel_spmd(nc, in_maps, core_ids=list(range(8)))
    out = np.empty((B, L, D), np.float32)
    for c in range(8):
        b, th = c // 2, c % 2
        out[b, th * TH:(th + 1) * TH, :] = res.results[c]["outT"].T
    if DEBUG or PROBE:
        kernel.last_results = res
    return out
